# version 01
# baseline (speedup 1.0000x reference)
import numpy as np
import ml_dtypes
import concourse.bass as bass
import concourse.mybir as mybir
mybir = mybir
from contextlib import ExitStack

F32 = mybir.dt.float32
BF16 = mybir.dt.bfloat16
I32 = mybir.dt.int32
AF = mybir.ActivationFunctionType
ALU = mybir.AluOpType
AX = mybir.AxisListType

B, T, C = 8, 4096, 1024
NH, HD = 16, 64
E, HH = 64, 256
Gsz, KTOP = 64, 16
NTOK = B * T
TQ = NTOK // Gsz
BUDGET = TQ * KTOP
SCF = NTOK // 128
BIGPOS = 1e30


class KB:
    def __init__(self, nc, stack):
        self.nc = nc
        self.stack = stack
        self.sems = {}
        self.cnt = {}
        for name in ("pe", "dve", "act", "pool", "qs", "qa", "qp"):
            self.sems[name] = stack.enter_context(nc.semaphore("sem_" + name))
            self.cnt[name] = 0
        self.consts = {}

    def inc(self, instr, eng):
        n = 16 if eng in ("qs", "qa", "qp") else 1
        instr.then_inc(self.sems[eng], n)
        self.cnt[eng] += 1
        return self.cnt[eng]

    def val(self, eng, count=None):
        if count is None:
            count = self.cnt[eng]
        return count * (16 if eng in ("qs", "qa", "qp") else 1)

    def wait(self, e, eng, count=None):
        e.wait_ge(self.sems[eng], self.val(eng, count))

    def wait_all(self, e, engs=("pe", "dve", "act", "pool", "qs", "qa", "qp")):
        for name in engs:
            if self.cnt[name] > 0:
                self.wait(e, name)

    def const(self, name, arr):
        h = self.nc.inline_tensor(np.ascontiguousarray(arr), name=name)
        self.consts[name] = h
        return h.ap()


def sbuf(kb, name, shape, dtype=F32):
    return kb.stack.enter_context(kb.nc.sbuf_tensor(name, shape, dtype))


def mkpsum(kb, name, shape=(128, 512), dtype=F32):
    return kb.stack.enter_context(kb.nc.psum_tensor(name, list(shape), dtype))


def emit_sort(kb, block, bufA, bufB, tbufA, tbufB, ps_tr, cst, nbat):
    nc = kb.nc
    W = nbat * 256
    state = {"cur": bufA, "alt": bufB}

    def wait_cde(e, minus_dve=0):
        kb.wait(e, "dve", kb.cnt["dve"] - minus_dve)
        kb.wait(e, "pool")
        kb.wait(e, "act")

    def stage_fland(m, j):
        cur, alt = state["cur"], state["alt"]
        d = 1 << j

        def views(buf, dirbit=None):
            if dirbit is None:
                vv = buf[:, :].rearrange("p (q two d) -> p q two d", two=2, d=d)
                return vv[:, :, 0, :], vv[:, :, 1, :]
            dm = 1 << m
            vv = buf[:, :].rearrange(
                "p (q dirb mid two d) -> p q dirb mid two d",
                dirb=2, two=2, d=d, mid=dm // (2 * d))
            return vv[:, :, dirbit, :, 0, :], vv[:, :, dirbit, :, 1, :]

        @block.vector
        def _(v):
            wait_cde(v)
            if m <= 7:
                loS, hiS = views(cur, 0)
                loD, hiD = views(alt, 0)
                v.tensor_tensor(loD, loS, hiS, op=ALU.max)
                v.tensor_tensor(hiD, loS, hiS, op=ALU.min)
                loS2, hiS2 = views(cur, 1)
                loD2, hiD2 = views(alt, 1)
                v.tensor_tensor(loD2, loS2, hiS2, op=ALU.min)
                kb.inc(v.tensor_tensor(hiD2, loS2, hiS2, op=ALU.max), "dve")
            else:
                loS, hiS = views(cur)
                loD, hiD = views(alt)
                v.tensor_tensor(loD, loS, hiS, op=ALU.max)
                kb.inc(v.tensor_tensor(hiD, loS, hiS, op=ALU.min), "dve")

        state["cur"], state["alt"] = alt, cur

    def negate(m):
        cur = state["cur"]
        half = (W // 2 // 512) * 512 or W // 2

        @block.vector
        def _(v):
            wait_cde(v)
            kb.inc(v.tensor_scalar_mul(cur[:, 0:half], cur[:, 0:half],
                                       cst[f"sgn{m}"][:, :]), "dve")

        @block.gpsimd
        def _(g):
            wait_cde(g, minus_dve=1)
            kb.inc(g.tensor_scalar_mul(cur[:, half:W], cur[:, half:W],
                                       cst[f"sgn{m}"][:, :]), "pool")

    def tland(m, jlist):
        cur = state["cur"]
        ntile = nbat * 2
        GT = 8

        for g0 in range(0, ntile, GT):
            ngt = min(g0 + GT, ntile) - g0
            for h0 in range(0, ngt, 4):
                nh = min(h0 + 4, ngt) - h0

                @block.tensor
                def _(t, g0=g0, h0=h0, nh=nh):
                    kb.wait(t, "dve")
                    kb.wait(t, "pool")
                    kb.wait(t, "act")
                    for k in range(nh):
                        kb.inc(t.transpose(
                            ps_tr[:, k * 128:(k + 1) * 128],
                            in_=cur[:, (g0 + h0 + k) * 128:(g0 + h0 + k + 1) * 128],
                            identity=cst["ident"][:, :]), "pe")

                @block.scalar
                def _(sc, h0=h0, nh=nh):
                    kb.wait(sc, "pe")
                    kb.inc(sc.copy(tbufA[:, h0 * 128:(h0 + nh) * 128],
                                   ps_tr[:, 0:nh * 128]), "act")

            tst = {"cur": tbufA, "alt": tbufB}
            for j in jlist:
                dp = 1 << (j - 8)
                dm = 1 << (m - 8)
                tcur, talt = tst["cur"], tst["alt"]

                def tviews(buf, dirbit, ngt=ngt, dp=dp, dm=dm):
                    if dm >= 128:
                        if dirbit == 1:
                            return None
                        vv = buf[:, 0:ngt * 128].rearrange(
                            "q (tile hi two d) -> q tile hi two d", two=2, d=dp,
                            tile=ngt, hi=128 // (2 * dp))
                        return vv[:, :, :, 0, :], vv[:, :, :, 1, :]
                    vv = buf[:, 0:ngt * 128].rearrange(
                        "q (tile hi dirb mid two d) -> q tile hi dirb mid two d",
                        tile=ngt, dirb=2, two=2, d=dp,
                        hi=128 // (2 * dm), mid=max(dm // (2 * dp), 1))
                    sel_lo = vv[:, :, :, dirbit, :, 0, :]
                    sel_hi = vv[:, :, :, dirbit, :, 1, :]
                    return sel_lo, sel_hi

                @block.vector
                def _(v, tcur=tcur, talt=talt, tviews=tviews):
                    wait_cde(v)
                    r0 = tviews(tcur, 0)
                    d0 = tviews(talt, 0)
                    v.tensor_tensor(d0[0], r0[0], r0[1], op=ALU.max)
                    v.tensor_tensor(d0[1], r0[0], r0[1], op=ALU.min)
                    r1 = tviews(tcur, 1)
                    if r1 is not None:
                        d1 = tviews(talt, 1)
                        v.tensor_tensor(d1[0], r1[0], r1[1], op=ALU.min)
                        kb.inc(v.tensor_tensor(d1[1], r1[0], r1[1], op=ALU.max), "dve")
                    else:
                        kb.inc(v.memset(cst["junk"][:, :], 0.0), "dve")

                tst["cur"], tst["alt"] = talt, tcur

            tc = tst["cur"]
            for h0 in range(0, ngt, 4):
                nh = min(h0 + 4, ngt) - h0

                @block.tensor
                def _(t, g0=g0, h0=h0, nh=nh, tc=tc):
                    kb.wait(t, "dve")
                    kb.wait(t, "pool")
                    kb.wait(t, "act")
                    for k in range(nh):
                        kb.inc(t.transpose(
                            ps_tr[:, k * 128:(k + 1) * 128],
                            in_=tc[:, (h0 + k) * 128:(h0 + k + 1) * 128],
                            identity=cst["ident"][:, :]), "pe")

                @block.scalar
                def _(sc, g0=g0, h0=h0, nh=nh):
                    kb.wait(sc, "pe")
                    kb.inc(sc.copy(cur[:, (g0 + h0) * 128:(g0 + h0 + nh) * 128],
                                   ps_tr[:, 0:nh * 128]), "act")

    for m in range(1, 16):
        tj = list(range(m - 1, 7, -1))
        if tj:
            tland(m, tj)
        fj = list(range(min(m - 1, 7), -1, -1))
        if m >= 8:
            skip_neg = (m - 8) >= 7
            if not skip_neg:
                negate(m)
            for j in fj:
                stage_fland(m, j)
            if not skip_neg:
                negate(m)
        else:
            for j in fj:
                stage_fland(m, j)

    if state["cur"] is not bufA:
        @block.vector
        def _(v):
            wait_cde(v)
            kb.inc(v.tensor_copy(bufA[:, 0:W // 2], bufB[:, 0:W // 2]), "dve")

        @block.gpsimd
        def _(g):
            wait_cde(g, minus_dve=1)
            kb.inc(g.tensor_copy(bufA[:, W // 2:W], bufB[:, W // 2:W]), "pool")


def emit_gs_bm(kb, block, buf, gs_out, bm_out, nbat):
    @block.vector
    def _(v):
        kb.wait_all(v, ("dve", "pool", "act"))
        vv = buf[:, :].rearrange("p (e q s) -> p e q s", q=4, s=64)
        kb.inc(v.reduce_sum(gs_out[:, :].rearrange("p (e q) -> p e q", q=4),
                            vv, axis=AX.X), "dve")

    @block.vector
    def _(v):
        kb.wait_all(v, ("pool", "act"))
        vv = buf[:, :].rearrange("p (e q s) -> p e q s", q=4, s=64)
        kb.inc(v.tensor_copy(bm_out[:, :].rearrange("p (e q) -> p e q", q=4),
                             vv[:, :, :, 63:64].squeeze(3)), "dve")


def emit_theta_sigma(kb, block, gs, bm, thA, thB, tbufA, tbufB, ps_tr, cst,
                     sig_fm, thbc, sigbc, scr, dram_scratch):
    nc = kb.nc

    @block.vector
    def _(v):
        kb.wait_all(v, ("dve", "pool", "act"))
        kb.inc(v.tensor_copy(thA[:, :], gs[:, :]), "dve")

    emit_sort(kb, block, thA, thB, tbufA, tbufB, ps_tr, cst, nbat=1)

    @block.tensor
    def _(t):
        kb.wait_all(t, ("dve", "pool", "act"))
        kb.inc(t.matmul(ps_tr[0:1, 0:1], lhsT=cst["oh31"][:, :],
                        rhs=thA[:, 255:256], start=True, stop=True), "pe")

    @block.scalar
    def _(sc):
        kb.wait(sc, "pe")
        kb.inc(sc.copy(scr["th11"][:, :], ps_tr[0:1, 0:1]), "act")

    @block.tensor
    def _(t):
        kb.wait(t, "act")
        kb.inc(t.matmul(ps_tr[:, 128:129], lhsT=cst["onescol"][:, :],
                        rhs=scr["th11"][:, :], start=True, stop=True), "pe")

    @block.vector
    def _(v):
        kb.wait(v, "pe")
        kb.inc(v.tensor_copy(thbc[:, :], ps_tr[:, 128:129]), "dve")

    @block.vector
    def _(v):
        kb.wait(v, "dve")
        kb.wait_all(v, ("pool", "act"))
        kb.inc(v.tensor_single_scalar(scr["mask"][:, :], gs[:, :], thbc[:, 0:1], op=ALU.is_ge), "dve")
        kb.wait(v, "dve")
        kb.inc(v.select(scr["cand"][:, :], scr["mask"][:, :], bm[:, :], scr["bigpos"][:, :]), "dve")
        kb.wait(v, "dve")
        kb.inc(v.tensor_reduce(scr["emin"][:, :],
                               scr["cand"][:, :].rearrange("p (e q) -> p e q", q=4),
                               axis=AX.X, op=ALU.min), "dve")

    @block.tensor
    def _(t):
        kb.wait(t, "dve")
        kb.inc(t.transpose(ps_tr[0:64, 0:128], in_=scr["emin"][:, :],
                           identity=cst["ident"][:, :]), "pe")

    @block.vector
    def _(v):
        kb.wait(v, "pe")
        kb.inc(v.tensor_copy(scr["eminT"][:, :], ps_tr[0:64, 0:128]), "dve")
        kb.wait(v, "dve")
        kb.inc(v.tensor_reduce(sig_fm[:, :], scr["eminT"][:, :],
                               axis=AX.X, op=ALU.min), "dve")

    @block.tensor
    def _(t):
        kb.wait(t, "dve")
        kb.inc(t.transpose(ps_tr[0:1, 256:320], in_=sig_fm[:, :],
                           identity=cst["ident"][0:64, 0:64]), "pe")

    @block.vector
    def _(v):
        kb.wait(v, "pe")
        kb.inc(v.tensor_copy(scr["sigrow"][:, :], ps_tr[0:1, 256:320]), "dve")

    @block.tensor
    def _(t):
        kb.wait(t, "dve")
        kb.inc(t.matmul(ps_tr[:, 320:384], lhsT=cst["onescol"][:, :],
                        rhs=scr["sigrow"][:, :], start=True, stop=True), "pe")

    @block.vector
    def _(v):
        kb.wait(v, "pe")
        kb.inc(v.tensor_copy(sigbc[:, :], ps_tr[:, 320:384]), "dve")


def emit_weight_cast(kb, block, w_in, w16):
    @block.gpsimd
    def _(g):
        for e in range(E):
            for nm in ("Wup", "Wgate"):
                src = w_in[nm][e].rearrange("(c p) h -> p c h", p=128)
                kb.inc(g.dma_start(out=w16[nm][e], in_=src), "qp")
            src = w_in["Wdown"][e].rearrange("(c p) d -> p c d", p=128)
            kb.inc(g.dma_start(out=w16["Wdown"][e], in_=src), "qp")


def emit_ffn_sweep(kb, block, x_img_in, x_img_out, fs_dram, w16, bias16,
                   bdown_sb, fsfm, ps, cst, bufs, dram, dbg=None):
    TQL = 1024
    NQ = T // TQL
    NTC = TQL // 512
    xq, yq, hgrp = bufs["xq"], bufs["yq"], bufs["hgrp"]
    silu_sb, fsbc = bufs["silu"], bufs["fsbc"]
    swu, swg, swd = bufs["swu"], bufs["swg"], bufs["swd"]
    arow, brow, abc, bbc = bufs["arow"], bufs["brow"], bufs["abc"], bufs["bbc"]
    xsq, xbf, srow = bufs["xsq"], bufs["xbf"], bufs["srow"]

    for q in range(NQ):
        @block.sync
        def _(s, q=q):
            kb.wait_all(s, ("dve", "qs"))
            for c in range(8):
                kb.inc(s.dma_start(out=xq[:, c * TQL:(c + 1) * TQL],
                                   in_=x_img_in[:, c * T + q * TQL: c * T + (q + 1) * TQL]), "qs")

        for tc in range(NTC):
            sl = slice(tc * 512, (tc + 1) * 512)
            for c in range(8):
                @block.scalar
                def _(sc, tc=tc, c=c, sl=sl):
                    if c == 0:
                        kb.wait(sc, "qs")
                    kb.wait(sc, "pe")
                    kb.inc(sc.activation(out=xsq[:, :], in_=xq[:, c * TQL:][:, sl],
                                         func=AF.Square), "act")

                @block.gpsimd
                def _(g, tc=tc, c=c, sl=sl):
                    if c == 0:
                        kb.wait(g, "qs")
                    kb.wait(g, "pe")
                    kb.inc(g.tensor_copy(xbf[:, :], xq[:, c * TQL:][:, sl]), "pool")

                @block.tensor
                def _(t, c=c):
                    kb.wait(t, "act")
                    kb.wait(t, "pool")
                    kb.wait(t, "dve")
                    t.matmul(ps[2][0:1, :], lhsT=cst["ones1_16"][:, :], rhs=xbf[:, :],
                             start=(c == 0), stop=(c == 7))
                    kb.inc(t.matmul(ps[3][0:1, :], lhsT=cst["ones1_16"][:, :],
                                    rhs=xsq[:, :], start=(c == 0), stop=(c == 7)), "pe")

            @block.vector
            def _(v, sl=sl):
                kb.wait(v, "pe")
                v.tensor_scalar_mul(srow[0:1, 0:512], ps[2][0:1, :], 1.0 / C)
                kb.inc(v.tensor_scalar_mul(srow[0:1, 512:1024], ps[3][0:1, :], 1.0 / C), "dve")
                kb.wait(v, "dve")
                kb.inc(v.tensor_tensor(abc[0:1, sl], srow[0:1, 0:512], srow[0:1, 0:512],
                                       op=ALU.mult), "dve")
                kb.wait(v, "dve")
                kb.inc(v.tensor_tensor(abc[0:1, sl], srow[0:1, 512:1024], abc[0:1, sl],
                                       op=ALU.subtract), "dve")

            @block.scalar
            def _(sc, sl=sl):
                kb.wait(sc, "dve")
                kb.inc(sc.activation(out=bbc[0:1, sl], in_=abc[0:1, sl], func=AF.Ln,
                              bias=cst["eps1"][0:1, 0:1]), "act")
                kb.wait(sc, "act")
                kb.inc(sc.activation(out=arow[0:1, sl], in_=bbc[0:1, sl], func=AF.Exp,
                                     scale=-0.5), "act")

            @block.vector
            def _(v, sl=sl):
                kb.wait(v, "act")
                kb.wait(v, "dve")
                kb.inc(v.tensor_tensor(brow[0:1, sl], srow[0:1, 0:512], arow[0:1, sl],
                                       op=ALU.mult), "dve")

        for half in range(2):
            @block.tensor
            def _(t, half=half):
                kb.wait(t, "dve")
                kb.wait(t, "act")
                t.matmul(ps[2][:, :], lhsT=cst["onescol"][:, :],
                         rhs=arow[0:1, half * 512:(half + 1) * 512], start=True, stop=True)
                kb.inc(t.matmul(ps[3][:, :], lhsT=cst["onescol"][:, :],
                                rhs=brow[0:1, half * 512:(half + 1) * 512],
                                start=True, stop=True), "pe")

            @block.vector
            def _(v, half=half):
                kb.wait(v, "pe")
                v.tensor_copy(abc[:, half * 512:(half + 1) * 512], ps[2][:, :])
                kb.inc(v.tensor_copy(bbc[:, half * 512:(half + 1) * 512], ps[3][:, :]), "dve")

        for c in range(8):
            @block.vector
            def _(v, c=c):
                kb.wait(v, "pe")
                kb.inc(v.tensor_tensor(yq[:, c * TQL:(c + 1) * TQL], xq[:, c * TQL:(c + 1) * TQL],
                                       abc[:, :], op=ALU.mult), "dve")
                kb.wait(v, "dve")
                kb.inc(v.tensor_tensor(yq[:, c * TQL:(c + 1) * TQL],
                                       yq[:, c * TQL:(c + 1) * TQL], bbc[:, :],
                                       op=ALU.subtract), "dve")

        if dbg and q == 0:
            @block.sync
            def _(s):
                kb.wait(s, "dve")
                kb.inc(s.dma_start(out=dbg["y0"], in_=yq[:, :]), "qs")

        for eg in range(8):
            @block.scalar
            def _(sc, q=q, eg=eg):
                if q + eg > 0:
                    kb.wait(sc, "dve")
                for ei in range(8):
                    kb.inc(sc.dma_start(out=swd[:, ei * 2048:(ei + 1) * 2048],
                                        in_=w16["Wdown"][eg * 8 + ei]), "qa")

            for ei in range(8):
                e = eg * 8 + ei
                wslot = e % 2

                @block.sync
                def _(s, e=e, wslot=wslot):
                    if e >= 2 or q > 0:
                        kb.wait(s, "pe")
                        kb.wait(s, "pool")
                    kb.inc(s.dma_start(out=swu[:, wslot * 2048:(wslot + 1) * 2048],
                                       in_=w16["Wup"][e]), "qs")
                    kb.inc(s.dma_start(out=swg[:, wslot * 2048:(wslot + 1) * 2048],
                                       in_=w16["Wgate"][e]), "qs")
                    kb.inc(s.dma_start(
                        out=fsbc[:, wslot * TQL:(wslot + 1) * TQL],
                        in_=fs_dram[e:e + 1, q * TQL:(q + 1) * TQL].to_broadcast([128, TQL])), "qs")

                for tc in range(NTC):
                    sl = slice(tc * 512, (tc + 1) * 512)
                    for hh in range(2):
                        upb = ps[0]
                        gab = ps[1]

                        @block.tensor
                        def _(t, e=e, tc=tc, hh=hh, wslot=wslot, sl=sl, upb=upb, gab=gab):
                            kb.wait(t, "qs")
                            kb.wait(t, "dve")
                            kb.wait(t, "act")
                            for c in range(8):
                                t.matmul(upb[:, :],
                                         lhsT=swu[:, wslot * 2048 + c * 256 + hh * 128:
                                                  wslot * 2048 + c * 256 + (hh + 1) * 128],
                                         rhs=yq[:, c * TQL:][:, sl],
                                         start=(c == 0), stop=(c == 7))
                            for c in range(8):
                                ins = t.matmul(gab[:, :],
                                         lhsT=swg[:, wslot * 2048 + c * 256 + hh * 128:
                                                  wslot * 2048 + c * 256 + (hh + 1) * 128],
                                         rhs=yq[:, c * TQL:][:, sl],
                                         start=(c == 0), stop=(c == 7))
                            kb.inc(ins, "pe")

                        @block.scalar
                        def _(sc, e=e, hh=hh, upb=upb):
                            kb.wait(sc, "pe")
                            kb.inc(sc.activation(out=silu_sb[:, hh * 512:(hh + 1) * 512],
                                                 in_=upb[:, :], func=AF.Silu,
                                                 bias=bias16["bup"][:, e * 2 + hh:e * 2 + hh + 1]), "act")

                        @block.vector
                        def _(v, e=e, ei=ei, tc=tc, hh=hh, sl=sl, gab=gab):
                            kb.wait(v, "act")
                            kb.wait(v, "pe")
                            dst = hgrp[:, ((ei * 2 + hh) * TQL):][:, sl]
                            kb.inc(v.scalar_tensor_tensor(
                                out=dst, in0=gab[:, :],
                                scalar=bias16["bgate"][:, e * 2 + hh:e * 2 + hh + 1],
                                in1=silu_sb[:, hh * 512:(hh + 1) * 512],
                                op0=ALU.add, op1=ALU.mult), "dve")

                        @block.gpsimd
                        def _(g, ei=ei, tc=tc, hh=hh, sl=sl, wslot=wslot):
                            kb.wait(g, "dve")
                            kb.wait(g, "qs")
                            dst = hgrp[:, ((ei * 2 + hh) * TQL):][:, sl]
                            kb.inc(g.tensor_tensor(dst, dst,
                                                   fsbc[:, wslot * TQL:][:, sl],
                                                   op=ALU.mult), "pool")

            for tc in range(NTC):
                sl = slice(tc * 512, (tc + 1) * 512)
                for dw in range(2):
                    for do in range(4):
                        dout = dw * 4 + do
                        pbank = ps[2 + do]

                        @block.tensor
                        def _(t, q=q, eg=eg, tc=tc, dout=dout, sl=sl, pbank=pbank):
                            kb.wait(t, "qa")
                            kb.wait(t, "pool")
                            kb.wait(t, "dve")
                            for ei in range(8):
                                for hc in range(2):
                                    first = (ei == 0 and hc == 0)
                                    last = (ei == 7 and hc == 1 and eg != 7)
                                    ins = t.matmul(pbank[:, :],
                                             lhsT=swd[:, ei * 2048 + hc * 1024 + dout * 128:
                                                      ei * 2048 + hc * 1024 + (dout + 1) * 128],
                                             rhs=hgrp[:, ((ei * 2 + hc) * TQL):][:, sl],
                                             start=first, stop=last)
                                    if last:
                                        kb.inc(ins, "pe")
                            if eg == 7:
                                kb.inc(t.matmul(pbank[:, :],
                                                lhsT=bdown_sb[:, dout * 128:(dout + 1) * 128],
                                                rhs=fsfm[:, q * TQL:][:, sl],
                                                start=False, stop=True), "pe")

                        @block.vector
                        def _(v, q=q, eg=eg, tc=tc, dout=dout, sl=sl, pbank=pbank):
                            kb.wait(v, "pe")
                            dsl = bufs["dacc"][:, (tc * 8 + dout) * 512:(tc * 8 + dout + 1) * 512]
                            if eg == 0:
                                kb.inc(v.tensor_copy(dsl, pbank[:, :]), "dve")
                            else:
                                kb.inc(v.tensor_tensor(dsl, dsl, pbank[:, :], op=ALU.add), "dve")

        for tc in range(NTC):
            for dout in range(8):
                @block.vector
                def _(v, tc=tc, dout=dout):
                    kb.wait(v, "pe")
                    sl = slice(tc * 512, (tc + 1) * 512)
                    kb.inc(v.scalar_tensor_tensor(
                        out=xq[:, dout * TQL:][:, sl],
                        in0=bufs["dacc"][:, (tc * 8 + dout) * 512:(tc * 8 + dout + 1) * 512],
                        scalar=0.5, in1=xq[:, dout * TQL:][:, sl],
                        op0=ALU.mult, op1=ALU.add), "dve")

        @block.sync
        def _(s, q=q):
            kb.wait(s, "dve")
            for c in range(8):
                kb.inc(s.dma_start(out=x_img_out[:, c * T + q * TQL: c * T + (q + 1) * TQL],
                                   in_=xq[:, c * TQL:(c + 1) * TQL]), "qs")


def emit_fs(kb, block, scores_own, sigbc, fsfm, fs_dram, ps, cst, bufs):
    stile = bufs["stile"]
    f0 = bufs["fs0"]
    msk = bufs["fsmask"]
    den = bufs["fsden"]
    for i in range(T // 128):
        @block.scalar
        def _(sc, i=i):
            if i > 0:
                kb.wait(sc, "dve")
            kb.inc(sc.dma_start(out=stile[:, :], in_=scores_own[i * 128:(i + 1) * 128, :]), "qa")

        @block.vector
        def _(v, i=i):
            kb.wait(v, "qa")
            kb.wait(v, "qs")
            if i > 0:
                kb.wait(v, "pe")
            kb.inc(v.tensor_tensor(msk[:, :], stile[:, :], sigbc[:, :], op=ALU.is_ge), "dve")
            kb.wait(v, "dve")
            kb.inc(v.tensor_tensor(f0[:, :], stile[:, :], msk[:, :], op=ALU.mult), "dve")
            kb.wait(v, "dve")
            kb.inc(v.reduce_sum(den[:, :], f0[:, :], axis=AX.X), "dve")
            kb.wait(v, "dve")
            v.tensor_scalar_add(den[:, :], den[:, :], 1e-6)
            kb.inc(v.reciprocal(den[:, :], den[:, :]), "dve")
            kb.wait(v, "dve")
            kb.inc(v.tensor_scalar_mul(f0[:, :], f0[:, :], den[:, :]), "dve")

        @block.tensor
        def _(t, i=i):
            kb.wait(t, "dve")
            if i % 4 == 0 and i > 0:
                kb.wait(t, "act")
            kb.inc(t.transpose(ps[2][0:64, (i % 4) * 128:(i % 4 + 1) * 128],
                               in_=f0[:, :], identity=cst["ident"][:, :]), "pe")

        if i % 4 == 3:
            @block.scalar
            def _(sc, i=i):
                kb.wait(sc, "pe")
                kb.inc(sc.copy(fsfm[:, (i - 3) * 128:(i + 1) * 128],
                               ps[2][0:64, 0:512]), "act")

    @block.sync
    def _(s):
        kb.wait(s, "act")
        kb.inc(s.dma_start(out=fs_dram, in_=fsfm[:, :]), "qs")
        kb.wait(s, "qs")
        kb.inc(s.dma_start(out=bufs["fsrb"][:, :], in_=fs_dram), "qs")


def build_l2(debug=False):
    nc = bass.Bass("TRN2", target_bir_lowering=False, debug=False)
    x_img = nc.dram_tensor("x_img", [128, 8 * T], F32, kind="ExternalInput").ap()
    scores_em = nc.dram_tensor("scores_em", [E, NTOK], F32, kind="ExternalInput").ap()
    scores_own = nc.dram_tensor("scores_own", [T, E], F32, kind="ExternalInput").ap()
    w_in = {
        "Wup": nc.dram_tensor("Wup", [E, C, HH], F32, kind="ExternalInput").ap(),
        "Wgate": nc.dram_tensor("Wgate", [E, C, HH], F32, kind="ExternalInput").ap(),
        "Wdown": nc.dram_tensor("Wdown", [E, HH, C], F32, kind="ExternalInput").ap(),
        "bup": nc.dram_tensor("bup", [E, HH], F32, kind="ExternalInput").ap(),
        "bgate": nc.dram_tensor("bgate", [E, HH], F32, kind="ExternalInput").ap(),
        "bdown": nc.dram_tensor("bdown", [E, C], F32, kind="ExternalInput").ap(),
    }
    x3_img = nc.dram_tensor("x3_img", [128, 8 * T], F32, kind="ExternalOutput").ap()
    dbg = {}
    if debug:
        dbg["fs"] = nc.dram_tensor("o_dbg_fs", [64, T], BF16, kind="ExternalOutput").ap()
        dbg["y0"] = nc.dram_tensor("o_dbg_y0", [128, 8 * 1024], BF16, kind="ExternalOutput").ap()
        dbg["sig"] = nc.dram_tensor("o_dbg_sig", [E, 1], F32, kind="ExternalOutput").ap()
        dbg["gs"] = nc.dram_tensor("o_dbg_gs", [128, E * 4], F32, kind="ExternalOutput").ap()
        dbg["bm"] = nc.dram_tensor("o_dbg_bm", [128, E * 4], F32, kind="ExternalOutput").ap()
        dbg["th"] = nc.dram_tensor("o_dbg_th", [1, 1], F32, kind="ExternalOutput").ap()
        dbg["srt"] = nc.dram_tensor("o_dbg_srt", [128, E * 256], F32, kind="ExternalOutput").ap()
        dbg["cand"] = nc.dram_tensor("o_dbg_cand", [128, E * 4], F32, kind="ExternalOutput").ap()
        dbg["emin"] = nc.dram_tensor("o_dbg_emin", [128, E], F32, kind="ExternalOutput").ap()
        dbg["mask"] = nc.dram_tensor("o_dbg_mask", [128, E * 4], mybir.dt.uint32, kind="ExternalOutput").ap()
        dbg["sigbc"] = nc.dram_tensor("o_dbg_sigbc", [128, E], F32, kind="ExternalOutput").ap()

    w16 = {
        "Wup": [nc.dram_tensor(f"w16up{e}", [128, 8, HH], BF16).ap() for e in range(E)],
        "Wgate": [nc.dram_tensor(f"w16ga{e}", [128, 8, HH], BF16).ap() for e in range(E)],
        "Wdown": [nc.dram_tensor(f"w16dn{e}", [128, 2, C], BF16).ap() for e in range(E)],
    }
    dram = {
        "theta": nc.dram_tensor("d_theta", [1, 1], F32).ap(),
        "sigma": nc.dram_tensor("d_sigma", [E, 1], F32).ap(),
        "arow": nc.dram_tensor("d_arow", [1, 1024], F32).ap(),
        "brow": nc.dram_tensor("d_brow", [1, 1024], F32).ap(),
        "fs": nc.dram_tensor("d_fs", [E, T], BF16).ap(),
    }

    with ExitStack() as stack:
        kb = KB(nc, stack)
        cst_dram = {
            "ident": kb.const("c_ident", np.eye(128, dtype=np.float32)),
            "oh31": kb.const("c_oh31", (np.arange(128) == 31).astype(np.float32).reshape(128, 1)),
            "ones1_16": kb.const("c_ones1_16", np.ones((128, 1), ml_dtypes.bfloat16)),
            "ones_row16": kb.const("c_ones_row16", np.ones((1, 512), ml_dtypes.bfloat16)),
            "eps1": kb.const("c_eps1", np.full((128, 1), 1e-5, np.float32)),
            "onescol": kb.const("c_onescol", np.ones((1, 128), np.float32)),
        }
        for m in range(8, 15):
            s = np.where(((np.arange(128) >> (m - 8)) & 1) == 0, 1.0, -1.0).astype(np.float32)
            cst_dram[f"sgn{m}"] = kb.const(f"c_sgn{m}", s.reshape(128, 1))

        cst = {}
        cst["ident"] = sbuf(kb, "ident", [128, 128])
        cst["oh31"] = sbuf(kb, "oh31", [128, 1])
        cst["ones1_16"] = sbuf(kb, "ones1_16", [128, 1], BF16)
        cst["ones_row16"] = sbuf(kb, "ones_row16", [1, 512], BF16)
        cst["eps1"] = sbuf(kb, "eps1", [128, 1])
        cst["onescol"] = sbuf(kb, "onescol", [1, 128])
        for m in range(8, 15):
            cst[f"sgn{m}"] = sbuf(kb, f"sgn{m}", [128, 1])
        cst["junk"] = sbuf(kb, "junk", [1, 1])

        ps = [mkpsum(kb, f"ps{i}") for i in range(8)]

        sig_fm = sbuf(kb, "sig_fm", [E, 1])
        thbc = sbuf(kb, "thbc", [128, 1])
        sigbc = sbuf(kb, "sigbc", [128, E])
        fsfm = sbuf(kb, "fsfm", [64, T], BF16)
        bias16 = {
            "bup": sbuf(kb, "bupS", [128, 128]),
            "bgate": sbuf(kb, "bgaS", [128, 128]),
            "bupR": sbuf(kb, "bupR", [128, 128]),
            "bgaR": sbuf(kb, "bgaR", [128, 128]),
        }
        bdown_sb = sbuf(kb, "bdownS", [64, C], BF16)

        bufs_rb = sbuf(kb, "bufs_rb", [128, 2048], BF16)
        block = stack.enter_context(nc.Block())

        emit_weight_cast(kb, block, w_in, w16)

        @block.sync
        def _(s):
            for nm in ("ident", "oh31", "ones1_16", "ones_row16", "eps1", "onescol"):
                kb.inc(s.dma_start(out=cst[nm][:, :], in_=cst_dram[nm]), "qs")
            for m in range(8, 15):
                kb.inc(s.dma_start(out=cst[f"sgn{m}"][:, :], in_=cst_dram[f"sgn{m}"]), "qs")

        @block.gpsimd
        def _(g):
            kb.inc(g.dma_start(out=bdown_sb[:, :], in_=w_in["bdown"]), "qp")
            kb.inc(g.memset(cst["junk"][:, :], 0.0), "pool")

        @block.sync
        def _(s):
            kb.inc(s.dma_start(out=bias16["bupR"][:, :],
                               in_=w_in["bup"].rearrange("e (r p) -> (e r) p", p=128)), "qs")
            kb.inc(s.dma_start(out=bias16["bgaR"][:, :],
                               in_=w_in["bgate"].rearrange("e (r p) -> (e r) p", p=128)), "qs")

        @block.tensor
        def _(t):
            kb.wait(t, "qs")
            t.transpose(ps[0][:, 0:128], in_=bias16["bupR"][:, :], identity=cst["ident"][:, :])
            kb.inc(t.transpose(ps[1][:, 0:128], in_=bias16["bgaR"][:, :],
                               identity=cst["ident"][:, :]), "pe")

        @block.vector
        def _(v):
            kb.wait(v, "pe")
            v.tensor_copy(bias16["bup"][:, :], ps[0][:, 0:128])
            kb.inc(v.tensor_copy(bias16["bgate"][:, :], ps[1][:, 0:128]), "dve")

        with ExitStack() as sp:
            def sb2(name, shape, dtype=F32):
                return sp.enter_context(nc.sbuf_tensor(name, shape, dtype))
            bufA = sb2("bufA", [128, E * 256])
            bufB = sb2("bufB", [128, E * 256])
            tbufA = sb2("tbufA", [128, 8 * 128])
            tbufB = sb2("tbufB", [128, 8 * 128])
            gs = sb2("gs", [128, E * 4])
            bm = sb2("bm", [128, E * 4])
            thA = sb2("thA", [128, 256])
            thB = sb2("thB", [128, 256])
            scr = {
                "th11": sb2("th11", [1, 1]),
                "mask": sb2("mask", [128, E * 4], mybir.dt.uint32),
                "cand": sb2("cand", [128, E * 4]),
                "emin": sb2("emin", [128, E]),
                "eminT": sb2("eminT", [64, 128]),
                "bigpos": sb2("bigpos", [128, E * 4]),
                "sigrow": sb2("sigrow", [1, 64]),
            }
            fsb = {
                "stile": sb2("stile", [128, 64]),
                "fs0": sb2("fs0", [128, 64]),
                "fsmask": sb2("fsmask", [128, 64]),
                "fsden": sb2("fsden", [128, 1]),
                "fsrb": sb2("fsrb", [64, T], BF16),
            }

            @block.sync
            def _(s):
                kb.wait(s, "qs")
                for gidx in range(8):
                    src = scores_em[gidx * 8:(gidx + 1) * 8, :].rearrange(
                        "e (p f) -> p e f", p=128)
                    kb.inc(s.dma_start(out=bufA[:, :].rearrange(
                        "p (e f) -> p e f", f=256)[:, gidx * 8:(gidx + 1) * 8, :],
                        in_=src), "qs")

            @block.vector
            def _(v):
                kb.wait(v, "qs")
                kb.inc(v.memset(scr["bigpos"][:, :], BIGPOS), "dve")

            emit_sort(kb, block, bufA, bufB, tbufA, tbufB, ps[2], cst, nbat=E)
            emit_gs_bm(kb, block, bufA, gs, bm, nbat=E)
            emit_theta_sigma(kb, block, gs, bm, thA, thB, tbufA, tbufB, ps[2], cst,
                             sig_fm, thbc, sigbc, scr, dram)
            if dbg:
                @block.sync
                def _(s):
                    kb.wait_all(s)
                    kb.inc(s.dma_start(out=dbg["gs"], in_=gs[:, :]), "qs")
                    kb.inc(s.dma_start(out=dbg["bm"], in_=bm[:, :]), "qs")
                    kb.inc(s.dma_start(out=dbg["th"], in_=scr["th11"][:, :]), "qs")
                    kb.inc(s.dma_start(out=dbg["srt"], in_=bufA[:, :]), "qs")
                    kb.inc(s.dma_start(out=dbg["cand"], in_=scr["cand"][:, :]), "qs")
                    kb.inc(s.dma_start(out=dbg["emin"], in_=scr["emin"][:, :]), "qs")
                    kb.inc(s.dma_start(out=dbg["mask"], in_=scr["mask"][:, :]), "qs")
                    kb.inc(s.dma_start(out=dbg["sigbc"], in_=sigbc[:, :]), "qs")
            emit_fs(kb, block, scores_own, sigbc, fsfm, dram["fs"], ps, cst, fsb)

            for engname in ("vector", "gpsimd", "scalar", "tensor", "sync"):
                def mk(e, _n=engname):
                    kb.wait_all(e)
                getattr(block, engname)(mk)

        @block.sync
        def _(s):
            kb.wait(s, "qp")
            for e in range(E):
                for nm in ("Wup", "Wgate", "Wdown"):
                    kb.inc(s.dma_start(out=bufs_rb[:, :],
                                       in_=w16[nm][e].rearrange("p c h -> p (c h)")), "qs")

        bufs = {
            "xq": sbuf(kb, "xq", [128, 8 * 1024]),
            "yq": sbuf(kb, "yq", [128, 8 * 1024], BF16),
            "hgrp": sbuf(kb, "hgrp", [128, 16 * 1024], BF16),
            "dacc": sbuf(kb, "dacc", [128, 16 * 512]),
            "silu": sbuf(kb, "silu", [128, 2 * 512], BF16),
            "fsbc": sbuf(kb, "fsbc", [128, 2 * 1024], BF16),
            "swu": sbuf(kb, "swu", [128, 2 * 2048], BF16),
            "swg": sbuf(kb, "swg", [128, 2 * 2048], BF16),
            "swd": sbuf(kb, "swd", [128, 8 * 2048], BF16),
            "arow": sbuf(kb, "arow", [1, 1024]),
            "brow": sbuf(kb, "brow", [1, 1024]),
            "abc": sbuf(kb, "abc", [128, 1024]),
            "bbc": sbuf(kb, "bbc", [128, 1024]),
            "xsq": sbuf(kb, "xsq", [128, 512], BF16),
            "xbf": sbuf(kb, "xbf", [128, 512], BF16),
            "srow": sbuf(kb, "srow", [1, 1024]),
        }
        if dbg:
            @block.sync
            def _(s):
                kb.wait_all(s)
                kb.inc(s.dma_start(out=dbg["fs"], in_=fsfm[:, :]), "qs")
                kb.inc(s.dma_start(out=dbg["sig"], in_=sig_fm[:, :]), "qs")
        emit_ffn_sweep(kb, block, x_img, x3_img, dram["fs"], w16, bias16,
                       bdown_sb, fsfm, ps, cst, bufs, dram, dbg=dbg)

        @block.sync
        def _(s):
            kb.wait_all(s)
    return nc


def emit_scan(kb, block, at_d, bt_d, kt_d, rt_d, v_d, cexp_d, o_d, ps, cst, bufs,
              epilogue_fn=None, cexp_sb=None):
    NCH = T // 64
    SEG = 512
    NSEG = T // SEG
    CPS = SEG // 64

    a_s, b_s, k_s, r_s = bufs["a_s"], bufs["b_s"], bufs["k_s"], bufs["r_s"]
    v_s = bufs["v_s"]
    cexp = bufs["cexp"]
    yt = bufs["yt"]
    y0p = bufs["y0p"]
    bd = bufs["bd"]
    scmm = bufs["scmm"]
    lad = bufs["lad"]
    mrkt = bufs["mrkt"]
    mkav = bufs["mkav"]
    btm = bufs["btm"]
    xs = bufs["xs"]
    dlt = bufs["dlt"]
    o_tm = bufs["o_tm"]

    if cexp_sb is None:
        @block.sync
        def _(s):
            kb.wait_all(s)
            kb.inc(s.dma_start(out=cexp[:, :], in_=cexp_d), "qs")
    else:
        cexp = cexp_sb

    @block.vector
    def _(v):
        kb.wait(v, "qs")
        kb.inc(v.memset(yt[:, :], 0.0), "dve")

    @block.gpsimd
    def _(g):
        kb.wait(g, "qs")
        kb.inc(g.memset(bd[:, :], 0.0), "pool")

    for seg in range(NSEG):
        @block.sync
        def _(s, seg=seg):
            kb.wait_all(s, ("pe", "dve", "act", "pool", "qs"))
            for c in range(8):
                for nm, buf in ((at_d, a_s), (bt_d, b_s), (kt_d, k_s), (rt_d, r_s)):
                    kb.inc(s.dma_start(out=buf[:, c * SEG:(c + 1) * SEG],
                                       in_=nm[:, c * T + seg * SEG:c * T + (seg + 1) * SEG]), "qs")
            for st in range(SEG // 128):
                kb.inc(s.dma_start(out=v_s[:, st * C:(st + 1) * C],
                                   in_=v_d[seg * SEG + st * 128: seg * SEG + (st + 1) * 128, :]), "qs")

        for hp in range(8):
            for ci in range(CPS):
                ch = seg * CPS + ci
                tsl = slice(ci * 64, (ci + 1) * 64)

                @block.vector
                def _(v, hp=hp, ci=ci, tsl=tsl):
                    kb.wait(v, "qs")
                    kb.wait(v, "pe")
                    for bi, buf in enumerate((a_s, b_s, k_s, r_s)):
                        src = buf[:, hp * SEG:][:, tsl]
                        v.tensor_copy(bd[0:64, bi * 128:bi * 128 + 64], src[0:64, :])
                        v.tensor_copy(bd[64:128, bi * 128 + 64:bi * 128 + 128], src[64:128, :])
                    kb.inc(v.memset(cst["junk"][:, :], 0.0), "dve")

                Abd = bd[:, 0:128]
                Bbd = bd[:, 128:256]
                Kbd = bd[:, 256:384]
                Rbd = bd[:, 384:512]
                Vbd = bd[:, 512:640]

                st0 = (ci * 64) // 128
                r0 = (ci * 64) % 128
                par = r0 // 64
                vs_blk = v_s[:, st0 * C + hp * 128: st0 * C + (hp + 1) * 128]

                @block.tensor
                def _(t, vs_blk=vs_blk, par=par):
                    kb.wait(t, "dve")
                    t.matmul(ps[1][:, 0:64], lhsT=cst[f"vperm{par}a"][:, :],
                             rhs=vs_blk[:, 0:64], start=True, stop=True)
                    kb.inc(t.matmul(ps[1][:, 64:128], lhsT=cst[f"vperm{par}b"][:, :],
                                    rhs=vs_blk[:, 64:128], start=True, stop=True), "pe")

                @block.vector
                def _(v, Vbd=Vbd):
                    kb.wait(v, "pe")
                    kb.inc(v.tensor_copy(Vbd, ps[1][:, 0:128]), "dve")

                @block.tensor
                def _(t, hp=hp):
                    kb.wait(t, "dve")
                    t.matmul(ps[0][:, 0:128], lhsT=Abd, rhs=Bbd, start=True, stop=True)
                    t.matmul(ps[0][:, 128:256], lhsT=Bbd, rhs=Abd, start=True, stop=True)
                    t.matmul(ps[0][:, 256:384], lhsT=Kbd, rhs=Abd, start=True, stop=True)
                    kb.inc(t.matmul(ps[0][:, 384:512], lhsT=Bbd, rhs=Rbd, start=True, stop=True), "pe")

                @block.vector
                def _(v):
                    kb.wait(v, "pe")
                    kb.inc(v.tensor_tensor(scmm[:, :], ps[0][:, :], cst["mask4"][:, :],
                                           op=ALU.mult), "dve")

                M_, N_, Nka, MrbT = (scmm[:, 0:128], scmm[:, 128:256],
                                     scmm[:, 256:384], scmm[:, 384:512])

                @block.tensor
                def _(t):
                    kb.wait(t, "dve")
                    t.matmul(ps[1][:, 0:128], lhsT=Kbd, rhs=Rbd, start=True, stop=True)
                    t.matmul(ps[1][:, 128:256], lhsT=Nka, rhs=Vbd, start=True, stop=True)
                    t.transpose(bufs["ps16"][:, 0:128], in_=Bbd, identity=cst["ident16"][:, :])
                    kb.inc(t.transpose(bufs["ps16"][:, 128:256], in_=Kbd,
                                       identity=cst["ident16"][:, :]), "pe")

                @block.vector
                def _(v):
                    kb.wait(v, "pe")
                    v.tensor_tensor(mrkt[:, :], ps[1][:, 0:128], cst["masku_inc"][:, :],
                                    op=ALU.mult)
                    v.tensor_copy(mkav[:, :], ps[1][:, 128:256])
                    kb.inc(v.tensor_copy(btm[:, :], bufs["ps16"][:, 0:256]), "dve")

                lpairs = [(None, None)]
                prevM, prevN = M_, N_
                for li in range(5):
                    o0 = (li % 2) * 256

                    @block.tensor
                    def _(t, prevM=prevM, prevN=prevN, o0=o0, li=li):
                        kb.wait(t, "dve")
                        t.matmul(ps[2][:, o0:o0 + 128], lhsT=prevN, rhs=prevM,
                                 start=True, stop=True)
                        kb.inc(t.matmul(ps[2][:, o0 + 128:o0 + 256], lhsT=prevM, rhs=prevN,
                                        start=True, stop=True), "pe")

                    @block.vector
                    def _(v, li=li, o0=o0):
                        kb.wait(v, "pe")
                        kb.inc(v.tensor_copy(lad[:, li * 256:(li + 1) * 256],
                                             ps[2][:, o0:o0 + 256]), "dve")

                    prevM = lad[:, li * 256:li * 256 + 128]
                    prevN = lad[:, li * 256 + 128:(li + 1) * 256]

                yth = yt[:, hp * 128:(hp + 1) * 128]
                ce = cexp[:, hp * NCH + ch:hp * NCH + ch + 1]

                @block.vector
                def _(v, yth=yth, ce=ce):
                    kb.wait(v, "dve")
                    kb.inc(v.tensor_scalar_mul(y0p[:, :], yth, ce), "dve")

                @block.tensor
                def _(t):
                    kb.wait(t, "dve")
                    kb.inc(t.matmul(ps[3][:, 0:128], lhsT=Abd, rhs=y0p[:, :],
                                    start=True, stop=True), "pe")

                @block.vector
                def _(v):
                    kb.wait(v, "pe")
                    kb.inc(v.tensor_tensor(xs[:, :], ps[3][:, 0:128], mkav[:, :],
                                           op=ALU.add), "dve")

                for li in range(6):
                    if li == 5:
                        lhs = N_
                    else:
                        lhs = lad[:, (4 - li) * 256 + 128:(5 - li) * 256]
                    o0 = 128 + (li % 2) * 128

                    @block.tensor
                    def _(t, lhs=lhs, o0=o0):
                        kb.wait(t, "dve")
                        kb.inc(t.matmul(ps[3][:, o0:o0 + 128], lhsT=lhs, rhs=xs[:, :],
                                        start=True, stop=True), "pe")

                    @block.vector
                    def _(v, o0=o0):
                        kb.wait(v, "pe")
                        kb.inc(v.tensor_tensor(xs[:, :], ps[3][:, o0:o0 + 128], xs[:, :],
                                               op=ALU.add), "dve")

                @block.tensor
                def _(t):
                    kb.wait(t, "dve")
                    t.matmul(ps[4][:, 0:128], lhsT=Rbd, rhs=y0p[:, :], start=True, stop=False)
                    t.matmul(ps[4][:, 0:128], lhsT=MrbT, rhs=xs[:, :], start=False, stop=False)
                    kb.inc(t.matmul(ps[4][:, 0:128], lhsT=mrkt[:, :], rhs=Vbd,
                                    start=False, stop=True), "pe")

                @block.vector
                def _(v, par=par):
                    kb.wait(v, "pe")
                    kb.inc(v.tensor_copy(bufs["osb"][:, par * 128:(par + 1) * 128],
                                         ps[4][:, 0:128]), "dve")

                if par == 1:
                    @block.tensor
                    def _(t):
                        kb.wait(t, "dve")
                        t.matmul(ps[4][:, 128:192], lhsT=cst["operm0a"][:, :],
                                 rhs=bufs["osb"][:, 0:64], start=True, stop=False)
                        t.matmul(ps[4][:, 192:256], lhsT=cst["operm0b"][:, :],
                                 rhs=bufs["osb"][:, 64:128], start=False, stop=False)
                        t.matmul(ps[4][:, 128:192], lhsT=cst["operm1a"][:, :],
                                 rhs=bufs["osb"][:, 128:192], start=False, stop=False)
                        kb.inc(t.matmul(ps[4][:, 192:256], lhsT=cst["operm1b"][:, :],
                                        rhs=bufs["osb"][:, 192:256], start=False, stop=True), "pe")

                    @block.vector
                    def _(v, hp=hp, st0=st0):
                        kb.wait(v, "pe")
                        dst = o_tm[:, st0 * C + hp * 128: st0 * C + (hp + 1) * 128]
                        kb.inc(v.tensor_copy(dst, ps[4][:, 128:256]), "dve")

                @block.tensor
                def _(t):
                    kb.wait(t, "dve")
                    t.matmul(ps[5][:, 0:128], lhsT=btm[:, 0:128], rhs=xs[:, :],
                             start=True, stop=False)
                    kb.inc(t.matmul(ps[5][:, 0:128], lhsT=btm[:, 128:256], rhs=Vbd,
                                    start=False, stop=True), "pe")

                @block.vector
                def _(v, yth=yth, ce=ce):
                    kb.wait(v, "pe")
                    kb.inc(v.tensor_scalar_mul(dlt[:, :], ps[5][:, 0:128], ce), "dve")
                    kb.wait(v, "dve")
                    kb.inc(v.tensor_scalar_mul(yth, yth, ce), "dve")
                    kb.wait(v, "dve")
                    kb.inc(v.scalar_tensor_tensor(out=yth, in0=yth, scalar=ce,
                                                  in1=dlt[:, :], op0=ALU.mult,
                                                  op1=ALU.add), "dve")

        if epilogue_fn is None:
            @block.sync
            def _(s, seg=seg):
                kb.wait(s, "dve")
                for st in range(SEG // 128):
                    kb.inc(s.dma_start(out=o_d[seg * SEG + st * 128: seg * SEG + (st + 1) * 128, :],
                                       in_=o_tm[:, st * C:(st + 1) * C]), "qs")
        else:
            epilogue_fn(seg)


VEC_IDX = {"x_r": 0, "x_w": 1, "x_k": 2, "x_v": 3, "x_a": 4, "k_k": 5, "k_a": 6,
           "r_k": 7, "wbias": 8, "neg_wbias": 9, "abias": 10, "one_m_ka": 11}


def vcol(aw, name, c):
    return aw["vecs"][:, VEC_IDX[name] * 8 + c:VEC_IDX[name] * 8 + c + 1]


def emit_attn_prep(kb, block, x1_img, aw, ps, ps16, cst, bufs, dram, stop_after=None):
    W = 512
    NT = T // W
    A, Bb, Cc, Ee = bufs["fA"], bufs["fB"], bufs["fC"], bufs["fE"]
    b = [bufs[f"b{i}"] for i in range(6)]
    z64 = bufs["z64"]
    carry = bufs["carry"]
    cval = bufs["cval"]
    arow, brow = bufs["arow"], bufs["brow"]
    abc, bbc = bufs["abc"], bufs["bbc"]
    class _SrowView:
        pass
    cexp = bufs["cexp_sb"]
    bonus = bufs["bonus"]

    def CH(buf, c):
        return buf[:, c * W:(c + 1) * W]

    @block.vector
    def _(v):
        kb.wait_all(v)
        kb.inc(v.memset(carry[:, :], 0.0), "dve")

    for ti in range(NT):
        @block.sync
        def _(s, ti=ti):
            kb.wait_all(s, ("dve", "pe", "act", "pool", "qs"))
            for c in range(8):
                kb.inc(s.dma_start(out=CH(A, c),
                                   in_=x1_img[:, c * T + ti * W:c * T + (ti + 1) * W]), "qs")

        @block.scalar
        def _(sc):
            kb.wait(sc, "qs")
            kb.wait(sc, "pe")
            for c in range(8):
                kb.inc(sc.activation(out=CH(b[0], c), in_=CH(A, c), func=AF.Square), "act")

        @block.gpsimd
        def _(g):
            kb.wait(g, "qs")
            kb.wait(g, "pe")
            for c in range(8):
                kb.inc(g.tensor_copy(CH(b[5], c), CH(A, c)), "pool")

        @block.tensor
        def _(t):
            kb.wait(t, "act")
            kb.wait(t, "pool")
            kb.wait(t, "dve")
            for c in range(8):
                t.matmul(ps[2][0:1, :], lhsT=cst["ones1_16"][:, :], rhs=CH(b[5], c),
                         start=(c == 0), stop=(c == 7))
            for c in range(8):
                ins_ = t.matmul(ps[3][0:1, :], lhsT=cst["ones1_16"][:, :], rhs=CH(b[0], c),
                                start=(c == 0), stop=(c == 7))
            kb.inc(ins_, "pe")

        @block.vector
        def _(v):
            kb.wait(v, "pe")
            v.tensor_scalar_mul(brow[0:1, 512:1024], ps[2][0:1, :], 1.0 / C)
            kb.inc(v.tensor_scalar_mul(arow[0:1, 512:1024], ps[3][0:1, :], 1.0 / C), "dve")
            kb.wait(v, "dve")
            kb.inc(v.tensor_tensor(arow[0:1, 0:512], brow[0:1, 512:1024], brow[0:1, 512:1024],
                                   op=ALU.mult), "dve")
            kb.wait(v, "dve")
            kb.inc(v.tensor_tensor(arow[0:1, 0:512], arow[0:1, 512:1024], arow[0:1, 0:512],
                                   op=ALU.subtract), "dve")

        @block.scalar
        def _(sc):
            kb.wait(sc, "dve")
            kb.inc(sc.activation(out=brow[0:1, 0:512], in_=arow[0:1, 0:512], func=AF.Ln,
                                 bias=cst["eps1"][0:1, 0:1]), "act")
            kb.wait(sc, "act")
            kb.inc(sc.activation(out=arow[0:1, 0:512], in_=brow[0:1, 0:512], func=AF.Exp,
                                 scale=-0.5), "act")

        @block.vector
        def _(v):
            kb.wait(v, "act")
            kb.wait(v, "dve")
            kb.inc(v.tensor_tensor(brow[0:1, 0:512], brow[0:1, 512:1024], arow[0:1, 0:512],
                                   op=ALU.mult), "dve")

        @block.tensor
        def _(t):
            kb.wait(t, "dve")
            t.matmul(ps[2][:, :], lhsT=cst["onescol"][:, :], rhs=arow[0:1, 0:512],
                     start=True, stop=True)
            kb.inc(t.matmul(ps[3][:, :], lhsT=cst["onescol"][:, :], rhs=brow[0:1, 0:512],
                            start=True, stop=True), "pe")

        @block.vector
        def _(v):
            kb.wait(v, "pe")
            v.tensor_copy(abc[:, :], ps[2][:, :])
            kb.inc(v.tensor_copy(bbc[:, :], ps[3][:, :]), "dve")
            kb.wait(v, "dve")
            for c in range(8):
                kb.inc(v.tensor_tensor(CH(Bb, c), CH(A, c), abc[:, :], op=ALU.mult), "dve")
            kb.wait(v, "dve")
            for c in range(8):
                kb.inc(v.tensor_tensor(CH(Bb, c), CH(Bb, c), bbc[:, :], op=ALU.subtract), "dve")

        @block.vector
        def _(v, ti=ti):
            kb.wait(v, "dve")
            for c in range(8):
                v.tensor_tensor(Cc[:, c * W + 1:(c + 1) * W], Bb[:, c * W:(c + 1) * W - 1],
                                Bb[:, c * W + 1:(c + 1) * W], op=ALU.subtract)
            kb.inc(v.tensor_tensor(Cc[:, 0:4096:W], carry[:, :], Bb[:, 0:4096:W],
                                   op=ALU.subtract), "dve")
            kb.wait(v, "dve")
            kb.inc(v.tensor_copy(carry[:, :], Bb[:, W - 1:4096:W]), "dve")
            kb.wait(v, "dve")
            for vi, nm in enumerate(("x_r", "x_w", "x_k", "x_v", "x_a")):
                for c in range(8):
                    ins_ = v.scalar_tensor_tensor(
                        out=CH(b[vi], c), in0=CH(Cc, c), scalar=vcol(aw, nm, c),
                        in1=CH(Bb, c), op0=ALU.mult, op1=ALU.add)
            kb.inc(ins_, "dve")

        if stop_after == "mix":
            return
        for oc in range(8):
            @block.tensor
            def _(t, oc=oc):
                kb.wait(t, "dve")
                kb.wait(t, "act")
                for c in range(8):
                    ins_ = t.matmul(ps[0][:, :],
                                    lhsT=aw["wr16"][:, c * 1024 + oc * 128:c * 1024 + (oc + 1) * 128],
                                    rhs=CH(b[0], c), start=(c == 0), stop=(c == 7))
                kb.inc(ins_, "pe")

            @block.scalar
            def _(sc, oc=oc):
                kb.wait(sc, "pe")
                kb.inc(sc.copy(CH(b[5], oc), ps[0][:, :]), "act")

        if stop_after == "rproj":
            return
        @block.tensor
        def _(t):
            kb.wait(t, "dve")
            kb.wait(t, "act")
            for c in range(8):
                ins_ = t.matmul(ps[5][0:64, 0:512], lhsT=aw["wla16"][:, c * 64:(c + 1) * 64],
                                rhs=CH(b[1], c), start=(c == 0), stop=(c == 7))
            kb.inc(ins_, "pe")

        @block.scalar
        def _(sc):
            kb.wait(sc, "pe")
            kb.inc(sc.activation(out=z64, in_=ps[5][0:64, 0:512], func=AF.Tanh), "act")

        for wv in range(2):
            for oci in range(4):
                oc = wv * 4 + oci

                @block.tensor
                def _(t, oc=oc, wv=wv, oci=oci):
                    kb.wait(t, "act")
                    if wv > 0 and oci == 0:
                        kb.wait(t, "dve")
                    kb.inc(t.matmul(ps[5][:, 0:512], lhsT=aw["wlb16"][:, oc * 128:(oc + 1) * 128],
                                    rhs=z64, start=True, stop=True), "pe")

                @block.scalar
                def _(sc, oc=oc, oci=oci):
                    kb.wait(sc, "pe")
                    sc.activation(out=CH(Ee, oci), in_=ps[5][:, 0:512], func=AF.Relu,
                                  scale=-1.0, bias=vcol(aw, "neg_wbias", oc))
                    kb.inc(sc.activation(out=CH(A, oc), in_=ps[5][:, 0:512],
                                         func=AF.Abs, bias=vcol(aw, "wbias", oc)), "act")

            @block.scalar
            def _(sc, wv=wv):
                kb.wait(sc, "act")
                for oci in range(4):
                    kb.inc(sc.activation(out=CH(A, wv * 4 + oci), in_=CH(A, wv * 4 + oci),
                                         func=AF.Exp, scale=-1.0), "act")
                kb.wait(sc, "act")
                for oci in range(4):
                    kb.inc(sc.activation(out=CH(A, wv * 4 + oci), in_=CH(A, wv * 4 + oci),
                                         func=AF.Ln, bias=1.0), "act")

            @block.vector
            def _(v, wv=wv):
                kb.wait(v, "act")
                for oci in range(4):
                    kb.inc(v.tensor_tensor(CH(A, wv * 4 + oci), CH(A, wv * 4 + oci),
                                           CH(Ee, oci), op=ALU.add), "dve")
                kb.wait(v, "dve")
                for oci in range(4):
                    kb.inc(v.tensor_scalar(CH(A, wv * 4 + oci), CH(A, wv * 4 + oci), -1.0, -0.5,
                                           op0=ALU.mult, op1=ALU.add), "dve")

        if stop_after == "wfm":
            return
        @block.vector
        def _(v, ti=ti):
            kb.wait(v, "dve")
            for c in range(8):
                for ch in range(8):
                    ins_ = v.tensor_tensor_scan(
                        out=Cc[:, c * W + ch * 64:c * W + (ch + 1) * 64],
                        data0=cst["ones64"][:, :],
                        data1=A[:, c * W + ch * 64:c * W + (ch + 1) * 64],
                        initial=0.0, op0=ALU.mult, op1=ALU.add)
            kb.inc(ins_, "dve")
            kb.wait(v, "dve")
            kb.inc(v.tensor_scalar_mul(
                cval[:, :], Cc[:, :].rearrange("p (cc t) -> p cc t", t=64)[:, :, 63], 0.5), "dve")

        @block.scalar
        def _(sc, ti=ti):
            kb.wait(sc, "dve")
            kb.inc(sc.activation(
                out=cexp[:, :].rearrange("p (cc n) -> p cc n", n=64)[:, :, ti * 8:(ti + 1) * 8],
                in_=cval[:, :].rearrange("p (cc n) -> p cc n", n=8), func=AF.Exp), "act")

        if stop_after == "cum":
            return
        for c in range(8):
            @block.vector
            def _(v, c=c):
                kb.wait(v, "dve")
                if c == 0:
                    kb.wait(v, "act")
                cv = cval[:, c * 8:(c + 1) * 8]
                g_ = Cc[:, c * W:(c + 1) * W].rearrange("p (ch t) -> p ch t", t=64)
                w_ = A[:, c * W:(c + 1) * W].rearrange("p (ch t) -> p ch t", t=64)
                e_ = Bb[:, c * W:(c + 1) * W].rearrange("p (ch t) -> p ch t", t=64)
                cbc = cv.unsqueeze(2).to_broadcast([128, 8, 64])
                kb.inc(v.tensor_tensor(e_, g_, w_, op=ALU.subtract), "dve")
                kb.wait(v, "dve")
                kb.inc(v.tensor_tensor(e_, e_, cbc, op=ALU.subtract), "dve")

        if stop_after == "efa_arg":
            return
        @block.tensor
        def _(t):
            kb.wait(t, "dve")
            kb.wait(t, "act")
            for c in range(8):
                ins_ = t.matmul(ps[5][0:64, 0:512], lhsT=aw["ala16"][:, c * 64:(c + 1) * 64],
                                rhs=CH(b[4], c), start=(c == 0), stop=(c == 7))
            kb.inc(ins_, "pe")

        @block.scalar
        def _(sc):
            kb.wait(sc, "pe")
            kb.inc(sc.copy(z64, ps[5][0:64, 0:512]), "act")

        for oc in range(8):
            @block.tensor
            def _(t, oc=oc):
                kb.wait(t, "act")
                kb.inc(t.matmul(ps[5][:, 0:512], lhsT=aw["alb16"][:, oc * 128:(oc + 1) * 128],
                                rhs=z64, start=True, stop=True), "pe")

            @block.scalar
            def _(sc, oc=oc):
                kb.wait(sc, "pe")
                kb.inc(sc.activation(out=CH(b[1], oc), in_=ps[5][:, 0:512],
                                     func=AF.Sigmoid, bias=vcol(aw, "abias", oc)), "act")

        if stop_after == "alora":
            return
        for oc in range(8):
            @block.tensor
            def _(t, oc=oc):
                kb.wait(t, "dve")
                kb.wait(t, "act")
                for c in range(8):
                    ins_ = t.matmul(ps[0][:, :],
                                    lhsT=aw["wk16"][:, c * 1024 + oc * 128:c * 1024 + (oc + 1) * 128],
                                    rhs=CH(b[2], c), start=(c == 0), stop=(c == 7))
                kb.inc(ins_, "pe")

            @block.scalar
            def _(sc, oc=oc):
                kb.wait(sc, "pe")
                kb.inc(sc.copy(CH(A, oc), ps[0][:, :]), "act")

        if stop_after == "kproj":
            return
        for c in range(8):
            @block.vector
            def _(v, c=c):
                kb.wait(v, "act")
                if c == 0:
                    kb.wait(v, "dve")
                kb.inc(v.tensor_scalar_mul(CH(Ee, c % 4), CH(A, c), vcol(aw, "k_k", c)), "dve")

            @block.scalar
            def _(sc, c=c):
                kb.wait(sc, "dve")
                kb.inc(sc.activation(out=CH(b[0], c), in_=CH(Ee, c % 4), func=AF.Square), "act")

        @block.tensor
        def _(t):
            kb.wait(t, "act")
            for c in range(8):
                ins_ = t.matmul(ps[5][0:16, 0:512], lhsT=aw[f"bdsel{c}"][:, :],
                                rhs=CH(b[0], c), start=(c == 0), stop=(c == 7))
            kb.inc(ins_, "pe")

        @block.vector
        def _(v):
            kb.wait(v, "pe")
            kb.inc(v.tensor_scalar_max(abc[0:16, 0:512], ps[5][0:16, 0:512], 1e-24), "dve")

        @block.scalar
        def _(sc):
            kb.wait(sc, "dve")
            kb.inc(sc.activation(out=bbc[0:16, 0:512], in_=abc[0:16, 0:512],
                                 func=AF.Ln), "act")
            kb.wait(sc, "act")
            kb.inc(sc.activation(out=b[0][0:16, 0:512], in_=bbc[0:16, 0:512], func=AF.Exp,
                                 scale=-0.5), "act")

        for c in range(8):
            @block.tensor
            def _(t, c=c):
                kb.wait(t, "act")
                if c > 0:
                    kb.wait(t, "dve")
                kb.inc(t.matmul(ps[5][:, 0:512], lhsT=aw[f"hsel{c}"][:, :],
                                rhs=b[0][0:16, 0:512], start=True, stop=True), "pe")

            @block.vector
            def _(v, c=c):
                kb.wait(v, "pe")
                kb.inc(v.tensor_scalar_mul(CH(Ee, c % 4), CH(A, c), vcol(aw, "k_k", c)), "dve")
                kb.wait(v, "dve")
                kb.inc(v.tensor_tensor(CH(b[2], c), CH(Ee, c % 4), ps[5][:, 0:512],
                                       op=ALU.mult), "dve")

        if stop_after == "t0":
            return
        for st in range(4):
            for nh2 in range(2):
                @block.tensor
                def _(t, st=st, nh2=nh2):
                    kb.wait(t, "dve")
                    if st == 0 and nh2 == 0:
                        kb.wait(t, "act")
                    for c in range(8):
                        ins_ = t.matmul(ps[4][:, 0:512],
                                        lhsT=b[3][:, c * W + st * 128:c * W + (st + 1) * 128],
                                        rhs=aw["wv16"][:, c * 1024 + nh2 * 512:c * 1024 + (nh2 + 1) * 512],
                                        start=(c == 0), stop=(c == 7))
                    kb.inc(ins_, "pe")

                @block.vector
                def _(v, st=st, nh2=nh2):
                    kb.wait(v, "pe")
                    kb.inc(v.tensor_copy(
                        b[4][:, st * 1024 + nh2 * 512:st * 1024 + (nh2 + 1) * 512],
                        ps[4][:, 0:512]), "dve")

        @block.sync
        def _(s, ti=ti):
            kb.wait(s, "dve")
            for st in range(4):
                kb.inc(s.dma_start(out=dram["v_d"][ti * 512 + st * 128: ti * 512 + (st + 1) * 128, :],
                                   in_=b[4][:, st * 1024:(st + 1) * 1024]), "qs")

        if stop_after == "vproj":
            return
        @block.tensor
        def _(t):
            kb.wait(t, "dve")
            for c in range(8):
                ins_ = t.matmul(ps[5][:, 0:512], lhsT=aw["gla16"][:, c * 128:(c + 1) * 128],
                                rhs=CH(b[5], c), start=(c == 0), stop=(c == 7))
            kb.inc(ins_, "pe")

        @block.scalar
        def _(sc):
            kb.wait(sc, "pe")
            kb.inc(sc.activation(out=b[0][:, 0:512], in_=ps[5][:, 0:512], func=AF.Sigmoid), "act")

        for st in range(4):
            for nh2 in range(2):
                @block.tensor
                def _(t, st=st, nh2=nh2):
                    kb.wait(t, "act")
                    if st > 0 or nh2 > 0:
                        kb.wait(t, "dve")
                    t.matmul(ps[4][:, 0:512],
                             lhsT=b[0][:, st * 128:(st + 1) * 128],
                             rhs=aw["glb16"][:, nh2 * 512:(nh2 + 1) * 512],
                             start=True, stop=False)
                    ins_ = t.matmul(ps[4][:, 0:512],
                                    lhsT=cst["onescol16"][0:1, 0:128],
                                    rhs=aw["gbias_row16"][0:1, nh2 * 512:(nh2 + 1) * 512],
                                    start=False, stop=True)
                    kb.inc(ins_, "pe")

                @block.vector
                def _(v, st=st, nh2=nh2):
                    kb.wait(v, "pe")
                    kb.inc(v.tensor_copy(
                        b[3][:, st * 1024 + nh2 * 512:st * 1024 + (nh2 + 1) * 512],
                        ps[4][:, 0:512]), "dve")

        @block.sync
        def _(s, ti=ti):
            kb.wait(s, "dve")
            for st in range(4):
                kb.inc(s.dma_start(out=dram["g_d"][ti * 512 + st * 128: ti * 512 + (st + 1) * 128, :],
                                   in_=b[3][:, st * 1024:(st + 1) * 1024]), "qs")

        if stop_after == "ggate":
            return
        def ef_arg(v, c, mode):
            cv = cval[:, c * 8:(c + 1) * 8]
            g_ = Cc[:, c * W:(c + 1) * W].rearrange("p (ch t) -> p ch t", t=64)
            e_ = Bb[:, c * W:(c + 1) * W].rearrange("p (ch t) -> p ch t", t=64)
            w_ = A
            cbc = cv.unsqueeze(2).to_broadcast([128, 8, 64])
            if mode == "b":
                kb.inc(v.tensor_tensor(e_, cbc, g_, op=ALU.subtract), "dve")
            else:
                kb.inc(v.tensor_tensor(e_, g_, cbc, op=ALU.subtract), "dve")

        for c in range(8):
            @block.scalar
            def _(sc, c=c):
                kb.wait(sc, "dve")
                if c == 0:
                    kb.wait(sc, "qs")
                kb.inc(sc.activation(out=CH(Bb, c), in_=CH(Bb, c), func=AF.Exp), "act")

            @block.vector
            def _(v, c=c):
                kb.wait(v, "act")
                kb.inc(v.scalar_tensor_tensor(out=CH(b[0], c), in0=CH(b[2], c), scalar=-1.0,
                                              in1=CH(Bb, c), op0=ALU.mult, op1=ALU.mult), "dve")

        @block.sync
        def _(s, ti=ti):
            kb.wait(s, "dve")
            for c in range(8):
                kb.inc(s.dma_start(out=dram["at"][:, c * T + ti * W:c * T + (ti + 1) * W],
                                   in_=CH(b[0], c)), "qs")

        if stop_after == "efa":
            return
        for c in range(8):
            @block.vector
            def _(v, c=c):
                kb.wait(v, "dve")
                kb.wait(v, "qs")
                ef_arg(v, c, "b")

            @block.scalar
            def _(sc, c=c):
                kb.wait(sc, "dve")
                kb.inc(sc.activation(out=CH(Bb, c), in_=CH(Bb, c), func=AF.Exp), "act")

            @block.vector
            def _(v, c=c):
                kb.wait(v, "act")
                kb.wait(v, "dve")
                kb.inc(v.tensor_tensor(CH(Ee, c % 4), CH(b[2], c), CH(b[1], c), op=ALU.mult), "dve")
                kb.wait(v, "dve")
                kb.inc(v.tensor_tensor(CH(b[0], c), CH(Ee, c % 4), CH(Bb, c), op=ALU.mult), "dve")
                kb.wait(v, "dve")
                kb.inc(v.tensor_scalar(CH(Ee, c % 4), CH(b[1], c), vcol(aw, "k_a", c),
                                       vcol(aw, "one_m_ka", c), op0=ALU.mult, op1=ALU.add), "dve")
                kb.wait(v, "dve")
                kb.inc(v.tensor_tensor(CH(A, c), CH(A, c), CH(Ee, c % 4), op=ALU.mult), "dve")
                kb.wait(v, "dve")
                kb.inc(v.tensor_tensor(CH(b[4], c), CH(A, c), CH(Bb, c), op=ALU.mult), "dve")

        @block.sync
        def _(s, ti=ti):
            kb.wait(s, "dve")
            for c in range(8):
                kb.inc(s.dma_start(out=dram["bt"][:, c * T + ti * W:c * T + (ti + 1) * W],
                                   in_=CH(b[0], c)), "qs")
                kb.inc(s.dma_start(out=dram["kt"][:, c * T + ti * W:c * T + (ti + 1) * W],
                                   in_=CH(b[4], c)), "qs")

        if stop_after == "efb":
            return
        for c in range(8):
            @block.vector
            def _(v, c=c):
                kb.wait(v, "dve")
                kb.wait(v, "qs")
                ef_arg(v, c, "r")

            @block.scalar
            def _(sc, c=c):
                kb.wait(sc, "dve")
                kb.inc(sc.activation(out=CH(Bb, c), in_=CH(Bb, c), func=AF.Exp), "act")

            @block.vector
            def _(v, c=c):
                kb.wait(v, "act")
                kb.wait(v, "dve")
                kb.inc(v.tensor_tensor(CH(b[0], c), CH(b[5], c), CH(Bb, c), op=ALU.mult), "dve")
                kb.wait(v, "dve")
                kb.inc(v.tensor_tensor(CH(Ee, c % 4), CH(b[5], c), CH(A, c), op=ALU.mult), "dve")
                kb.wait(v, "dve")
                kb.inc(v.tensor_scalar_mul(CH(b[4], c), CH(Ee, c % 4), vcol(aw, "r_k", c)), "dve")

        @block.sync
        def _(s, ti=ti):
            kb.wait(s, "dve")
            for c in range(8):
                kb.inc(s.dma_start(out=dram["rt"][:, c * T + ti * W:c * T + (ti + 1) * W],
                                   in_=CH(b[0], c)), "qs")

        @block.tensor
        def _(t, ti=ti):
            kb.wait(t, "dve")
            for c in range(8):
                ins_ = t.matmul(ps[5][0:16, 0:512], lhsT=aw[f"bdsel{c}"][:, :],
                                rhs=CH(b[4], c), start=(c == 0), stop=(c == 7))
            kb.inc(ins_, "pe")

        @block.vector
        def _(v, ti=ti):
            kb.wait(v, "pe")
            kb.inc(v.tensor_copy(bonus[:, ti * W:(ti + 1) * W], ps[5][0:16, 0:512]), "dve")


def emit_attn_setup(kb, block, nc, a_in, aw, aw16_dram, ps, cst, rbbuf):
    @block.gpsimd
    def _(g):
        for nm, src, csz in (("wr16", a_in["W_r"], 1024), ("wk16", a_in["W_k"], 1024),
                             ("wv16", a_in["W_v"], 1024), ("wo16", a_in["W_o"], 1024),
                             ("wla16", a_in["w_lora_a"], 64), ("ala16", a_in["a_lora_a"], 64),
                             ("gla16", a_in["g_lora_a"], 128)):
            kb.inc(g.dma_start(out=aw16_dram[nm],
                               in_=src.rearrange("(c p) n -> p c n", p=128)), "qp")
        for nm, src in (("wlb16", a_in["w_lora_b"]), ("alb16", a_in["a_lora_b"]),
                        ("glb16", a_in["g_lora_b"])):
            kb.inc(g.dma_start(out=aw16_dram[nm], in_=src), "qp")
        kb.inc(g.dma_start(out=aw16_dram["gbias"], in_=a_in["g_lora_bias"].unsqueeze(0)), "qp")

    @block.sync
    def _(s):
        kb.wait(s, "qp")
        for nm in ("wr16", "wk16", "wv16", "wla16", "ala16", "gla16"):
            kb.inc(s.dma_start(out=aw[nm][:, :],
                               in_=aw16_dram[nm].rearrange("p a b -> p (a b)")), "qs")
        for nm in ("wlb16", "alb16", "glb16"):
            kb.inc(s.dma_start(out=aw[nm][:, :], in_=aw16_dram[nm]), "qs")
        kb.inc(s.dma_start(out=aw["gbias_row16"][:, :], in_=aw16_dram["gbias"]), "qs")
        kb.wait(s, "qs")
        for nm in ("wr16", "wk16", "wv16", "wla16", "ala16", "gla16"):
            kb.inc(s.dma_start(out=aw[nm][:, :],
                               in_=aw16_dram[nm].rearrange("p a b -> p (a b)")), "qs")
        for nm in ("wlb16", "alb16", "glb16"):
            kb.inc(s.dma_start(out=aw[nm][:, :], in_=aw16_dram[nm]), "qs")
        kb.inc(s.dma_start(out=aw["gbias_row16"][:, :], in_=aw16_dram["gbias"]), "qs")
        vl = [("x_r", a_in["x_r"]), ("x_w", a_in["x_w"]), ("x_k", a_in["x_k"]),
              ("x_v", a_in["x_v"]), ("x_a", a_in["x_a"]), ("k_k", a_in["k_k"]),
              ("k_a", a_in["k_a"]), ("r_k", a_in["r_k"]),
              ("wbias", a_in["w_lora_bias"]), ("abias", a_in["a_lora_bias"])]
        for nm, src in vl:
            vi = VEC_IDX[nm]
            kb.inc(s.dma_start(out=aw["vstage"][vi * 8:(vi + 1) * 8, :],
                               in_=src.rearrange("(c p) -> c p", p=128) if len(src.shape) == 1
                               else src.rearrange("h d -> (h d)").rearrange("(c p) -> c p", p=128)), "qs")
        kb.inc(s.dma_start(out=aw["gnwrow"][0:1, :], in_=a_in["gn_weight"].unsqueeze(0)), "qs")
        kb.inc(s.dma_start(out=aw["gnbrow"][0:1, :], in_=a_in["gn_bias"].unsqueeze(0)), "qs")

    @block.tensor
    def _(t):
        kb.wait(t, "qs")
        kb.inc(t.transpose(ps[0][:, 0:128], in_=aw["vstage"][:, :],
                           identity=cst["ident"][:, :]), "pe")

    @block.vector
    def _(v):
        kb.wait(v, "pe")
        kb.inc(v.tensor_copy(aw["vecs"][:, 0:96], ps[0][:, 0:96]), "dve")
        kb.wait(v, "dve")
        kb.inc(v.tensor_scalar_mul(aw["vecs"][:, VEC_IDX["neg_wbias"] * 8:VEC_IDX["neg_wbias"] * 8 + 8],
                                   aw["vecs"][:, VEC_IDX["wbias"] * 8:VEC_IDX["wbias"] * 8 + 8], -1.0), "dve")
        kb.wait(v, "dve")
        kb.inc(v.tensor_scalar(aw["vecs"][:, VEC_IDX["one_m_ka"] * 8:VEC_IDX["one_m_ka"] * 8 + 8],
                               aw["vecs"][:, VEC_IDX["k_a"] * 8:VEC_IDX["k_a"] * 8 + 8],
                               -1.0, 1.0, op0=ALU.mult, op1=ALU.add), "dve")
        kb.wait(v, "dve")
        kb.inc(v.memset(cst["junk"][:, :], 0.0), "dve")


def emit_gnbc(kb, block, aw, ps, cst):
    for half in range(2):
        @block.tensor
        def _(t, half=half):
            kb.wait(t, "dve")
            kb.wait(t, "act")
            t.matmul(ps[1][:, :], lhsT=cst["onescol"][:, :],
                     rhs=aw["gnwrow"][0:1, half * 512:(half + 1) * 512], start=True, stop=True)
            kb.inc(t.matmul(ps[2][:, :], lhsT=cst["onescol"][:, :],
                            rhs=aw["gnbrow"][0:1, half * 512:(half + 1) * 512],
                            start=True, stop=True), "pe")

        @block.vector
        def _(v, half=half):
            kb.wait(v, "pe")
            v.tensor_copy(aw["gnwbc"][:, half * 512:(half + 1) * 512], ps[1][:, :])
            kb.inc(v.tensor_copy(aw["gnbbc"][:, half * 512:(half + 1) * 512], ps[2][:, :]), "dve")


def make_attn_epilogue(kb, block, aw, ps, ps16, cst, sb2, dram, x1_img, x2_img):
    o_tm, v_s, bonus = sb2["o_tm"], sb2["v_s"], sb2["bonus"]
    ep_g, ep_st, ep_st2 = sb2["ep_g"], sb2["ep_st"], sb2["ep_st2"]
    ep_sq, ep_bt = sb2["ep_sq"], sb2["ep_bt"]
    ep_ogf, ep_x1 = sb2["ep_ogf"], sb2["ep_x1"]
    SEG = 512

    def epi(seg):
        tok0 = seg * SEG

        for st in range(4):
            @block.sync
            def _(s, st=st):
                kb.wait(s, "dve")
                kb.inc(s.dma_start(out=ep_g[:, 0:1024],
                                   in_=dram["g_d"][tok0 + st * 128: tok0 + (st + 1) * 128, :]),
                       "qs")
            o = o_tm[:, st * C:(st + 1) * C]
            ov = o.rearrange("p (h i) -> p h i", i=64)

            @block.tensor
            def _(t, st=st):
                kb.wait(t, "dve")
                kb.inc(t.transpose(ps16[:, 0:16],
                                   in_=bonus[0:16, tok0 + st * 128: tok0 + (st + 1) * 128],
                                   identity=cst["ident16"][0:16, 0:16]), "pe")

            @block.vector
            def _(v, st=st):
                kb.wait(v, "pe")
                kb.inc(v.tensor_copy(ep_bt[:, st * 16:(st + 1) * 16], ps16[:, 0:16]), "dve")

            @block.scalar
            def _(sc, o=o):
                kb.wait(sc, "dve")
                kb.inc(sc.activation(out=ep_sq[:, :], in_=o, func=AF.Square), "act")

            @block.vector
            def _(v, o=o, ov=ov):
                kb.wait(v, "dve")
                kb.inc(v.reduce_sum(ep_st[:, :], ov, axis=AX.X), "dve")
                kb.wait(v, "act")
                kb.inc(v.reduce_sum(ep_st2[:, :],
                                    ep_sq[:, :].rearrange("p (h i) -> p h i", i=64),
                                    axis=AX.X), "dve")
                kb.wait(v, "dve")
                kb.inc(v.tensor_scalar_mul(ep_st[:, :], ep_st[:, :], 1.0 / 64), "dve")
                kb.wait(v, "dve")
                kb.inc(v.tensor_scalar_mul(ep_st2[:, :], ep_st2[:, :], 1.0 / 64), "dve")
                kb.wait(v, "dve")
                kb.inc(v.tensor_tensor(ep_sq[:, 0:16], ep_st[:, :], ep_st[:, :],
                                       op=ALU.mult), "dve")
                kb.wait(v, "dve")
                kb.inc(v.tensor_tensor(ep_st2[:, :], ep_st2[:, :], ep_sq[:, 0:16],
                                       op=ALU.subtract), "dve")

            @block.scalar
            def _(sc):
                kb.wait(sc, "dve")
                kb.inc(sc.activation(out=ep_st2[:, :], in_=ep_st2[:, :], func=AF.Ln,
                                     bias=cst["eps1"][:, 0:1]), "act")
                kb.wait(sc, "act")
                kb.inc(sc.activation(out=ep_st2[:, :], in_=ep_st2[:, :], func=AF.Exp,
                                     scale=-0.5), "act")

            @block.vector
            def _(v, st=st, o=o, ov=ov):
                kb.wait(v, "act")
                kb.wait(v, "qs")
                mbc = ep_st[:, :].unsqueeze(2).to_broadcast([128, 16, 64])
                rbc = ep_st2[:, :].unsqueeze(2).to_broadcast([128, 16, 64])
                bbc_ = ep_bt[:, st * 16:(st + 1) * 16].unsqueeze(2).to_broadcast([128, 16, 64])
                vv_ = v_s[:, st * C:(st + 1) * C].rearrange("p (h i) -> p h i", i=64)
                kb.inc(v.tensor_tensor(ov, ov, mbc, op=ALU.subtract), "dve")
                kb.wait(v, "dve")
                kb.inc(v.tensor_tensor(ov, ov, rbc, op=ALU.mult), "dve")
                kb.wait(v, "dve")
                kb.inc(v.tensor_tensor(o, o, aw["gnwbc"][:, :], op=ALU.mult), "dve")
                kb.wait(v, "dve")
                kb.inc(v.tensor_tensor(o, o, aw["gnbbc"][:, :], op=ALU.add), "dve")
                kb.wait(v, "dve")
                kb.inc(v.tensor_tensor(ep_sq[:, :].rearrange("p (h i) -> p h i", i=64),
                                       vv_, bbc_, op=ALU.mult), "dve")
                kb.wait(v, "dve")
                kb.inc(v.tensor_tensor(o, o, ep_sq[:, :], op=ALU.add), "dve")
                kb.wait(v, "dve")
                kb.inc(v.tensor_tensor(o, o, ep_g[:, 0:1024], op=ALU.mult), "dve")

            for half in range(2):
                @block.tensor
                def _(t, st=st, half=half):
                    kb.wait(t, "dve")
                    for cc in range(4):
                        c = half * 4 + cc
                        ins_ = t.transpose(ps[half][:, cc * 128:(cc + 1) * 128],
                                           in_=o_tm[:, st * C + c * 128: st * C + (c + 1) * 128],
                                           identity=cst["ident"][:, :])
                    kb.inc(ins_, "pe")

                @block.vector
                def _(v, st=st, half=half):
                    kb.wait(v, "pe")
                    for cc in range(4):
                        c = half * 4 + cc
                        ins_ = v.tensor_copy(
                            ep_ogf[:, c * 512 + st * 128: c * 512 + (st + 1) * 128],
                            ps[half][:, cc * 128:(cc + 1) * 128])
                    kb.inc(ins_, "dve")

        for oc in range(8):
            @block.sync
            def _(s, oc=oc):
                kb.wait(s, "dve")
                kb.inc(s.dma_start(out=ep_x1[:, (oc % 4) * 512:(oc % 4 + 1) * 512],
                                   in_=x1_img[:, oc * T + tok0: oc * T + tok0 + SEG]), "qs")

            @block.tensor
            def _(t, oc=oc):
                kb.wait(t, "dve")
                bank = ps[2 + oc % 2]
                for c in range(8):
                    ins_ = t.matmul(bank[:, 0:512],
                                    lhsT=aw["wo16"][:, c * 1024 + oc * 128: c * 1024 + (oc + 1) * 128],
                                    rhs=ep_ogf[:, c * 512:(c + 1) * 512],
                                    start=(c == 0), stop=(c == 7))
                kb.inc(ins_, "pe")

            @block.vector
            def _(v, oc=oc):
                kb.wait(v, "pe")
                kb.wait(v, "qs")
                sl = ep_x1[:, (oc % 4) * 512:(oc % 4 + 1) * 512]
                kb.inc(v.tensor_tensor(sl, ps[2 + oc % 2][:, 0:512], sl, op=ALU.add), "dve")

            @block.sync
            def _(s, oc=oc):
                kb.wait(s, "dve")
                kb.inc(s.dma_start(out=x2_img[:, oc * T + tok0: oc * T + tok0 + SEG],
                                   in_=ep_x1[:, (oc % 4) * 512:(oc % 4 + 1) * 512]), "qs")

    return epi


def emit_router(kb, block, x_src, sc_out, rw, ps, cst, bufs):
    rx, rsq, rst, rl = bufs["rx"], bufs["rsq"], bufs["rst"], bufs["rl"]
    for ti in range(8):
        @block.sync
        def _(s, ti=ti):
            kb.wait(s, "pe")
            kb.wait(s, "act")
            for c in range(8):
                kb.inc(s.dma_start(out=rx[:, c * 512:(c + 1) * 512],
                                   in_=x_src[:, c * T + ti * 512:c * T + (ti + 1) * 512]),
                       "qs")

        for st in range(4):
            @block.tensor
            def _(t, ti=ti, st=st):
                kb.wait(t, "qs")
                kb.wait(t, "dve")
                for c in range(8):
                    t.matmul(ps[0][:, 0:64],
                             lhsT=rx[:, c * 512 + st * 128:c * 512 + (st + 1) * 128],
                             rhs=rw["wr_sb"][:, c * 64:(c + 1) * 64],
                             start=(c == 0), stop=(c == 7))
                for c in range(8):
                    ins_ = t.matmul(ps[1][:, 0:1],
                                    lhsT=rx[:, c * 512 + st * 128:c * 512 + (st + 1) * 128],
                                    rhs=cst["ones1f"][:, :],
                                    start=(c == 0), stop=(c == 7))
                kb.inc(ins_, "pe")

            @block.scalar
            def _(sc, ti=ti, st=st):
                kb.wait(sc, "pe")
                for c in range(8):
                    kb.inc(sc.activation(
                        out=rsq[:, c * 128:(c + 1) * 128],
                        in_=rx[:, c * 512 + st * 128:c * 512 + (st + 1) * 128],
                        func=AF.Square), "act")

            @block.tensor
            def _(t, st=st):
                kb.wait(t, "act")
                for c in range(8):
                    ins_ = t.matmul(ps[1][:, 1:2],
                                    lhsT=rsq[:, c * 128:(c + 1) * 128],
                                    rhs=cst["ones1f"][:, :],
                                    start=(c == 0), stop=(c == 7))
                kb.inc(ins_, "pe")

            @block.vector
            def _(v, st=st):
                kb.wait(v, "pe")
                kb.inc(v.tensor_copy(rl[:, :], ps[0][:, 0:64]), "dve")
                kb.inc(v.tensor_scalar_mul(rst[:, 0:1], ps[1][:, 0:1], 1.0 / C), "dve")
                kb.inc(v.tensor_scalar_mul(rst[:, 5:6], ps[1][:, 1:2], 1.0 / C), "dve")
                kb.wait(v, "dve")
                kb.inc(v.tensor_tensor(rst[:, 2:3], rst[:, 0:1], rst[:, 0:1],
                                       op=ALU.mult), "dve")
                kb.wait(v, "dve")
                kb.inc(v.tensor_tensor(rst[:, 2:3], rst[:, 5:6], rst[:, 2:3],
                                       op=ALU.subtract), "dve")

            @block.scalar
            def _(sc):
                kb.wait(sc, "dve")
                kb.inc(sc.activation(out=rst[:, 1:2], in_=rst[:, 2:3], func=AF.Ln,
                                     bias=cst["eps1"][:, 0:1]), "act")
                kb.wait(sc, "act")
                kb.inc(sc.activation(out=rst[:, 1:2], in_=rst[:, 1:2], func=AF.Exp,
                                     scale=-0.5), "act")

            @block.vector
            def _(v):
                kb.wait(v, "act")
                kb.inc(v.tensor_scalar_mul(rl[:, :], rl[:, :], rst[:, 1:2]), "dve")
                kb.wait(v, "dve")
                kb.inc(v.scalar_tensor_tensor(out=rst[:, 2:3], in0=rst[:, 0:1],
                                              scalar=-1.0, in1=rst[:, 1:2],
                                              op0=ALU.mult, op1=ALU.mult), "dve")
                kb.wait(v, "dve")
                kb.inc(v.scalar_tensor_tensor(out=rl[:, :], in0=rw["wrcsbc"][:, :],
                                              scalar=rst[:, 2:3], in1=rl[:, :],
                                              op0=ALU.mult, op1=ALU.add), "dve")
                kb.wait(v, "dve")
                kb.inc(v.tensor_tensor(rl[:, :], rl[:, :], rw["brbc"][:, :],
                                       op=ALU.add), "dve")
                kb.wait(v, "dve")
                kb.inc(v.reduce_max(rst[:, 3:4], rl[:, :], axis=AX.X), "dve")
                kb.wait(v, "dve")
                kb.inc(v.tensor_scalar_mul(rst[:, 4:5], rst[:, 3:4], -1.0), "dve")

            @block.scalar
            def _(sc):
                kb.wait(sc, "dve")
                kb.inc(sc.activation(out=rl[:, :], in_=rl[:, :], func=AF.Exp,
                                     bias=rst[:, 4:5]), "act")

            @block.vector
            def _(v):
                kb.wait(v, "act")
                kb.inc(v.reduce_sum(rst[:, 6:7], rl[:, :], axis=AX.X), "dve")
                kb.wait(v, "dve")
                kb.inc(v.reciprocal(rst[:, 6:7], rst[:, 6:7]), "dve")
                kb.wait(v, "dve")
                kb.inc(v.tensor_scalar_mul(rl[:, :], rl[:, :], rst[:, 6:7]), "dve")

            @block.sync
            def _(s, ti=ti, st=st):
                kb.wait(s, "dve")
                kb.inc(s.dma_start(
                    out=sc_out[ti * 512 + st * 128: ti * 512 + (st + 1) * 128, :],
                    in_=rl[:, :]), "qs")


def emit_router_weights(kb, block, Wr_in, br_in, rw, ps, cst):
    @block.sync
    def _(s):
        kb.inc(s.dma_start(out=rw["wr_sb"][:, :].rearrange("p (c e) -> p c e", e=64),
                           in_=Wr_in.rearrange("(c p) e -> p c e", p=128)), "qs")
        kb.inc(s.dma_start(out=rw["brrow"][0:1, :], in_=br_in.unsqueeze(0)), "qs")

    @block.tensor
    def _(t):
        kb.wait(t, "qs")
        for c in range(8):
            ins_ = t.matmul(ps[0][0:1, 0:64], lhsT=cst["ones1f"][:, :],
                            rhs=rw["wr_sb"][:, c * 64:(c + 1) * 64],
                            start=(c == 0), stop=(c == 7))
        kb.inc(ins_, "pe")

    @block.vector
    def _(v):
        kb.wait(v, "pe")
        kb.inc(v.tensor_copy(rw["wcrow"][0:1, :], ps[0][0:1, 0:64]), "dve")

    @block.tensor
    def _(t):
        kb.wait(t, "dve")
        t.matmul(ps[0][:, 0:64], lhsT=cst["onescol"][:, :], rhs=rw["wcrow"][0:1, :],
                 start=True, stop=True)
        kb.inc(t.matmul(ps[1][:, 0:64], lhsT=cst["onescol"][:, :], rhs=rw["brrow"][0:1, :],
                        start=True, stop=True), "pe")

    @block.vector
    def _(v):
        kb.wait(v, "pe")
        v.tensor_copy(rw["wrcsbc"][:, :], ps[0][:, 0:64])
        kb.inc(v.tensor_copy(rw["brbc"][:, :], ps[1][:, 0:64]), "dve")


def build_lA():
    nc = bass.Bass("TRN2", target_bir_lowering=False, debug=False)
    x_img = nc.dram_tensor("x_img", [128, 8 * T], F32, kind="ExternalInput").ap()
    Wr_in = nc.dram_tensor("Wr", [C, E], F32, kind="ExternalInput").ap()
    br_in = nc.dram_tensor("br", [E], F32, kind="ExternalInput").ap()
    sc_out = nc.dram_tensor("sc_own", [T, E], F32, kind="ExternalOutput").ap()
    with ExitStack() as stack:
        kb = KB(nc, stack)
        cst_dram = {
            "ones1f": kb.const("c_ones1f", np.ones((128, 1), np.float32)),
            "onescol": kb.const("c_onescol", np.ones((1, 128), np.float32)),
            "eps1": kb.const("c_eps1", np.full((128, 1), 1e-5, np.float32)),
        }
        cst = {nm: sbuf(kb, nm, list(v.shape)) for nm, v in
               (("ones1f", np.ones((128, 1))), ("onescol", np.ones((1, 128))),
                ("eps1", np.ones((128, 1))))}
        ps = [mkpsum(kb, f"ps{i}") for i in range(2)]
        rw = {
            "wr_sb": sbuf(kb, "wr_sb", [128, 512]),
            "brrow": sbuf(kb, "brrow", [1, 64]),
            "wcrow": sbuf(kb, "wcrow", [1, 64]),
            "brbc": sbuf(kb, "brbc", [128, 64]),
            "wrcsbc": sbuf(kb, "wrcsbc", [128, 64]),
        }
        bufs = {
            "rx": sbuf(kb, "rx", [128, 4096]),
            "rsq": sbuf(kb, "rsq", [128, 1024]),
            "rst": sbuf(kb, "rst", [128, 8]),
            "rl": sbuf(kb, "rl", [128, 64]),
        }
        block = stack.enter_context(nc.Block())

        @block.sync
        def _(s):
            for nm in cst_dram:
                kb.inc(s.dma_start(out=cst[nm][:, :], in_=cst_dram[nm]), "qs")

        emit_router_weights(kb, block, Wr_in, br_in, rw, ps, cst)
        emit_router(kb, block, x_img, sc_out, rw, ps, cst, bufs)

        @block.sync
        def _(s):
            kb.wait_all(s)
    return nc


def build_lB():
    nc = bass.Bass("TRN2", target_bir_lowering=False, debug=False)
    x_img = nc.dram_tensor("x_img", [128, 8 * T], F32, kind="ExternalInput").ap()
    scores_em = nc.dram_tensor("scores_em", [E, NTOK], F32, kind="ExternalInput").ap()
    scores_own = nc.dram_tensor("scores_own", [T, E], F32, kind="ExternalInput").ap()
    w_in = {nm: nc.dram_tensor(nm, shp, F32, kind="ExternalInput").ap() for nm, shp in
            (("Wup", [E, C, HH]), ("Wgate", [E, C, HH]), ("Wdown", [E, HH, C]),
             ("bup", [E, HH]), ("bgate", [E, HH]), ("bdown", [E, C]))}
    Wr_in = nc.dram_tensor("Wr", [C, E], F32, kind="ExternalInput").ap()
    br_in = nc.dram_tensor("br", [E], F32, kind="ExternalInput").ap()
    a_names = ["x_r", "x_w", "x_k", "x_v", "x_a", "k_k", "k_a", "r_k", "W_r", "W_k",
               "W_v", "W_o", "w_lora_a", "w_lora_b", "w_lora_bias", "a_lora_a",
               "a_lora_b", "a_lora_bias", "g_lora_a", "g_lora_b", "g_lora_bias",
               "gn_weight", "gn_bias"]
    a_shapes = {"x_r": [C], "x_w": [C], "x_k": [C], "x_v": [C], "x_a": [C], "k_k": [C],
                "k_a": [C], "r_k": [NH, HD], "W_r": [C, C], "W_k": [C, C], "W_v": [C, C],
                "W_o": [C, C], "w_lora_a": [C, 64], "w_lora_b": [64, C],
                "w_lora_bias": [C], "a_lora_a": [C, 64], "a_lora_b": [64, C],
                "a_lora_bias": [C], "g_lora_a": [C, 128], "g_lora_b": [128, C],
                "g_lora_bias": [C], "gn_weight": [C], "gn_bias": [C]}
    a_in = {nm: nc.dram_tensor(nm, a_shapes[nm], F32, kind="ExternalInput").ap()
            for nm in a_names}
    x2_img = nc.dram_tensor("x2_img", [128, 8 * T], F32, kind="ExternalOutput").ap()
    sc3_own = nc.dram_tensor("sc3_own", [T, E], F32, kind="ExternalOutput").ap()

    x1_img = nc.dram_tensor("d_x1", [128, 8 * T], F32).ap()
    w16 = {
        "Wup": [nc.dram_tensor(f"w16up{e}", [128, 8, HH], BF16).ap() for e in range(E)],
        "Wgate": [nc.dram_tensor(f"w16ga{e}", [128, 8, HH], BF16).ap() for e in range(E)],
        "Wdown": [nc.dram_tensor(f"w16dn{e}", [128, 2, C], BF16).ap() for e in range(E)],
    }
    aw16_dram = {
        "wr16": nc.dram_tensor("d_wr16", [128, 8, 1024], BF16).ap(),
        "wk16": nc.dram_tensor("d_wk16", [128, 8, 1024], BF16).ap(),
        "wv16": nc.dram_tensor("d_wv16", [128, 8, 1024], BF16).ap(),
        "wo16": nc.dram_tensor("d_wo16", [128, 8, 1024], BF16).ap(),
        "wla16": nc.dram_tensor("d_wla16", [128, 8, 64], BF16).ap(),
        "ala16": nc.dram_tensor("d_ala16", [128, 8, 64], BF16).ap(),
        "gla16": nc.dram_tensor("d_gla16", [128, 8, 128], BF16).ap(),
        "wlb16": nc.dram_tensor("d_wlb16", [64, 1024], BF16).ap(),
        "alb16": nc.dram_tensor("d_alb16", [64, 1024], BF16).ap(),
        "glb16": nc.dram_tensor("d_glb16", [128, 1024], BF16).ap(),
        "gbias": nc.dram_tensor("d_gbias", [1, 1024], BF16).ap(),
    }
    adram = {
        "at": nc.dram_tensor("d_at", [128, 8 * T], BF16).ap(),
        "bt": nc.dram_tensor("d_bt", [128, 8 * T], BF16).ap(),
        "kt": nc.dram_tensor("d_kt", [128, 8 * T], BF16).ap(),
        "rt": nc.dram_tensor("d_rt", [128, 8 * T], BF16).ap(),
        "v_d": nc.dram_tensor("d_v", [T, C], BF16).ap(),
        "g_d": nc.dram_tensor("d_g", [T, C], BF16).ap(),
    }
    dram = {
        "theta": nc.dram_tensor("d_theta", [1, 1], F32).ap(),
        "sigma": nc.dram_tensor("d_sigma", [E, 1], F32).ap(),
        "arow": nc.dram_tensor("d_arow", [1, 1024], F32).ap(),
        "brow": nc.dram_tensor("d_brow", [1, 1024], F32).ap(),
        "fs": nc.dram_tensor("d_fs", [E, T], BF16).ap(),
    }

    with ExitStack() as stack:
        kb = KB(nc, stack)
        cstn = {
            "ident": np.eye(128, dtype=np.float32),
            "oh31": (np.arange(128) == 31).astype(np.float32).reshape(128, 1),
            "onescol": np.ones((1, 128), np.float32),
            "eps1": np.full((128, 1), 1e-5, np.float32),
            "ones64": np.ones((128, 64), np.float32),
            "ones1f": np.ones((128, 1), np.float32),
        }
        for m in range(8, 15):
            cstn[f"sgn{m}"] = np.where(((np.arange(128) >> (m - 8)) & 1) == 0,
                                       1.0, -1.0).astype(np.float32).reshape(128, 1)
        BFD = ml_dtypes.bfloat16
        cstn16 = {
            "ident16": np.eye(128, dtype=np.float32).astype(BFD),
            "onescol16": np.ones((1, 128), np.float32).astype(BFD),
            "ones1_16": np.ones((128, 1), np.float32).astype(BFD),
            "ones_row16": np.ones((1, 512), np.float32).astype(BFD),
        }
        tl = np.tril(np.ones((64, 64), np.float32), -1)
        tu_s = np.triu(np.ones((64, 64), np.float32), 1)
        tu_i = np.triu(np.ones((64, 64), np.float32), 0)

        def bd2(m):
            zz = np.zeros((128, 128), np.float32)
            zz[:64, :64] = m
            zz[64:, 64:] = m
            return zz
        cstn16["mask4"] = np.concatenate([bd2(tl), bd2(tu_s), bd2(tu_s), bd2(tu_i)],
                                         1).astype(BFD)
        cstn16["masku_inc"] = bd2(tu_i).astype(BFD)
        for par in range(2):
            pa = np.zeros((128, 128), np.float32)
            pb = np.zeros((128, 128), np.float32)
            qa_ = np.zeros((128, 128), np.float32)
            qb_ = np.zeros((128, 128), np.float32)
            for u in range(64):
                pa[par * 64 + u, u] = 1.0
                pb[par * 64 + u, 64 + u] = 1.0
                qa_[u, par * 64 + u] = 1.0
                qb_[64 + u, par * 64 + u] = 1.0
            cstn16[f"vperm{par}a"] = pa.astype(BFD)
            cstn16[f"vperm{par}b"] = pb.astype(BFD)
            cstn[f"operm{par}a"] = qa_
            cstn[f"operm{par}b"] = qb_
        awn = {}
        for c in range(8):
            m = np.zeros((128, 16), np.float32)
            m[0:64, 2 * c] = 1.0
            m[64:128, 2 * c + 1] = 1.0
            awn[f"bdsel{c}"] = m.astype(BFD)
            h = np.zeros((16, 128), np.float32)
            h[2 * c, 0:64] = 1.0
            h[2 * c + 1, 64:128] = 1.0
            awn[f"hsel{c}"] = h.astype(BFD)

        ATT_ONLY = {"ident16", "mask4", "masku_inc", "vperm0a", "vperm0b",
                    "vperm1a", "vperm1b", "operm0a", "operm0b", "operm1a",
                    "operm1b", "ones64"}
        cst_dram = {k: kb.const("c_" + k, v) for k, v in cstn.items()}
        cst_dram.update({k: kb.const("c_" + k, v) for k, v in cstn16.items()})
        cst_shapes = {k: (list(v.shape), F32) for k, v in cstn.items()}
        cst_shapes.update({k: (list(v.shape), BF16) for k, v in cstn16.items()})
        adr = {k: kb.const("c_" + k, v) for k, v in awn.items()}
        cst = {}
        for k in cst_shapes:
            if k not in ATT_ONLY:
                cst[k] = sbuf(kb, "s_" + k, *cst_shapes[k])
        cst["junk"] = sbuf(kb, "junk", [1, 1])

        ps = [mkpsum(kb, f"ps{i}") for i in range(7)]
        ps16 = mkpsum(kb, "ps16", (128, 1024), BF16)

        aw = {}
        for c in range(8):
            aw[f"bdsel{c}"] = sbuf(kb, f"bdsel{c}", [128, 16], BF16)
            aw[f"hsel{c}"] = sbuf(kb, f"hsel{c}", [16, 128], BF16)
        abufs = {}

        block = stack.enter_context(nc.Block())

        @block.sync
        def _(s):
            for k2 in cst_dram:
                if k2 not in ATT_ONLY:
                    kb.inc(s.dma_start(out=cst[k2][:, :], in_=cst_dram[k2]), "qs")
            for k2 in awn:
                kb.inc(s.dma_start(out=aw[k2][:, :], in_=adr[k2]), "qs")

        @block.gpsimd
        def _(g):
            kb.inc(g.memset(cst["junk"][:, :], 0.0), "pool")

        with ExitStack() as sf:
            def sbf(name, shape, dtype=F32):
                return sf.enter_context(nc.sbuf_tensor(name, shape, dtype))
            sig_fm = sbf("sig_fm", [E, 1])
            thbc = sbf("thbc", [128, 1])
            sigbc = sbf("sigbc", [128, E])
            fsfm = sbf("fsfm", [64, T], BF16)
            bias16 = {
                "bup": sbf("bupS", [128, 128]),
                "bgate": sbf("bgaS", [128, 128]),
                "bupR": sbf("bupR", [128, 128]),
                "bgaR": sbf("bgaR", [128, 128]),
            }
            bdown_sb = sbf("bdownS", [64, C], BF16)

            emit_weight_cast(kb, block, w_in, w16)

            @block.gpsimd
            def _(g):
                kb.inc(g.dma_start(out=bdown_sb[:, :], in_=w_in["bdown"]), "qp")

            @block.sync
            def _(s):
                kb.inc(s.dma_start(out=bias16["bupR"][:, :],
                                   in_=w_in["bup"].rearrange("e (r p) -> (e r) p", p=128)), "qs")
                kb.inc(s.dma_start(out=bias16["bgaR"][:, :],
                                   in_=w_in["bgate"].rearrange("e (r p) -> (e r) p", p=128)), "qs")

            @block.tensor
            def _(t):
                kb.wait(t, "qs")
                kb.wait(t, "dve")
                t.transpose(ps[0][:, 0:128], in_=bias16["bupR"][:, :],
                            identity=cst["ident"][:, :])
                kb.inc(t.transpose(ps[1][:, 0:128], in_=bias16["bgaR"][:, :],
                                   identity=cst["ident"][:, :]), "pe")

            @block.vector
            def _(v):
                kb.wait(v, "pe")
                v.tensor_copy(bias16["bup"][:, :], ps[0][:, 0:128])
                kb.inc(v.tensor_copy(bias16["bgate"][:, :], ps[1][:, 0:128]), "dve")

            with ExitStack() as sp:
                def sb2(name, shape, dtype=F32):
                    return sp.enter_context(nc.sbuf_tensor(name, shape, dtype))
                bufA = sb2("bufA", [128, E * 256])
                bufB = sb2("bufB", [128, E * 256])
                tbufA = sb2("tbufA", [128, 8 * 128])
                tbufB = sb2("tbufB", [128, 8 * 128])
                gs = sb2("gs", [128, E * 4])
                bm = sb2("bm", [128, E * 4])
                thA = sb2("thA", [128, 256])
                thB = sb2("thB", [128, 256])
                scr = {
                    "th11": sb2("th11", [1, 1]),
                    "mask": sb2("mask", [128, E * 4], mybir.dt.uint32),
                    "cand": sb2("cand", [128, E * 4]),
                    "emin": sb2("emin", [128, E]),
                    "eminT": sb2("eminT", [64, 128]),
                    "bigpos": sb2("bigpos", [128, E * 4]),
                    "sigrow": sb2("sigrow", [1, 64]),
                }
                fsb = {
                    "stile": sb2("stile", [128, 64]),
                    "fs0": sb2("fs0", [128, 64]),
                    "fsmask": sb2("fsmask", [128, 64]),
                    "fsden": sb2("fsden", [128, 1]),
                    "fsrb": sb2("fsrb", [64, T], BF16),
                }

                @block.sync
                def _(s):
                    kb.wait(s, "qs")
                    for gidx in range(8):
                        src = scores_em[gidx * 8:(gidx + 1) * 8, :].rearrange(
                            "e (p f) -> p e f", p=128)
                        kb.inc(s.dma_start(out=bufA[:, :].rearrange(
                            "p (e f) -> p e f", f=256)[:, gidx * 8:(gidx + 1) * 8, :],
                            in_=src), "qs")

                @block.vector
                def _(v):
                    kb.wait(v, "qs")
                    kb.wait(v, "dve")
                    kb.inc(v.memset(scr["bigpos"][:, :], BIGPOS), "dve")

                emit_sort(kb, block, bufA, bufB, tbufA, tbufB, ps[2], cst, nbat=E)
                emit_gs_bm(kb, block, bufA, gs, bm, nbat=E)
                emit_theta_sigma(kb, block, gs, bm, thA, thB, tbufA, tbufB, ps[2], cst,
                                 sig_fm, thbc, sigbc, scr, dram)
                emit_fs(kb, block, scores_own, sigbc, fsfm, dram["fs"], ps, cst, fsb)

                rbw = sb2("rbw", [128, 2048], BF16)

                @block.sync
                def _(s):
                    kb.wait(s, "qp")
                    for e in range(E):
                        for nm in ("Wup", "Wgate", "Wdown"):
                            kb.inc(s.dma_start(out=rbw[:, :],
                                               in_=w16[nm][e].rearrange("p c h -> p (c h)")), "qs")

                for engname in ("vector", "gpsimd", "scalar", "tensor", "sync"):
                    def mk(e, _n=engname):
                        kb.wait_all(e)
                    getattr(block, engname)(mk)

            with ExitStack() as sw:
                def sb3(name, shape, dtype=F32):
                    return sw.enter_context(nc.sbuf_tensor(name, shape, dtype))
                fbufs = {
                    "xq": sb3("xq", [128, 8 * 1024]),
                    "yq": sb3("yq", [128, 8 * 1024], BF16),
                    "hgrp": sb3("hgrp", [128, 16 * 1024], BF16),
                    "dacc": sb3("dacc", [128, 16 * 512]),
                    "silu": sb3("silu", [128, 2 * 512], BF16),
                    "fsbc": sb3("fsbc", [128, 2 * 1024], BF16),
                    "swu": sb3("swu", [128, 2 * 2048], BF16),
                    "swg": sb3("swg", [128, 2 * 2048], BF16),
                    "swd": sb3("swd", [128, 8 * 2048], BF16),
                    "arow": sb3("f_arow", [1, 1024]),
                    "brow": sb3("f_brow", [1, 1024]),
                    "abc": sb3("f_abc", [128, 1024]),
                    "bbc": sb3("f_bbc", [128, 1024]),
                    "xsq": sb3("xsq", [128, 512], BF16),
                    "xbf": sb3("xbf", [128, 512], BF16),
                    "srow": sb3("srow", [1, 1024]),
                }
                emit_ffn_sweep(kb, block, x_img, x1_img, dram["fs"], w16, bias16,
                               bdown_sb, fsfm, ps, cst, fbufs, dram)
                for engname in ("vector", "gpsimd", "scalar", "tensor", "sync"):
                    def mk(e, _n=engname):
                        kb.wait_all(e)
                    getattr(block, engname)(mk)

        with ExitStack() as sa:
            def sb4(name, shape, dtype=F32):
                return sa.enter_context(nc.sbuf_tensor(name, shape, dtype))
            aw["wr16"] = sb4("wr16", [128, 8192], BF16)
            aw["wk16"] = sb4("wk16", [128, 8192], BF16)
            aw["wv16"] = sb4("wv16", [128, 8192], BF16)
            aw["wla16"] = sb4("wla16", [128, 512], BF16)
            aw["ala16"] = sb4("ala16", [128, 512], BF16)
            aw["gla16"] = sb4("gla16", [128, 1024], BF16)
            aw["wlb16"] = sb4("wlb16", [64, 1024], BF16)
            aw["alb16"] = sb4("alb16", [64, 1024], BF16)
            aw["glb16"] = sb4("glb16", [128, 1024], BF16)
            aw["gbias_row16"] = sb4("gbias_row16", [1, 1024], BF16)
            aw["vecs"] = sb4("vecs", [128, 96])
            aw["vstage"] = sb4("vstage", [128, 128])
            aw["gnwrow"] = sb4("gnwrow", [1, 1024])
            aw["gnbrow"] = sb4("gnbrow", [1, 1024])
            abufs["cexp_sb"] = sb4("cexp_sb", [128, 512])
            abufs["bonus"] = sb4("bonus", [16, T], BF16)
            for k2 in ATT_ONLY:
                cst[k2] = sb4("s_" + k2, *cst_shapes[k2])

            @block.sync
            def _(s):
                kb.wait(s, "dve")
                for k2 in ATT_ONLY:
                    kb.inc(s.dma_start(out=cst[k2][:, :], in_=cst_dram[k2]), "qs")

            emit_attn_setup(kb, block, nc, a_in, aw, aw16_dram, ps, cst, None)

            with ExitStack() as s1:
                def sba(name, shape, dtype=F32):
                    return s1.enter_context(nc.sbuf_tensor(name, shape, dtype))
                for nm in ("fA", "fB", "fC"):
                    abufs[nm] = sba(nm, [128, 4096])
                abufs["fE"] = sba("fE", [128, 2048])
                for i in range(6):
                    abufs[f"b{i}"] = sba(f"b{i}", [128, 4096], BF16)
                abufs["carry"] = sba("carry", [128, 8])
                abufs["cval"] = sba("cval", [128, 64])
                abufs["arow"] = sba("p_arow", [1, 1024])
                abufs["brow"] = sba("p_brow", [1, 1024])
                abufs["abc"] = sba("p_abc", [128, 512])
                abufs["bbc"] = sba("p_bbc", [128, 512])
                abufs["z64"] = abufs["abc"][0:64, :].bitcast(BF16)[:, 0:512]
                emit_attn_prep(kb, block, x1_img, aw, ps, ps16, cst, abufs, adram)
                for engname in ("vector", "gpsimd", "scalar", "tensor", "sync"):
                    def mk(e, _n=engname):
                        kb.wait_all(e)
                    getattr(block, engname)(mk)

            aw["wo16"] = sb4("wo16", [128, 8192], BF16)
            aw["gnwbc"] = sb4("gnwbc", [128, 1024])
            aw["gnbbc"] = sb4("gnbbc", [128, 1024])

            @block.sync
            def _(s):
                kb.inc(s.dma_start(out=aw["wo16"][:, :],
                                   in_=aw16_dram["wo16"].rearrange("p a b -> p (a b)")), "qs")
            emit_gnbc(kb, block, aw, ps, cst)

            sb2d = {
                "a_s": sb4("a_s", [128, 8 * 512], BF16),
                "b_s": sb4("b_s", [128, 8 * 512], BF16),
                "k_s": sb4("k_s", [128, 8 * 512], BF16),
                "r_s": sb4("r_s", [128, 8 * 512], BF16),
                "v_s": sb4("v_s", [128, 4 * C], BF16),
                "cexp": None,
                "yt": sb4("yt", [128, 8 * 128]),
                "y0p": sb4("y0p", [128, 128], BF16),
                "bd": sb4("bd", [128, 5 * 128], BF16),
                "scmm": sb4("scmm", [128, 512], BF16),
                "lad": sb4("lad", [128, 10 * 128], BF16),
                "mrkt": sb4("mrkt", [128, 128], BF16),
                "mkav": sb4("mkav", [128, 128]),
                "btm": sb4("btm", [128, 256], BF16),
                "xs": sb4("xs", [128, 128], BF16),
                "dlt": sb4("dlt", [128, 128]),
                "o_tm": sb4("o_tm", [128, 4 * C]),
                "osb": sb4("osb", [128, 256]),
                "ps16": ps16,
                "bonus": abufs["bonus"],
                "ep_g": sb4("ep_g", [128, 1024], BF16),
                "ep_st": sb4("ep_st", [128, 16]),
                "ep_st2": sb4("ep_st2", [128, 16]),
                "ep_sq": sb4("ep_sq", [128, 1024]),
                "ep_bt": sb4("ep_bt", [128, 64]),
                "ep_ogf": sb4("ep_ogf", [128, 4096], BF16),
                "ep_x1": sb4("ep_x1", [128, 2048]),
            }
            epi = make_attn_epilogue(kb, block, aw, ps, ps16, cst, sb2d, adram,
                                     x1_img, x2_img)
            emit_scan(kb, block, adram["at"], adram["bt"], adram["kt"], adram["rt"],
                      adram["v_d"], None, None, ps, cst, sb2d,
                      epilogue_fn=epi, cexp_sb=abufs["cexp_sb"])
            for engname in ("vector", "gpsimd", "scalar", "tensor", "sync"):
                def mk(e, _n=engname):
                    kb.wait_all(e)
                getattr(block, engname)(mk)

        rw = {
            "wr_sb": sbuf(kb, "wr_sb", [128, 512]),
            "brrow": sbuf(kb, "brrow", [1, 64]),
            "wcrow": sbuf(kb, "wcrow", [1, 64]),
            "brbc": sbuf(kb, "brbc", [128, 64]),
            "wrcsbc": sbuf(kb, "wrcsbc", [128, 64]),
        }
        emit_router_weights(kb, block, Wr_in, br_in, rw, ps, cst)
        rbufs = {
            "rx": sbuf(kb, "rx", [128, 4096]),
            "rsq": sbuf(kb, "rsq", [128, 1024]),
            "rst": sbuf(kb, "rst", [128, 8]),
            "rl": sbuf(kb, "rl", [128, 64]),
        }
        emit_router(kb, block, x2_img, sc3_own, rw, ps, cst, rbufs)

        @block.sync
        def _(s):
            kb.wait_all(s)
    return nc


A_NAMES = ["x_r", "x_w", "x_k", "x_v", "x_a", "k_k", "k_a", "r_k", "W_r", "W_k",
           "W_v", "W_o", "w_lora_a", "w_lora_b", "w_lora_bias", "a_lora_a",
           "a_lora_b", "a_lora_bias", "g_lora_a", "g_lora_b", "g_lora_bias",
           "gn_weight", "gn_bias"]


def _fm_img(xb):
    return np.ascontiguousarray(
        xb.T.reshape(8, 128, T).transpose(1, 0, 2).reshape(128, 8 * T))


def _from_img(img):
    return np.ascontiguousarray(
        np.asarray(img).reshape(128, 8, T).transpose(1, 0, 2).reshape(C, T).T)


def kernel(**inputs):
    from concourse.bass_utils import run_bass_kernel_spmd
    x = np.ascontiguousarray(np.asarray(inputs["L0_attn_hidden_states"], np.float32))
    imgs = [_fm_img(np.ascontiguousarray(x[b])) for b in range(B)]
    Wr = np.ascontiguousarray(np.asarray(inputs["L0_ffn_Wr"], np.float32))
    br = np.ascontiguousarray(np.asarray(inputs["L0_ffn_br"], np.float32))
    ffn = {nm: np.ascontiguousarray(np.asarray(inputs["L0_ffn_" + nm], np.float32))
           for nm in ("Wup", "Wgate", "Wdown", "bup", "bgate", "bdown")}
    attn = {nm: np.ascontiguousarray(np.asarray(inputs["L0_attn_" + nm], np.float32))
            for nm in A_NAMES}
    cores = list(range(B))

    ncA = build_lA()
    resA = run_bass_kernel_spmd(
        ncA, [{"x_img": imgs[b], "Wr": Wr, "br": br} for b in range(B)],
        core_ids=cores)
    sc1 = [np.ascontiguousarray(np.asarray(resA.results[b]["sc_own"], np.float32))
           for b in range(B)]
    scores_em = np.ascontiguousarray(np.concatenate(sc1, axis=0).T)

    ncB = build_lB()
    insB = []
    for b in range(B):
        d = {"x_img": imgs[b], "scores_em": scores_em, "scores_own": sc1[b],
             "Wr": Wr, "br": br}
        d.update(ffn)
        d.update(attn)
        insB.append(d)
    resB = run_bass_kernel_spmd(ncB, insB, core_ids=cores)
    x2 = [np.ascontiguousarray(np.asarray(resB.results[b]["x2_img"], np.float32))
          for b in range(B)]
    sc3 = [np.ascontiguousarray(np.asarray(resB.results[b]["sc3_own"], np.float32))
           for b in range(B)]
    scores_em3 = np.ascontiguousarray(np.concatenate(sc3, axis=0).T)

    ncC = build_l2()
    insC = []
    for b in range(B):
        d = {"x_img": x2[b], "scores_em": scores_em3, "scores_own": sc3[b]}
        d.update(ffn)
        insC.append(d)
    resC = run_bass_kernel_spmd(ncC, insC, core_ids=cores)
    out = np.empty((B, T, C), np.float32)
    for b in range(B):
        out[b] = _from_img(np.asarray(resC.results[b]["x3_img"], np.float32))
    return out


# version 02
# speedup vs baseline: 1.0051x; 1.0051x over previous
import numpy as np
import ml_dtypes
import concourse.bass as bass
import concourse.mybir as mybir
mybir = mybir
from contextlib import ExitStack

F32 = mybir.dt.float32
BF16 = mybir.dt.bfloat16
I32 = mybir.dt.int32
AF = mybir.ActivationFunctionType
ALU = mybir.AluOpType
AX = mybir.AxisListType

B, T, C = 8, 4096, 1024
NH, HD = 16, 64
E, HH = 64, 256
Gsz, KTOP = 64, 16
NTOK = B * T
TQ = NTOK // Gsz
BUDGET = TQ * KTOP
SCF = NTOK // 128
BIGPOS = 1e30


class KB:
    def __init__(self, nc, stack):
        self.nc = nc
        self.stack = stack
        self.sems = {}
        self.cnt = {}
        for name in ("pe", "dve", "act", "pool", "qs", "qa", "qp"):
            self.sems[name] = stack.enter_context(nc.semaphore("sem_" + name))
            self.cnt[name] = 0
        self.consts = {}

    def inc(self, instr, eng):
        n = 16 if eng in ("qs", "qa", "qp") else 1
        instr.then_inc(self.sems[eng], n)
        self.cnt[eng] += 1
        return self.cnt[eng]

    def val(self, eng, count=None):
        if count is None:
            count = self.cnt[eng]
        return count * (16 if eng in ("qs", "qa", "qp") else 1)

    def wait(self, e, eng, count=None):
        e.wait_ge(self.sems[eng], self.val(eng, count))

    def wait_all(self, e, engs=("pe", "dve", "act", "pool", "qs", "qa", "qp")):
        for name in engs:
            if self.cnt[name] > 0:
                self.wait(e, name)

    def const(self, name, arr):
        h = self.nc.inline_tensor(np.ascontiguousarray(arr), name=name)
        self.consts[name] = h
        return h.ap()


def sbuf(kb, name, shape, dtype=F32):
    return kb.stack.enter_context(kb.nc.sbuf_tensor(name, shape, dtype))


def mkpsum(kb, name, shape=(128, 512), dtype=F32):
    return kb.stack.enter_context(kb.nc.psum_tensor(name, list(shape), dtype))


def emit_sort(kb, block, bufA, bufB, tbufA, tbufB, ps_tr, cst, nbat):
    nc = kb.nc
    W = nbat * 256
    state = {"cur": bufA, "alt": bufB}

    def wait_cde(e, minus_dve=0):
        kb.wait(e, "dve", kb.cnt["dve"] - minus_dve)
        kb.wait(e, "pool")
        kb.wait(e, "act")

    def stage_fland(m, j):
        cur, alt = state["cur"], state["alt"]
        d = 1 << j

        def views(buf, dirbit=None):
            if dirbit is None:
                vv = buf[:, :].rearrange("p (q two d) -> p q two d", two=2, d=d)
                return vv[:, :, 0, :], vv[:, :, 1, :]
            dm = 1 << m
            vv = buf[:, :].rearrange(
                "p (q dirb mid two d) -> p q dirb mid two d",
                dirb=2, two=2, d=d, mid=dm // (2 * d))
            return vv[:, :, dirbit, :, 0, :], vv[:, :, dirbit, :, 1, :]

        @block.vector
        def _(v):
            wait_cde(v)
            if m <= 7:
                loS, hiS = views(cur, 0)
                loD, hiD = views(alt, 0)
                v.tensor_tensor(loD, loS, hiS, op=ALU.max)
                v.tensor_tensor(hiD, loS, hiS, op=ALU.min)
                loS2, hiS2 = views(cur, 1)
                loD2, hiD2 = views(alt, 1)
                v.tensor_tensor(loD2, loS2, hiS2, op=ALU.min)
                kb.inc(v.tensor_tensor(hiD2, loS2, hiS2, op=ALU.max), "dve")
            else:
                loS, hiS = views(cur)
                loD, hiD = views(alt)
                v.tensor_tensor(loD, loS, hiS, op=ALU.max)
                kb.inc(v.tensor_tensor(hiD, loS, hiS, op=ALU.min), "dve")

        state["cur"], state["alt"] = alt, cur

    def negate(m):
        cur = state["cur"]
        half = (W // 2 // 512) * 512 or W // 2

        @block.vector
        def _(v):
            wait_cde(v)
            kb.inc(v.tensor_scalar_mul(cur[:, 0:half], cur[:, 0:half],
                                       cst[f"sgn{m}"][:, :]), "dve")

        @block.gpsimd
        def _(g):
            wait_cde(g, minus_dve=1)
            kb.inc(g.tensor_scalar_mul(cur[:, half:W], cur[:, half:W],
                                       cst[f"sgn{m}"][:, :]), "pool")

    def tland(m, jlist):
        cur = state["cur"]
        ntile = nbat * 2
        GT = 8

        for g0 in range(0, ntile, GT):
            ngt = min(g0 + GT, ntile) - g0
            for h0 in range(0, ngt, 4):
                nh = min(h0 + 4, ngt) - h0

                @block.tensor
                def _(t, g0=g0, h0=h0, nh=nh):
                    kb.wait(t, "dve")
                    kb.wait(t, "pool")
                    kb.wait(t, "act")
                    for k in range(nh):
                        kb.inc(t.transpose(
                            ps_tr[:, k * 128:(k + 1) * 128],
                            in_=cur[:, (g0 + h0 + k) * 128:(g0 + h0 + k + 1) * 128],
                            identity=cst["ident"][:, :]), "pe")

                @block.scalar
                def _(sc, h0=h0, nh=nh):
                    kb.wait(sc, "pe")
                    kb.inc(sc.copy(tbufA[:, h0 * 128:(h0 + nh) * 128],
                                   ps_tr[:, 0:nh * 128]), "act")

            tst = {"cur": tbufA, "alt": tbufB}
            for j in jlist:
                dp = 1 << (j - 8)
                dm = 1 << (m - 8)
                tcur, talt = tst["cur"], tst["alt"]

                def tviews(buf, dirbit, ngt=ngt, dp=dp, dm=dm):
                    if dm >= 128:
                        if dirbit == 1:
                            return None
                        vv = buf[:, 0:ngt * 128].rearrange(
                            "q (tile hi two d) -> q tile hi two d", two=2, d=dp,
                            tile=ngt, hi=128 // (2 * dp))
                        return vv[:, :, :, 0, :], vv[:, :, :, 1, :]
                    vv = buf[:, 0:ngt * 128].rearrange(
                        "q (tile hi dirb mid two d) -> q tile hi dirb mid two d",
                        tile=ngt, dirb=2, two=2, d=dp,
                        hi=128 // (2 * dm), mid=max(dm // (2 * dp), 1))
                    sel_lo = vv[:, :, :, dirbit, :, 0, :]
                    sel_hi = vv[:, :, :, dirbit, :, 1, :]
                    return sel_lo, sel_hi

                @block.vector
                def _(v, tcur=tcur, talt=talt, tviews=tviews):
                    wait_cde(v)
                    r0 = tviews(tcur, 0)
                    d0 = tviews(talt, 0)
                    v.tensor_tensor(d0[0], r0[0], r0[1], op=ALU.max)
                    v.tensor_tensor(d0[1], r0[0], r0[1], op=ALU.min)
                    r1 = tviews(tcur, 1)
                    if r1 is not None:
                        d1 = tviews(talt, 1)
                        v.tensor_tensor(d1[0], r1[0], r1[1], op=ALU.min)
                        kb.inc(v.tensor_tensor(d1[1], r1[0], r1[1], op=ALU.max), "dve")
                    else:
                        kb.inc(v.memset(cst["junk"][:, :], 0.0), "dve")

                tst["cur"], tst["alt"] = talt, tcur

            tc = tst["cur"]
            for h0 in range(0, ngt, 4):
                nh = min(h0 + 4, ngt) - h0

                @block.tensor
                def _(t, g0=g0, h0=h0, nh=nh, tc=tc):
                    kb.wait(t, "dve")
                    kb.wait(t, "pool")
                    kb.wait(t, "act")
                    for k in range(nh):
                        kb.inc(t.transpose(
                            ps_tr[:, k * 128:(k + 1) * 128],
                            in_=tc[:, (h0 + k) * 128:(h0 + k + 1) * 128],
                            identity=cst["ident"][:, :]), "pe")

                @block.scalar
                def _(sc, g0=g0, h0=h0, nh=nh):
                    kb.wait(sc, "pe")
                    kb.inc(sc.copy(cur[:, (g0 + h0) * 128:(g0 + h0 + nh) * 128],
                                   ps_tr[:, 0:nh * 128]), "act")

    for m in range(1, 16):
        tj = list(range(m - 1, 7, -1))
        if tj:
            tland(m, tj)
        fj = list(range(min(m - 1, 7), -1, -1))
        if m >= 8:
            skip_neg = (m - 8) >= 7
            if not skip_neg:
                negate(m)
            for j in fj:
                stage_fland(m, j)
            if not skip_neg:
                negate(m)
        else:
            for j in fj:
                stage_fland(m, j)

    if state["cur"] is not bufA:
        @block.vector
        def _(v):
            wait_cde(v)
            kb.inc(v.tensor_copy(bufA[:, 0:W // 2], bufB[:, 0:W // 2]), "dve")

        @block.gpsimd
        def _(g):
            wait_cde(g, minus_dve=1)
            kb.inc(g.tensor_copy(bufA[:, W // 2:W], bufB[:, W // 2:W]), "pool")


def emit_gs_bm(kb, block, buf, gs_out, bm_out, nbat):
    @block.vector
    def _(v):
        kb.wait_all(v, ("dve", "pool", "act"))
        vv = buf[:, :].rearrange("p (e q s) -> p e q s", q=4, s=64)
        kb.inc(v.reduce_sum(gs_out[:, :].rearrange("p (e q) -> p e q", q=4),
                            vv, axis=AX.X), "dve")

    @block.vector
    def _(v):
        kb.wait_all(v, ("pool", "act"))
        vv = buf[:, :].rearrange("p (e q s) -> p e q s", q=4, s=64)
        kb.inc(v.tensor_copy(bm_out[:, :].rearrange("p (e q) -> p e q", q=4),
                             vv[:, :, :, 63:64].squeeze(3)), "dve")


def emit_theta_sigma(kb, block, gs, bm, thA, thB, tbufA, tbufB, ps_tr, cst,
                     sig_fm, thbc, sigbc, scr, dram_scratch):
    nc = kb.nc

    @block.vector
    def _(v):
        kb.wait_all(v, ("dve", "pool", "act"))
        kb.inc(v.tensor_copy(thA[:, :], gs[:, :]), "dve")

    emit_sort(kb, block, thA, thB, tbufA, tbufB, ps_tr, cst, nbat=1)

    @block.tensor
    def _(t):
        kb.wait_all(t, ("dve", "pool", "act"))
        kb.inc(t.matmul(ps_tr[0:1, 0:1], lhsT=cst["oh31"][:, :],
                        rhs=thA[:, 255:256], start=True, stop=True), "pe")

    @block.scalar
    def _(sc):
        kb.wait(sc, "pe")
        kb.inc(sc.copy(scr["th11"][:, :], ps_tr[0:1, 0:1]), "act")

    @block.tensor
    def _(t):
        kb.wait(t, "act")
        kb.inc(t.matmul(ps_tr[:, 128:129], lhsT=cst["onescol"][:, :],
                        rhs=scr["th11"][:, :], start=True, stop=True), "pe")

    @block.vector
    def _(v):
        kb.wait(v, "pe")
        kb.inc(v.tensor_copy(thbc[:, :], ps_tr[:, 128:129]), "dve")

    @block.vector
    def _(v):
        kb.wait(v, "dve")
        kb.wait_all(v, ("pool", "act"))
        kb.inc(v.tensor_single_scalar(thA[:, :], gs[:, :], thbc[:, 0:1], op=ALU.is_ge), "dve")
        kb.wait(v, "dve")
        kb.inc(v.tensor_tensor(scr["cand"][:, :], thA[:, :], bm[:, :], op=ALU.mult), "dve")
        kb.wait(v, "dve")
        kb.inc(v.tensor_scalar(thB[:, :], thA[:, :], -BIGPOS, BIGPOS,
                               op0=ALU.mult, op1=ALU.add), "dve")
        kb.wait(v, "dve")
        kb.inc(v.tensor_tensor(scr["cand"][:, :], scr["cand"][:, :], thB[:, :],
                               op=ALU.add), "dve")
        kb.wait(v, "dve")
        kb.inc(v.tensor_reduce(scr["emin"][:, :],
                               scr["cand"][:, :].rearrange("p (e q) -> p e q", q=4),
                               axis=AX.X, op=ALU.min), "dve")

    @block.tensor
    def _(t):
        kb.wait(t, "dve")
        kb.inc(t.transpose(ps_tr[0:64, 0:128], in_=scr["emin"][:, :],
                           identity=cst["ident"][:, :]), "pe")

    @block.vector
    def _(v):
        kb.wait(v, "pe")
        kb.inc(v.tensor_copy(scr["eminT"][:, :], ps_tr[0:64, 0:128]), "dve")
        kb.wait(v, "dve")
        kb.inc(v.tensor_reduce(sig_fm[:, :], scr["eminT"][:, :],
                               axis=AX.X, op=ALU.min), "dve")

    @block.tensor
    def _(t):
        kb.wait(t, "dve")
        kb.inc(t.transpose(ps_tr[0:1, 256:320], in_=sig_fm[:, :],
                           identity=cst["ident"][0:64, 0:64]), "pe")

    @block.vector
    def _(v):
        kb.wait(v, "pe")
        kb.inc(v.tensor_copy(scr["sigrow"][:, :], ps_tr[0:1, 256:320]), "dve")

    @block.tensor
    def _(t):
        kb.wait(t, "dve")
        kb.inc(t.matmul(ps_tr[:, 320:384], lhsT=cst["onescol"][:, :],
                        rhs=scr["sigrow"][:, :], start=True, stop=True), "pe")

    @block.vector
    def _(v):
        kb.wait(v, "pe")
        kb.inc(v.tensor_copy(sigbc[:, :], ps_tr[:, 320:384]), "dve")


def emit_weight_cast(kb, block, w_in, w16):
    @block.gpsimd
    def _(g):
        for e in range(E):
            for nm in ("Wup", "Wgate"):
                src = w_in[nm][e].rearrange("(c p) h -> p c h", p=128)
                kb.inc(g.dma_start(out=w16[nm][e], in_=src), "qp")
            src = w_in["Wdown"][e].rearrange("(c p) d -> p c d", p=128)
            kb.inc(g.dma_start(out=w16["Wdown"][e], in_=src), "qp")


def emit_ffn_sweep(kb, block, x_img_in, x_img_out, fs_dram, w16, bias16,
                   bdown_sb, fsfm, ps, cst, bufs, dram, dbg=None):
    TQL = 1024
    NQ = T // TQL
    NTC = TQL // 512
    xq, yq, hgrp = bufs["xq"], bufs["yq"], bufs["hgrp"]
    silu_sb, fsbc = bufs["silu"], bufs["fsbc"]
    swu, swg, swd = bufs["swu"], bufs["swg"], bufs["swd"]
    arow, brow, abc, bbc = bufs["arow"], bufs["brow"], bufs["abc"], bufs["bbc"]
    xsq, xbf, srow = bufs["xsq"], bufs["xbf"], bufs["srow"]

    for q in range(NQ):
        @block.sync
        def _(s, q=q):
            kb.wait_all(s, ("dve", "qs"))
            for c in range(8):
                kb.inc(s.dma_start(out=xq[:, c * TQL:(c + 1) * TQL],
                                   in_=x_img_in[:, c * T + q * TQL: c * T + (q + 1) * TQL]), "qs")

        for tc in range(NTC):
            sl = slice(tc * 512, (tc + 1) * 512)
            for c in range(8):
                @block.scalar
                def _(sc, tc=tc, c=c, sl=sl):
                    if c == 0:
                        kb.wait(sc, "qs")
                    kb.wait(sc, "pe")
                    kb.inc(sc.activation(out=xsq[:, :], in_=xq[:, c * TQL:][:, sl],
                                         func=AF.Square), "act")

                @block.gpsimd
                def _(g, tc=tc, c=c, sl=sl):
                    if c == 0:
                        kb.wait(g, "qs")
                    kb.wait(g, "pe")
                    kb.inc(g.tensor_copy(xbf[:, :], xq[:, c * TQL:][:, sl]), "pool")

                @block.tensor
                def _(t, c=c):
                    kb.wait(t, "act")
                    kb.wait(t, "pool")
                    kb.wait(t, "dve")
                    t.matmul(ps[2][0:1, :], lhsT=cst["ones1_16"][:, :], rhs=xbf[:, :],
                             start=(c == 0), stop=(c == 7))
                    kb.inc(t.matmul(ps[3][0:1, :], lhsT=cst["ones1_16"][:, :],
                                    rhs=xsq[:, :], start=(c == 0), stop=(c == 7)), "pe")

            @block.vector
            def _(v, sl=sl):
                kb.wait(v, "pe")
                v.tensor_scalar_mul(srow[0:1, 0:512], ps[2][0:1, :], 1.0 / C)
                kb.inc(v.tensor_scalar_mul(srow[0:1, 512:1024], ps[3][0:1, :], 1.0 / C), "dve")
                kb.wait(v, "dve")
                kb.inc(v.tensor_tensor(abc[0:1, sl], srow[0:1, 0:512], srow[0:1, 0:512],
                                       op=ALU.mult), "dve")
                kb.wait(v, "dve")
                kb.inc(v.tensor_tensor(abc[0:1, sl], srow[0:1, 512:1024], abc[0:1, sl],
                                       op=ALU.subtract), "dve")

            @block.scalar
            def _(sc, sl=sl):
                kb.wait(sc, "dve")
                kb.inc(sc.activation(out=bbc[0:1, sl], in_=abc[0:1, sl], func=AF.Ln,
                              bias=cst["eps1"][0:1, 0:1]), "act")
                kb.wait(sc, "act")
                kb.inc(sc.activation(out=arow[0:1, sl], in_=bbc[0:1, sl], func=AF.Exp,
                                     scale=-0.5), "act")

            @block.vector
            def _(v, sl=sl):
                kb.wait(v, "act")
                kb.wait(v, "dve")
                kb.inc(v.tensor_tensor(brow[0:1, sl], srow[0:1, 0:512], arow[0:1, sl],
                                       op=ALU.mult), "dve")

        for half in range(2):
            @block.tensor
            def _(t, half=half):
                kb.wait(t, "dve")
                kb.wait(t, "act")
                t.matmul(ps[2][:, :], lhsT=cst["onescol"][:, :],
                         rhs=arow[0:1, half * 512:(half + 1) * 512], start=True, stop=True)
                kb.inc(t.matmul(ps[3][:, :], lhsT=cst["onescol"][:, :],
                                rhs=brow[0:1, half * 512:(half + 1) * 512],
                                start=True, stop=True), "pe")

            @block.vector
            def _(v, half=half):
                kb.wait(v, "pe")
                v.tensor_copy(abc[:, half * 512:(half + 1) * 512], ps[2][:, :])
                kb.inc(v.tensor_copy(bbc[:, half * 512:(half + 1) * 512], ps[3][:, :]), "dve")

        for c in range(8):
            @block.vector
            def _(v, c=c):
                kb.wait(v, "pe")
                kb.inc(v.tensor_tensor(yq[:, c * TQL:(c + 1) * TQL], xq[:, c * TQL:(c + 1) * TQL],
                                       abc[:, :], op=ALU.mult), "dve")
                kb.wait(v, "dve")
                kb.inc(v.tensor_tensor(yq[:, c * TQL:(c + 1) * TQL],
                                       yq[:, c * TQL:(c + 1) * TQL], bbc[:, :],
                                       op=ALU.subtract), "dve")

        if dbg and q == 0:
            @block.sync
            def _(s):
                kb.wait(s, "dve")
                kb.inc(s.dma_start(out=dbg["y0"], in_=yq[:, :]), "qs")

        for eg in range(8):
            @block.scalar
            def _(sc, q=q, eg=eg):
                if q + eg > 0:
                    kb.wait(sc, "dve")
                for ei in range(8):
                    kb.inc(sc.dma_start(out=swd[:, ei * 2048:(ei + 1) * 2048],
                                        in_=w16["Wdown"][eg * 8 + ei]), "qa")

            for ei in range(8):
                e = eg * 8 + ei
                wslot = e % 2

                @block.sync
                def _(s, e=e, wslot=wslot):
                    if e >= 2 or q > 0:
                        kb.wait(s, "pe")
                        kb.wait(s, "pool")
                    kb.inc(s.dma_start(out=swu[:, wslot * 2048:(wslot + 1) * 2048],
                                       in_=w16["Wup"][e]), "qs")
                    kb.inc(s.dma_start(out=swg[:, wslot * 2048:(wslot + 1) * 2048],
                                       in_=w16["Wgate"][e]), "qs")
                    kb.inc(s.dma_start(
                        out=fsbc[:, wslot * TQL:(wslot + 1) * TQL],
                        in_=fs_dram[e:e + 1, q * TQL:(q + 1) * TQL].to_broadcast([128, TQL])), "qs")

                for tc in range(NTC):
                    sl = slice(tc * 512, (tc + 1) * 512)
                    for hh in range(2):
                        upb = ps[0]
                        gab = ps[1]

                        @block.tensor
                        def _(t, e=e, tc=tc, hh=hh, wslot=wslot, sl=sl, upb=upb, gab=gab):
                            kb.wait(t, "qs")
                            kb.wait(t, "dve")
                            kb.wait(t, "act")
                            for c in range(8):
                                t.matmul(upb[:, :],
                                         lhsT=swu[:, wslot * 2048 + c * 256 + hh * 128:
                                                  wslot * 2048 + c * 256 + (hh + 1) * 128],
                                         rhs=yq[:, c * TQL:][:, sl],
                                         start=(c == 0), stop=(c == 7))
                            for c in range(8):
                                ins = t.matmul(gab[:, :],
                                         lhsT=swg[:, wslot * 2048 + c * 256 + hh * 128:
                                                  wslot * 2048 + c * 256 + (hh + 1) * 128],
                                         rhs=yq[:, c * TQL:][:, sl],
                                         start=(c == 0), stop=(c == 7))
                            kb.inc(ins, "pe")

                        @block.scalar
                        def _(sc, e=e, hh=hh, upb=upb):
                            kb.wait(sc, "pe")
                            kb.inc(sc.activation(out=silu_sb[:, hh * 512:(hh + 1) * 512],
                                                 in_=upb[:, :], func=AF.Silu,
                                                 bias=bias16["bup"][:, e * 2 + hh:e * 2 + hh + 1]), "act")

                        @block.vector
                        def _(v, e=e, ei=ei, tc=tc, hh=hh, sl=sl, gab=gab):
                            kb.wait(v, "act")
                            kb.wait(v, "pe")
                            dst = hgrp[:, ((ei * 2 + hh) * TQL):][:, sl]
                            kb.inc(v.scalar_tensor_tensor(
                                out=dst, in0=gab[:, :],
                                scalar=bias16["bgate"][:, e * 2 + hh:e * 2 + hh + 1],
                                in1=silu_sb[:, hh * 512:(hh + 1) * 512],
                                op0=ALU.add, op1=ALU.mult), "dve")

                        @block.gpsimd
                        def _(g, ei=ei, tc=tc, hh=hh, sl=sl, wslot=wslot):
                            kb.wait(g, "dve")
                            kb.wait(g, "qs")
                            dst = hgrp[:, ((ei * 2 + hh) * TQL):][:, sl]
                            kb.inc(g.tensor_tensor(dst, dst,
                                                   fsbc[:, wslot * TQL:][:, sl],
                                                   op=ALU.mult), "pool")

            for tc in range(NTC):
                sl = slice(tc * 512, (tc + 1) * 512)
                for dw in range(2):
                    for do in range(4):
                        dout = dw * 4 + do
                        pbank = ps[2 + do]

                        @block.tensor
                        def _(t, q=q, eg=eg, tc=tc, dout=dout, sl=sl, pbank=pbank):
                            kb.wait(t, "qa")
                            kb.wait(t, "pool")
                            kb.wait(t, "dve")
                            for ei in range(8):
                                for hc in range(2):
                                    first = (ei == 0 and hc == 0)
                                    last = (ei == 7 and hc == 1 and eg != 7)
                                    ins = t.matmul(pbank[:, :],
                                             lhsT=swd[:, ei * 2048 + hc * 1024 + dout * 128:
                                                      ei * 2048 + hc * 1024 + (dout + 1) * 128],
                                             rhs=hgrp[:, ((ei * 2 + hc) * TQL):][:, sl],
                                             start=first, stop=last)
                                    if last:
                                        kb.inc(ins, "pe")
                            if eg == 7:
                                kb.inc(t.matmul(pbank[:, :],
                                                lhsT=bdown_sb[:, dout * 128:(dout + 1) * 128],
                                                rhs=fsfm[:, q * TQL:][:, sl],
                                                start=False, stop=True), "pe")

                        @block.vector
                        def _(v, q=q, eg=eg, tc=tc, dout=dout, sl=sl, pbank=pbank):
                            kb.wait(v, "pe")
                            dsl = bufs["dacc"][:, (tc * 8 + dout) * 512:(tc * 8 + dout + 1) * 512]
                            if eg == 0:
                                kb.inc(v.tensor_copy(dsl, pbank[:, :]), "dve")
                            else:
                                kb.inc(v.tensor_tensor(dsl, dsl, pbank[:, :], op=ALU.add), "dve")

        for tc in range(NTC):
            for dout in range(8):
                @block.vector
                def _(v, tc=tc, dout=dout):
                    kb.wait(v, "pe")
                    sl = slice(tc * 512, (tc + 1) * 512)
                    kb.inc(v.scalar_tensor_tensor(
                        out=xq[:, dout * TQL:][:, sl],
                        in0=bufs["dacc"][:, (tc * 8 + dout) * 512:(tc * 8 + dout + 1) * 512],
                        scalar=0.5, in1=xq[:, dout * TQL:][:, sl],
                        op0=ALU.mult, op1=ALU.add), "dve")

        @block.sync
        def _(s, q=q):
            kb.wait(s, "dve")
            for c in range(8):
                kb.inc(s.dma_start(out=x_img_out[:, c * T + q * TQL: c * T + (q + 1) * TQL],
                                   in_=xq[:, c * TQL:(c + 1) * TQL]), "qs")


def emit_fs(kb, block, scores_own, sigbc, fsfm, fs_dram, ps, cst, bufs):
    stile = bufs["stile"]
    f0 = bufs["fs0"]
    msk = bufs["fsmask"]
    den = bufs["fsden"]
    for i in range(T // 128):
        @block.scalar
        def _(sc, i=i):
            if i > 0:
                kb.wait(sc, "dve")
            kb.inc(sc.dma_start(out=stile[:, :], in_=scores_own[i * 128:(i + 1) * 128, :]), "qa")

        @block.vector
        def _(v, i=i):
            kb.wait(v, "qa")
            kb.wait(v, "qs")
            if i > 0:
                kb.wait(v, "pe")
            kb.inc(v.tensor_tensor(msk[:, :], stile[:, :], sigbc[:, :], op=ALU.is_ge), "dve")
            kb.wait(v, "dve")
            kb.inc(v.tensor_tensor(f0[:, :], stile[:, :], msk[:, :], op=ALU.mult), "dve")
            kb.wait(v, "dve")
            kb.inc(v.reduce_sum(den[:, :], f0[:, :], axis=AX.X), "dve")
            kb.wait(v, "dve")
            v.tensor_scalar_add(den[:, :], den[:, :], 1e-6)
            kb.inc(v.reciprocal(den[:, :], den[:, :]), "dve")
            kb.wait(v, "dve")
            kb.inc(v.tensor_scalar_mul(f0[:, :], f0[:, :], den[:, :]), "dve")

        @block.tensor
        def _(t, i=i):
            kb.wait(t, "dve")
            if i % 4 == 0 and i > 0:
                kb.wait(t, "act")
            kb.inc(t.transpose(ps[2][0:64, (i % 4) * 128:(i % 4 + 1) * 128],
                               in_=f0[:, :], identity=cst["ident"][:, :]), "pe")

        if i % 4 == 3:
            @block.scalar
            def _(sc, i=i):
                kb.wait(sc, "pe")
                kb.inc(sc.copy(fsfm[:, (i - 3) * 128:(i + 1) * 128],
                               ps[2][0:64, 0:512]), "act")

    @block.sync
    def _(s):
        kb.wait(s, "act")
        kb.inc(s.dma_start(out=fs_dram, in_=fsfm[:, :]), "qs")
        kb.wait(s, "qs")
        kb.inc(s.dma_start(out=bufs["fsrb"][:, :], in_=fs_dram), "qs")


def build_l2(debug=False):
    nc = bass.Bass("TRN2", target_bir_lowering=False, debug=False)
    x_img = nc.dram_tensor("x_img", [128, 8 * T], F32, kind="ExternalInput").ap()
    scores_em = nc.dram_tensor("scores_em", [E, NTOK], F32, kind="ExternalInput").ap()
    scores_own = nc.dram_tensor("scores_own", [T, E], F32, kind="ExternalInput").ap()
    w_in = {
        "Wup": nc.dram_tensor("Wup", [E, C, HH], F32, kind="ExternalInput").ap(),
        "Wgate": nc.dram_tensor("Wgate", [E, C, HH], F32, kind="ExternalInput").ap(),
        "Wdown": nc.dram_tensor("Wdown", [E, HH, C], F32, kind="ExternalInput").ap(),
        "bup": nc.dram_tensor("bup", [E, HH], F32, kind="ExternalInput").ap(),
        "bgate": nc.dram_tensor("bgate", [E, HH], F32, kind="ExternalInput").ap(),
        "bdown": nc.dram_tensor("bdown", [E, C], F32, kind="ExternalInput").ap(),
    }
    x3_img = nc.dram_tensor("x3_img", [128, 8 * T], F32, kind="ExternalOutput").ap()
    dbg = {}
    if debug:
        dbg["fs"] = nc.dram_tensor("o_dbg_fs", [64, T], BF16, kind="ExternalOutput").ap()
        dbg["y0"] = nc.dram_tensor("o_dbg_y0", [128, 8 * 1024], BF16, kind="ExternalOutput").ap()
        dbg["sig"] = nc.dram_tensor("o_dbg_sig", [E, 1], F32, kind="ExternalOutput").ap()
        dbg["gs"] = nc.dram_tensor("o_dbg_gs", [128, E * 4], F32, kind="ExternalOutput").ap()
        dbg["bm"] = nc.dram_tensor("o_dbg_bm", [128, E * 4], F32, kind="ExternalOutput").ap()
        dbg["th"] = nc.dram_tensor("o_dbg_th", [1, 1], F32, kind="ExternalOutput").ap()
        dbg["srt"] = nc.dram_tensor("o_dbg_srt", [128, E * 256], F32, kind="ExternalOutput").ap()
        dbg["cand"] = nc.dram_tensor("o_dbg_cand", [128, E * 4], F32, kind="ExternalOutput").ap()
        dbg["emin"] = nc.dram_tensor("o_dbg_emin", [128, E], F32, kind="ExternalOutput").ap()
        dbg["mask"] = nc.dram_tensor("o_dbg_mask", [128, E * 4], mybir.dt.uint32, kind="ExternalOutput").ap()
        dbg["sigbc"] = nc.dram_tensor("o_dbg_sigbc", [128, E], F32, kind="ExternalOutput").ap()

    w16 = {
        "Wup": [nc.dram_tensor(f"w16up{e}", [128, 8, HH], BF16).ap() for e in range(E)],
        "Wgate": [nc.dram_tensor(f"w16ga{e}", [128, 8, HH], BF16).ap() for e in range(E)],
        "Wdown": [nc.dram_tensor(f"w16dn{e}", [128, 2, C], BF16).ap() for e in range(E)],
    }
    dram = {
        "theta": nc.dram_tensor("d_theta", [1, 1], F32).ap(),
        "sigma": nc.dram_tensor("d_sigma", [E, 1], F32).ap(),
        "arow": nc.dram_tensor("d_arow", [1, 1024], F32).ap(),
        "brow": nc.dram_tensor("d_brow", [1, 1024], F32).ap(),
        "fs": nc.dram_tensor("d_fs", [E, T], BF16).ap(),
    }

    with ExitStack() as stack:
        kb = KB(nc, stack)
        cst_dram = {
            "ident": kb.const("c_ident", np.eye(128, dtype=np.float32)),
            "oh31": kb.const("c_oh31", (np.arange(128) == 31).astype(np.float32).reshape(128, 1)),
            "ones1_16": kb.const("c_ones1_16", np.ones((128, 1), ml_dtypes.bfloat16)),
            "ones_row16": kb.const("c_ones_row16", np.ones((1, 512), ml_dtypes.bfloat16)),
            "eps1": kb.const("c_eps1", np.full((128, 1), 1e-5, np.float32)),
            "onescol": kb.const("c_onescol", np.ones((1, 128), np.float32)),
        }
        for m in range(8, 15):
            s = np.where(((np.arange(128) >> (m - 8)) & 1) == 0, 1.0, -1.0).astype(np.float32)
            cst_dram[f"sgn{m}"] = kb.const(f"c_sgn{m}", s.reshape(128, 1))

        cst = {}
        cst["ident"] = sbuf(kb, "ident", [128, 128])
        cst["oh31"] = sbuf(kb, "oh31", [128, 1])
        cst["ones1_16"] = sbuf(kb, "ones1_16", [128, 1], BF16)
        cst["ones_row16"] = sbuf(kb, "ones_row16", [1, 512], BF16)
        cst["eps1"] = sbuf(kb, "eps1", [128, 1])
        cst["onescol"] = sbuf(kb, "onescol", [1, 128])
        for m in range(8, 15):
            cst[f"sgn{m}"] = sbuf(kb, f"sgn{m}", [128, 1])
        cst["junk"] = sbuf(kb, "junk", [1, 1])

        ps = [mkpsum(kb, f"ps{i}") for i in range(8)]

        sig_fm = sbuf(kb, "sig_fm", [E, 1])
        thbc = sbuf(kb, "thbc", [128, 1])
        sigbc = sbuf(kb, "sigbc", [128, E])
        fsfm = sbuf(kb, "fsfm", [64, T], BF16)
        bias16 = {
            "bup": sbuf(kb, "bupS", [128, 128]),
            "bgate": sbuf(kb, "bgaS", [128, 128]),
            "bupR": sbuf(kb, "bupR", [128, 128]),
            "bgaR": sbuf(kb, "bgaR", [128, 128]),
        }
        bdown_sb = sbuf(kb, "bdownS", [64, C], BF16)

        bufs_rb = sbuf(kb, "bufs_rb", [128, 2048], BF16)
        block = stack.enter_context(nc.Block())

        emit_weight_cast(kb, block, w_in, w16)

        @block.sync
        def _(s):
            for nm in ("ident", "oh31", "ones1_16", "ones_row16", "eps1", "onescol"):
                kb.inc(s.dma_start(out=cst[nm][:, :], in_=cst_dram[nm]), "qs")
            for m in range(8, 15):
                kb.inc(s.dma_start(out=cst[f"sgn{m}"][:, :], in_=cst_dram[f"sgn{m}"]), "qs")

        @block.gpsimd
        def _(g):
            kb.inc(g.dma_start(out=bdown_sb[:, :], in_=w_in["bdown"]), "qp")
            kb.inc(g.memset(cst["junk"][:, :], 0.0), "pool")

        @block.sync
        def _(s):
            kb.inc(s.dma_start(out=bias16["bupR"][:, :],
                               in_=w_in["bup"].rearrange("e (r p) -> (e r) p", p=128)), "qs")
            kb.inc(s.dma_start(out=bias16["bgaR"][:, :],
                               in_=w_in["bgate"].rearrange("e (r p) -> (e r) p", p=128)), "qs")

        @block.tensor
        def _(t):
            kb.wait(t, "qs")
            t.transpose(ps[0][:, 0:128], in_=bias16["bupR"][:, :], identity=cst["ident"][:, :])
            kb.inc(t.transpose(ps[1][:, 0:128], in_=bias16["bgaR"][:, :],
                               identity=cst["ident"][:, :]), "pe")

        @block.vector
        def _(v):
            kb.wait(v, "pe")
            v.tensor_copy(bias16["bup"][:, :], ps[0][:, 0:128])
            kb.inc(v.tensor_copy(bias16["bgate"][:, :], ps[1][:, 0:128]), "dve")

        with ExitStack() as sp:
            def sb2(name, shape, dtype=F32):
                return sp.enter_context(nc.sbuf_tensor(name, shape, dtype))
            bufA = sb2("bufA", [128, E * 256])
            bufB = sb2("bufB", [128, E * 256])
            tbufA = sb2("tbufA", [128, 8 * 128])
            tbufB = sb2("tbufB", [128, 8 * 128])
            gs = sb2("gs", [128, E * 4])
            bm = sb2("bm", [128, E * 4])
            thA = sb2("thA", [128, 256])
            thB = sb2("thB", [128, 256])
            scr = {
                "th11": sb2("th11", [1, 1]),
                "mask": sb2("mask", [128, E * 4], mybir.dt.uint32),
                "cand": sb2("cand", [128, E * 4]),
                "emin": sb2("emin", [128, E]),
                "eminT": sb2("eminT", [64, 128]),
                "bigpos": sb2("bigpos", [128, E * 4]),
                "sigrow": sb2("sigrow", [1, 64]),
            }
            fsb = {
                "stile": sb2("stile", [128, 64]),
                "fs0": sb2("fs0", [128, 64]),
                "fsmask": sb2("fsmask", [128, 64]),
                "fsden": sb2("fsden", [128, 1]),
                "fsrb": sb2("fsrb", [64, T], BF16),
            }

            @block.sync
            def _(s):
                kb.wait(s, "qs")
                for gidx in range(8):
                    src = scores_em[gidx * 8:(gidx + 1) * 8, :].rearrange(
                        "e (p f) -> p e f", p=128)
                    kb.inc(s.dma_start(out=bufA[:, :].rearrange(
                        "p (e f) -> p e f", f=256)[:, gidx * 8:(gidx + 1) * 8, :],
                        in_=src), "qs")

            @block.vector
            def _(v):
                kb.wait(v, "qs")
                kb.inc(v.memset(scr["bigpos"][:, :], BIGPOS), "dve")

            emit_sort(kb, block, bufA, bufB, tbufA, tbufB, ps[2], cst, nbat=E)
            emit_gs_bm(kb, block, bufA, gs, bm, nbat=E)
            emit_theta_sigma(kb, block, gs, bm, thA, thB, tbufA, tbufB, ps[2], cst,
                             sig_fm, thbc, sigbc, scr, dram)
            if dbg:
                @block.sync
                def _(s):
                    kb.wait_all(s)
                    kb.inc(s.dma_start(out=dbg["gs"], in_=gs[:, :]), "qs")
                    kb.inc(s.dma_start(out=dbg["bm"], in_=bm[:, :]), "qs")
                    kb.inc(s.dma_start(out=dbg["th"], in_=scr["th11"][:, :]), "qs")
                    kb.inc(s.dma_start(out=dbg["srt"], in_=bufA[:, :]), "qs")
                    kb.inc(s.dma_start(out=dbg["cand"], in_=scr["cand"][:, :]), "qs")
                    kb.inc(s.dma_start(out=dbg["emin"], in_=scr["emin"][:, :]), "qs")
                    kb.inc(s.dma_start(out=dbg["mask"], in_=scr["mask"][:, :]), "qs")
                    kb.inc(s.dma_start(out=dbg["sigbc"], in_=sigbc[:, :]), "qs")
            emit_fs(kb, block, scores_own, sigbc, fsfm, dram["fs"], ps, cst, fsb)

            for engname in ("vector", "gpsimd", "scalar", "tensor", "sync"):
                def mk(e, _n=engname):
                    kb.wait_all(e)
                getattr(block, engname)(mk)

        @block.sync
        def _(s):
            kb.wait(s, "qp")
            for e in range(E):
                for nm in ("Wup", "Wgate", "Wdown"):
                    kb.inc(s.dma_start(out=bufs_rb[:, :],
                                       in_=w16[nm][e].rearrange("p c h -> p (c h)")), "qs")

        bufs = {
            "xq": sbuf(kb, "xq", [128, 8 * 1024]),
            "yq": sbuf(kb, "yq", [128, 8 * 1024], BF16),
            "hgrp": sbuf(kb, "hgrp", [128, 16 * 1024], BF16),
            "dacc": sbuf(kb, "dacc", [128, 16 * 512]),
            "silu": sbuf(kb, "silu", [128, 2 * 512], BF16),
            "fsbc": sbuf(kb, "fsbc", [128, 2 * 1024], BF16),
            "swu": sbuf(kb, "swu", [128, 2 * 2048], BF16),
            "swg": sbuf(kb, "swg", [128, 2 * 2048], BF16),
            "swd": sbuf(kb, "swd", [128, 8 * 2048], BF16),
            "arow": sbuf(kb, "arow", [1, 1024]),
            "brow": sbuf(kb, "brow", [1, 1024]),
            "abc": sbuf(kb, "abc", [128, 1024]),
            "bbc": sbuf(kb, "bbc", [128, 1024]),
            "xsq": sbuf(kb, "xsq", [128, 512], BF16),
            "xbf": sbuf(kb, "xbf", [128, 512], BF16),
            "srow": sbuf(kb, "srow", [1, 1024]),
        }
        if dbg:
            @block.sync
            def _(s):
                kb.wait_all(s)
                kb.inc(s.dma_start(out=dbg["fs"], in_=fsfm[:, :]), "qs")
                kb.inc(s.dma_start(out=dbg["sig"], in_=sig_fm[:, :]), "qs")
        emit_ffn_sweep(kb, block, x_img, x3_img, dram["fs"], w16, bias16,
                       bdown_sb, fsfm, ps, cst, bufs, dram, dbg=dbg)

        @block.sync
        def _(s):
            kb.wait_all(s)
    return nc


def emit_scan(kb, block, at_d, bt_d, kt_d, rt_d, v_d, cexp_d, o_d, ps, cst, bufs,
              epilogue_fn=None, cexp_sb=None):
    NCH = T // 64
    SEG = 512
    NSEG = T // SEG
    CPS = SEG // 64

    a_s, b_s, k_s, r_s = bufs["a_s"], bufs["b_s"], bufs["k_s"], bufs["r_s"]
    v_s = bufs["v_s"]
    cexp = bufs["cexp"]
    yt = bufs["yt"]
    y0p = bufs["y0p"]
    bd = bufs["bd"]
    scmm = bufs["scmm"]
    lad = bufs["lad"]
    mrkt = bufs["mrkt"]
    mkav = bufs["mkav"]
    btm = bufs["btm"]
    xs = bufs["xs"]
    dlt = bufs["dlt"]
    o_tm = bufs["o_tm"]

    if cexp_sb is None:
        @block.sync
        def _(s):
            kb.wait_all(s)
            kb.inc(s.dma_start(out=cexp[:, :], in_=cexp_d), "qs")
    else:
        cexp = cexp_sb

    @block.vector
    def _(v):
        kb.wait(v, "qs")
        kb.inc(v.memset(yt[:, :], 0.0), "dve")

    @block.gpsimd
    def _(g):
        kb.wait(g, "qs")
        kb.inc(g.memset(bd[:, :], 0.0), "pool")

    for seg in range(NSEG):
        @block.sync
        def _(s, seg=seg):
            kb.wait_all(s, ("pe", "dve", "act", "pool", "qs"))
            for c in range(8):
                for nm, buf in ((at_d, a_s), (bt_d, b_s), (kt_d, k_s), (rt_d, r_s)):
                    kb.inc(s.dma_start(out=buf[:, c * SEG:(c + 1) * SEG],
                                       in_=nm[:, c * T + seg * SEG:c * T + (seg + 1) * SEG]), "qs")
            for st in range(SEG // 128):
                kb.inc(s.dma_start(out=v_s[:, st * C:(st + 1) * C],
                                   in_=v_d[seg * SEG + st * 128: seg * SEG + (st + 1) * 128, :]), "qs")

        for hp in range(8):
            for ci in range(CPS):
                ch = seg * CPS + ci
                tsl = slice(ci * 64, (ci + 1) * 64)

                @block.vector
                def _(v, hp=hp, ci=ci, tsl=tsl):
                    kb.wait(v, "qs")
                    kb.wait(v, "pe")
                    for bi, buf in enumerate((a_s, b_s, k_s, r_s)):
                        src = buf[:, hp * SEG:][:, tsl]
                        v.tensor_copy(bd[0:64, bi * 128:bi * 128 + 64], src[0:64, :])
                        v.tensor_copy(bd[64:128, bi * 128 + 64:bi * 128 + 128], src[64:128, :])
                    kb.inc(v.memset(cst["junk"][:, :], 0.0), "dve")

                Abd = bd[:, 0:128]
                Bbd = bd[:, 128:256]
                Kbd = bd[:, 256:384]
                Rbd = bd[:, 384:512]
                Vbd = bd[:, 512:640]

                st0 = (ci * 64) // 128
                r0 = (ci * 64) % 128
                par = r0 // 64
                vs_blk = v_s[:, st0 * C + hp * 128: st0 * C + (hp + 1) * 128]

                @block.tensor
                def _(t, vs_blk=vs_blk, par=par):
                    kb.wait(t, "dve")
                    t.matmul(ps[1][:, 0:64], lhsT=cst[f"vperm{par}a"][:, :],
                             rhs=vs_blk[:, 0:64], start=True, stop=True)
                    kb.inc(t.matmul(ps[1][:, 64:128], lhsT=cst[f"vperm{par}b"][:, :],
                                    rhs=vs_blk[:, 64:128], start=True, stop=True), "pe")

                @block.vector
                def _(v, Vbd=Vbd):
                    kb.wait(v, "pe")
                    kb.inc(v.tensor_copy(Vbd, ps[1][:, 0:128]), "dve")

                @block.tensor
                def _(t, hp=hp):
                    kb.wait(t, "dve")
                    t.matmul(ps[0][:, 0:128], lhsT=Abd, rhs=Bbd, start=True, stop=True)
                    t.matmul(ps[0][:, 128:256], lhsT=Bbd, rhs=Abd, start=True, stop=True)
                    t.matmul(ps[0][:, 256:384], lhsT=Kbd, rhs=Abd, start=True, stop=True)
                    kb.inc(t.matmul(ps[0][:, 384:512], lhsT=Bbd, rhs=Rbd, start=True, stop=True), "pe")

                @block.vector
                def _(v):
                    kb.wait(v, "pe")
                    kb.inc(v.tensor_tensor(scmm[:, :], ps[0][:, :], cst["mask4"][:, :],
                                           op=ALU.mult), "dve")

                M_, N_, Nka, MrbT = (scmm[:, 0:128], scmm[:, 128:256],
                                     scmm[:, 256:384], scmm[:, 384:512])

                @block.tensor
                def _(t):
                    kb.wait(t, "dve")
                    t.matmul(ps[1][:, 0:128], lhsT=Kbd, rhs=Rbd, start=True, stop=True)
                    t.matmul(ps[1][:, 128:256], lhsT=Nka, rhs=Vbd, start=True, stop=True)
                    t.transpose(bufs["ps16"][:, 0:128], in_=Bbd, identity=cst["ident16"][:, :])
                    kb.inc(t.transpose(bufs["ps16"][:, 128:256], in_=Kbd,
                                       identity=cst["ident16"][:, :]), "pe")

                @block.vector
                def _(v):
                    kb.wait(v, "pe")
                    v.tensor_tensor(mrkt[:, :], ps[1][:, 0:128], cst["masku_inc"][:, :],
                                    op=ALU.mult)
                    v.tensor_copy(mkav[:, :], ps[1][:, 128:256])
                    kb.inc(v.tensor_copy(btm[:, :], bufs["ps16"][:, 0:256]), "dve")

                lpairs = [(None, None)]
                prevM, prevN = M_, N_
                for li in range(5):
                    o0 = (li % 2) * 256

                    @block.tensor
                    def _(t, prevM=prevM, prevN=prevN, o0=o0, li=li):
                        kb.wait(t, "dve")
                        t.matmul(ps[2][:, o0:o0 + 128], lhsT=prevN, rhs=prevM,
                                 start=True, stop=True)
                        kb.inc(t.matmul(ps[2][:, o0 + 128:o0 + 256], lhsT=prevM, rhs=prevN,
                                        start=True, stop=True), "pe")

                    @block.vector
                    def _(v, li=li, o0=o0):
                        kb.wait(v, "pe")
                        kb.inc(v.tensor_copy(lad[:, li * 256:(li + 1) * 256],
                                             ps[2][:, o0:o0 + 256]), "dve")

                    prevM = lad[:, li * 256:li * 256 + 128]
                    prevN = lad[:, li * 256 + 128:(li + 1) * 256]

                yth = yt[:, hp * 128:(hp + 1) * 128]
                ce = cexp[:, hp * NCH + ch:hp * NCH + ch + 1]

                @block.vector
                def _(v, yth=yth, ce=ce):
                    kb.wait(v, "dve")
                    kb.inc(v.tensor_scalar_mul(y0p[:, :], yth, ce), "dve")

                @block.tensor
                def _(t):
                    kb.wait(t, "dve")
                    kb.inc(t.matmul(ps[3][:, 0:128], lhsT=Abd, rhs=y0p[:, :],
                                    start=True, stop=True), "pe")

                @block.vector
                def _(v):
                    kb.wait(v, "pe")
                    kb.inc(v.tensor_tensor(xs[:, :], ps[3][:, 0:128], mkav[:, :],
                                           op=ALU.add), "dve")

                for li in range(6):
                    if li == 5:
                        lhs = N_
                    else:
                        lhs = lad[:, (4 - li) * 256 + 128:(5 - li) * 256]
                    o0 = 128 + (li % 2) * 128

                    @block.tensor
                    def _(t, lhs=lhs, o0=o0):
                        kb.wait(t, "dve")
                        kb.inc(t.matmul(ps[3][:, o0:o0 + 128], lhsT=lhs, rhs=xs[:, :],
                                        start=True, stop=True), "pe")

                    @block.vector
                    def _(v, o0=o0):
                        kb.wait(v, "pe")
                        kb.inc(v.tensor_tensor(xs[:, :], ps[3][:, o0:o0 + 128], xs[:, :],
                                               op=ALU.add), "dve")

                @block.tensor
                def _(t):
                    kb.wait(t, "dve")
                    t.matmul(ps[4][:, 0:128], lhsT=Rbd, rhs=y0p[:, :], start=True, stop=False)
                    t.matmul(ps[4][:, 0:128], lhsT=MrbT, rhs=xs[:, :], start=False, stop=False)
                    kb.inc(t.matmul(ps[4][:, 0:128], lhsT=mrkt[:, :], rhs=Vbd,
                                    start=False, stop=True), "pe")

                @block.vector
                def _(v, par=par):
                    kb.wait(v, "pe")
                    kb.inc(v.tensor_copy(bufs["osb"][:, par * 128:(par + 1) * 128],
                                         ps[4][:, 0:128]), "dve")

                if par == 1:
                    @block.tensor
                    def _(t):
                        kb.wait(t, "dve")
                        t.matmul(ps[4][:, 128:192], lhsT=cst["operm0a"][:, :],
                                 rhs=bufs["osb"][:, 0:64], start=True, stop=False)
                        t.matmul(ps[4][:, 192:256], lhsT=cst["operm0b"][:, :],
                                 rhs=bufs["osb"][:, 64:128], start=False, stop=False)
                        t.matmul(ps[4][:, 128:192], lhsT=cst["operm1a"][:, :],
                                 rhs=bufs["osb"][:, 128:192], start=False, stop=False)
                        kb.inc(t.matmul(ps[4][:, 192:256], lhsT=cst["operm1b"][:, :],
                                        rhs=bufs["osb"][:, 192:256], start=False, stop=True), "pe")

                    @block.vector
                    def _(v, hp=hp, st0=st0):
                        kb.wait(v, "pe")
                        dst = o_tm[:, st0 * C + hp * 128: st0 * C + (hp + 1) * 128]
                        kb.inc(v.tensor_copy(dst, ps[4][:, 128:256]), "dve")

                @block.tensor
                def _(t):
                    kb.wait(t, "dve")
                    t.matmul(ps[5][:, 0:128], lhsT=btm[:, 0:128], rhs=xs[:, :],
                             start=True, stop=False)
                    kb.inc(t.matmul(ps[5][:, 0:128], lhsT=btm[:, 128:256], rhs=Vbd,
                                    start=False, stop=True), "pe")

                @block.vector
                def _(v, yth=yth, ce=ce):
                    kb.wait(v, "pe")
                    kb.inc(v.tensor_scalar_mul(dlt[:, :], ps[5][:, 0:128], ce), "dve")
                    kb.wait(v, "dve")
                    kb.inc(v.tensor_scalar_mul(yth, yth, ce), "dve")
                    kb.wait(v, "dve")
                    kb.inc(v.scalar_tensor_tensor(out=yth, in0=yth, scalar=ce,
                                                  in1=dlt[:, :], op0=ALU.mult,
                                                  op1=ALU.add), "dve")

        if epilogue_fn is None:
            @block.sync
            def _(s, seg=seg):
                kb.wait(s, "dve")
                for st in range(SEG // 128):
                    kb.inc(s.dma_start(out=o_d[seg * SEG + st * 128: seg * SEG + (st + 1) * 128, :],
                                       in_=o_tm[:, st * C:(st + 1) * C]), "qs")
        else:
            epilogue_fn(seg)


VEC_IDX = {"x_r": 0, "x_w": 1, "x_k": 2, "x_v": 3, "x_a": 4, "k_k": 5, "k_a": 6,
           "r_k": 7, "wbias": 8, "neg_wbias": 9, "abias": 10, "one_m_ka": 11}


def vcol(aw, name, c):
    return aw["vecs"][:, VEC_IDX[name] * 8 + c:VEC_IDX[name] * 8 + c + 1]


def emit_attn_prep(kb, block, x1_img, aw, ps, ps16, cst, bufs, dram, stop_after=None):
    W = 512
    NT = T // W
    A, Bb, Cc, Ee = bufs["fA"], bufs["fB"], bufs["fC"], bufs["fE"]
    b = [bufs[f"b{i}"] for i in range(6)]
    z64 = bufs["z64"]
    carry = bufs["carry"]
    cval = bufs["cval"]
    arow, brow = bufs["arow"], bufs["brow"]
    abc, bbc = bufs["abc"], bufs["bbc"]
    class _SrowView:
        pass
    cexp = bufs["cexp_sb"]
    bonus = bufs["bonus"]

    def CH(buf, c):
        return buf[:, c * W:(c + 1) * W]

    @block.vector
    def _(v):
        kb.wait_all(v)
        kb.inc(v.memset(carry[:, :], 0.0), "dve")

    for ti in range(NT):
        @block.sync
        def _(s, ti=ti):
            kb.wait_all(s, ("dve", "pe", "act", "pool", "qs"))
            for c in range(8):
                kb.inc(s.dma_start(out=CH(A, c),
                                   in_=x1_img[:, c * T + ti * W:c * T + (ti + 1) * W]), "qs")

        @block.scalar
        def _(sc):
            kb.wait(sc, "qs")
            kb.wait(sc, "pe")
            for c in range(8):
                kb.inc(sc.activation(out=CH(b[0], c), in_=CH(A, c), func=AF.Square), "act")

        @block.gpsimd
        def _(g):
            kb.wait(g, "qs")
            kb.wait(g, "pe")
            for c in range(8):
                kb.inc(g.tensor_copy(CH(b[5], c), CH(A, c)), "pool")

        @block.tensor
        def _(t):
            kb.wait(t, "act")
            kb.wait(t, "pool")
            kb.wait(t, "dve")
            for c in range(8):
                t.matmul(ps[2][0:1, :], lhsT=cst["ones1_16"][:, :], rhs=CH(b[5], c),
                         start=(c == 0), stop=(c == 7))
            for c in range(8):
                ins_ = t.matmul(ps[3][0:1, :], lhsT=cst["ones1_16"][:, :], rhs=CH(b[0], c),
                                start=(c == 0), stop=(c == 7))
            kb.inc(ins_, "pe")

        @block.vector
        def _(v):
            kb.wait(v, "pe")
            v.tensor_scalar_mul(brow[0:1, 512:1024], ps[2][0:1, :], 1.0 / C)
            kb.inc(v.tensor_scalar_mul(arow[0:1, 512:1024], ps[3][0:1, :], 1.0 / C), "dve")
            kb.wait(v, "dve")
            kb.inc(v.tensor_tensor(arow[0:1, 0:512], brow[0:1, 512:1024], brow[0:1, 512:1024],
                                   op=ALU.mult), "dve")
            kb.wait(v, "dve")
            kb.inc(v.tensor_tensor(arow[0:1, 0:512], arow[0:1, 512:1024], arow[0:1, 0:512],
                                   op=ALU.subtract), "dve")

        @block.scalar
        def _(sc):
            kb.wait(sc, "dve")
            kb.inc(sc.activation(out=brow[0:1, 0:512], in_=arow[0:1, 0:512], func=AF.Ln,
                                 bias=cst["eps1"][0:1, 0:1]), "act")
            kb.wait(sc, "act")
            kb.inc(sc.activation(out=arow[0:1, 0:512], in_=brow[0:1, 0:512], func=AF.Exp,
                                 scale=-0.5), "act")

        @block.vector
        def _(v):
            kb.wait(v, "act")
            kb.wait(v, "dve")
            kb.inc(v.tensor_tensor(brow[0:1, 0:512], brow[0:1, 512:1024], arow[0:1, 0:512],
                                   op=ALU.mult), "dve")

        @block.tensor
        def _(t):
            kb.wait(t, "dve")
            t.matmul(ps[2][:, :], lhsT=cst["onescol"][:, :], rhs=arow[0:1, 0:512],
                     start=True, stop=True)
            kb.inc(t.matmul(ps[3][:, :], lhsT=cst["onescol"][:, :], rhs=brow[0:1, 0:512],
                            start=True, stop=True), "pe")

        @block.vector
        def _(v):
            kb.wait(v, "pe")
            v.tensor_copy(abc[:, :], ps[2][:, :])
            kb.inc(v.tensor_copy(bbc[:, :], ps[3][:, :]), "dve")
            kb.wait(v, "dve")
            for c in range(8):
                kb.inc(v.tensor_tensor(CH(Bb, c), CH(A, c), abc[:, :], op=ALU.mult), "dve")
            kb.wait(v, "dve")
            for c in range(8):
                kb.inc(v.tensor_tensor(CH(Bb, c), CH(Bb, c), bbc[:, :], op=ALU.subtract), "dve")

        @block.vector
        def _(v, ti=ti):
            kb.wait(v, "dve")
            for c in range(8):
                v.tensor_tensor(Cc[:, c * W + 1:(c + 1) * W], Bb[:, c * W:(c + 1) * W - 1],
                                Bb[:, c * W + 1:(c + 1) * W], op=ALU.subtract)
            kb.inc(v.tensor_tensor(Cc[:, 0:4096:W], carry[:, :], Bb[:, 0:4096:W],
                                   op=ALU.subtract), "dve")
            kb.wait(v, "dve")
            kb.inc(v.tensor_copy(carry[:, :], Bb[:, W - 1:4096:W]), "dve")
            kb.wait(v, "dve")
            for vi, nm in enumerate(("x_r", "x_w", "x_k", "x_v", "x_a")):
                for c in range(8):
                    ins_ = v.scalar_tensor_tensor(
                        out=CH(b[vi], c), in0=CH(Cc, c), scalar=vcol(aw, nm, c),
                        in1=CH(Bb, c), op0=ALU.mult, op1=ALU.add)
            kb.inc(ins_, "dve")

        if stop_after == "mix":
            return
        for oc in range(8):
            @block.tensor
            def _(t, oc=oc):
                kb.wait(t, "dve")
                kb.wait(t, "act")
                for c in range(8):
                    ins_ = t.matmul(ps[0][:, :],
                                    lhsT=aw["wr16"][:, c * 1024 + oc * 128:c * 1024 + (oc + 1) * 128],
                                    rhs=CH(b[0], c), start=(c == 0), stop=(c == 7))
                kb.inc(ins_, "pe")

            @block.scalar
            def _(sc, oc=oc):
                kb.wait(sc, "pe")
                kb.inc(sc.copy(CH(b[5], oc), ps[0][:, :]), "act")

        if stop_after == "rproj":
            return
        @block.tensor
        def _(t):
            kb.wait(t, "dve")
            kb.wait(t, "act")
            for c in range(8):
                ins_ = t.matmul(ps[5][0:64, 0:512], lhsT=aw["wla16"][:, c * 64:(c + 1) * 64],
                                rhs=CH(b[1], c), start=(c == 0), stop=(c == 7))
            kb.inc(ins_, "pe")

        @block.scalar
        def _(sc):
            kb.wait(sc, "pe")
            kb.inc(sc.activation(out=z64, in_=ps[5][0:64, 0:512], func=AF.Tanh), "act")

        for wv in range(2):
            for oci in range(4):
                oc = wv * 4 + oci

                @block.tensor
                def _(t, oc=oc, wv=wv, oci=oci):
                    kb.wait(t, "act")
                    if wv > 0 and oci == 0:
                        kb.wait(t, "dve")
                    kb.inc(t.matmul(ps[5][:, 0:512], lhsT=aw["wlb16"][:, oc * 128:(oc + 1) * 128],
                                    rhs=z64, start=True, stop=True), "pe")

                @block.scalar
                def _(sc, oc=oc, oci=oci):
                    kb.wait(sc, "pe")
                    sc.activation(out=CH(Ee, oci), in_=ps[5][:, 0:512], func=AF.Relu,
                                  scale=-1.0, bias=vcol(aw, "neg_wbias", oc))
                    kb.inc(sc.activation(out=CH(A, oc), in_=ps[5][:, 0:512],
                                         func=AF.Abs, bias=vcol(aw, "wbias", oc)), "act")

            @block.scalar
            def _(sc, wv=wv):
                kb.wait(sc, "act")
                for oci in range(4):
                    kb.inc(sc.activation(out=CH(A, wv * 4 + oci), in_=CH(A, wv * 4 + oci),
                                         func=AF.Exp, scale=-1.0), "act")
                kb.wait(sc, "act")
                for oci in range(4):
                    kb.inc(sc.activation(out=CH(A, wv * 4 + oci), in_=CH(A, wv * 4 + oci),
                                         func=AF.Ln, bias=1.0), "act")

            @block.vector
            def _(v, wv=wv):
                kb.wait(v, "act")
                for oci in range(4):
                    kb.inc(v.tensor_tensor(CH(A, wv * 4 + oci), CH(A, wv * 4 + oci),
                                           CH(Ee, oci), op=ALU.add), "dve")
                kb.wait(v, "dve")
                for oci in range(4):
                    kb.inc(v.tensor_scalar(CH(A, wv * 4 + oci), CH(A, wv * 4 + oci), -1.0, -0.5,
                                           op0=ALU.mult, op1=ALU.add), "dve")

        if stop_after == "wfm":
            return
        @block.vector
        def _(v, ti=ti):
            kb.wait(v, "dve")
            for c in range(8):
                for ch in range(8):
                    ins_ = v.tensor_tensor_scan(
                        out=Cc[:, c * W + ch * 64:c * W + (ch + 1) * 64],
                        data0=cst["ones64"][:, :],
                        data1=A[:, c * W + ch * 64:c * W + (ch + 1) * 64],
                        initial=0.0, op0=ALU.mult, op1=ALU.add)
            kb.inc(ins_, "dve")
            kb.wait(v, "dve")
            kb.inc(v.tensor_scalar_mul(
                cval[:, :], Cc[:, :].rearrange("p (cc t) -> p cc t", t=64)[:, :, 63], 0.5), "dve")

        @block.scalar
        def _(sc, ti=ti):
            kb.wait(sc, "dve")
            kb.inc(sc.activation(
                out=cexp[:, :].rearrange("p (cc n) -> p cc n", n=64)[:, :, ti * 8:(ti + 1) * 8],
                in_=cval[:, :].rearrange("p (cc n) -> p cc n", n=8), func=AF.Exp), "act")

        if stop_after == "cum":
            return
        for c in range(8):
            @block.vector
            def _(v, c=c):
                kb.wait(v, "dve")
                if c == 0:
                    kb.wait(v, "act")
                cv = cval[:, c * 8:(c + 1) * 8]
                g_ = Cc[:, c * W:(c + 1) * W].rearrange("p (ch t) -> p ch t", t=64)
                w_ = A[:, c * W:(c + 1) * W].rearrange("p (ch t) -> p ch t", t=64)
                e_ = Bb[:, c * W:(c + 1) * W].rearrange("p (ch t) -> p ch t", t=64)
                cbc = cv.unsqueeze(2).to_broadcast([128, 8, 64])
                kb.inc(v.tensor_tensor(e_, g_, w_, op=ALU.subtract), "dve")
                kb.wait(v, "dve")
                kb.inc(v.tensor_tensor(e_, e_, cbc, op=ALU.subtract), "dve")

        if stop_after == "efa_arg":
            return
        @block.tensor
        def _(t):
            kb.wait(t, "dve")
            kb.wait(t, "act")
            for c in range(8):
                ins_ = t.matmul(ps[5][0:64, 0:512], lhsT=aw["ala16"][:, c * 64:(c + 1) * 64],
                                rhs=CH(b[4], c), start=(c == 0), stop=(c == 7))
            kb.inc(ins_, "pe")

        @block.scalar
        def _(sc):
            kb.wait(sc, "pe")
            kb.inc(sc.copy(z64, ps[5][0:64, 0:512]), "act")

        for oc in range(8):
            @block.tensor
            def _(t, oc=oc):
                kb.wait(t, "act")
                kb.inc(t.matmul(ps[5][:, 0:512], lhsT=aw["alb16"][:, oc * 128:(oc + 1) * 128],
                                rhs=z64, start=True, stop=True), "pe")

            @block.scalar
            def _(sc, oc=oc):
                kb.wait(sc, "pe")
                kb.inc(sc.activation(out=CH(b[1], oc), in_=ps[5][:, 0:512],
                                     func=AF.Sigmoid, bias=vcol(aw, "abias", oc)), "act")

        if stop_after == "alora":
            return
        for oc in range(8):
            @block.tensor
            def _(t, oc=oc):
                kb.wait(t, "dve")
                kb.wait(t, "act")
                for c in range(8):
                    ins_ = t.matmul(ps[0][:, :],
                                    lhsT=aw["wk16"][:, c * 1024 + oc * 128:c * 1024 + (oc + 1) * 128],
                                    rhs=CH(b[2], c), start=(c == 0), stop=(c == 7))
                kb.inc(ins_, "pe")

            @block.scalar
            def _(sc, oc=oc):
                kb.wait(sc, "pe")
                kb.inc(sc.copy(CH(A, oc), ps[0][:, :]), "act")

        if stop_after == "kproj":
            return
        for c in range(8):
            @block.vector
            def _(v, c=c):
                kb.wait(v, "act")
                if c == 0:
                    kb.wait(v, "dve")
                kb.inc(v.tensor_scalar_mul(CH(Ee, c % 4), CH(A, c), vcol(aw, "k_k", c)), "dve")

            @block.scalar
            def _(sc, c=c):
                kb.wait(sc, "dve")
                kb.inc(sc.activation(out=CH(b[0], c), in_=CH(Ee, c % 4), func=AF.Square), "act")

        @block.tensor
        def _(t):
            kb.wait(t, "act")
            for c in range(8):
                ins_ = t.matmul(ps[5][0:16, 0:512], lhsT=aw[f"bdsel{c}"][:, :],
                                rhs=CH(b[0], c), start=(c == 0), stop=(c == 7))
            kb.inc(ins_, "pe")

        @block.vector
        def _(v):
            kb.wait(v, "pe")
            kb.inc(v.tensor_scalar_max(abc[0:16, 0:512], ps[5][0:16, 0:512], 1e-24), "dve")

        @block.scalar
        def _(sc):
            kb.wait(sc, "dve")
            kb.inc(sc.activation(out=bbc[0:16, 0:512], in_=abc[0:16, 0:512],
                                 func=AF.Ln), "act")
            kb.wait(sc, "act")
            kb.inc(sc.activation(out=b[0][0:16, 0:512], in_=bbc[0:16, 0:512], func=AF.Exp,
                                 scale=-0.5), "act")

        for c in range(8):
            @block.tensor
            def _(t, c=c):
                kb.wait(t, "act")
                if c > 0:
                    kb.wait(t, "dve")
                kb.inc(t.matmul(ps[5][:, 0:512], lhsT=aw[f"hsel{c}"][:, :],
                                rhs=b[0][0:16, 0:512], start=True, stop=True), "pe")

            @block.vector
            def _(v, c=c):
                kb.wait(v, "pe")
                kb.inc(v.tensor_scalar_mul(CH(Ee, c % 4), CH(A, c), vcol(aw, "k_k", c)), "dve")
                kb.wait(v, "dve")
                kb.inc(v.tensor_tensor(CH(b[2], c), CH(Ee, c % 4), ps[5][:, 0:512],
                                       op=ALU.mult), "dve")

        if stop_after == "t0":
            return
        for st in range(4):
            for nh2 in range(2):
                @block.tensor
                def _(t, st=st, nh2=nh2):
                    kb.wait(t, "dve")
                    if st == 0 and nh2 == 0:
                        kb.wait(t, "act")
                    for c in range(8):
                        ins_ = t.matmul(ps[4][:, 0:512],
                                        lhsT=b[3][:, c * W + st * 128:c * W + (st + 1) * 128],
                                        rhs=aw["wv16"][:, c * 1024 + nh2 * 512:c * 1024 + (nh2 + 1) * 512],
                                        start=(c == 0), stop=(c == 7))
                    kb.inc(ins_, "pe")

                @block.vector
                def _(v, st=st, nh2=nh2):
                    kb.wait(v, "pe")
                    kb.inc(v.tensor_copy(
                        b[4][:, st * 1024 + nh2 * 512:st * 1024 + (nh2 + 1) * 512],
                        ps[4][:, 0:512]), "dve")

        @block.sync
        def _(s, ti=ti):
            kb.wait(s, "dve")
            for st in range(4):
                kb.inc(s.dma_start(out=dram["v_d"][ti * 512 + st * 128: ti * 512 + (st + 1) * 128, :],
                                   in_=b[4][:, st * 1024:(st + 1) * 1024]), "qs")

        if stop_after == "vproj":
            return
        @block.tensor
        def _(t):
            kb.wait(t, "dve")
            for c in range(8):
                ins_ = t.matmul(ps[5][:, 0:512], lhsT=aw["gla16"][:, c * 128:(c + 1) * 128],
                                rhs=CH(b[5], c), start=(c == 0), stop=(c == 7))
            kb.inc(ins_, "pe")

        @block.scalar
        def _(sc):
            kb.wait(sc, "pe")
            kb.inc(sc.activation(out=b[0][:, 0:512], in_=ps[5][:, 0:512], func=AF.Sigmoid), "act")

        for st in range(4):
            for nh2 in range(2):
                @block.tensor
                def _(t, st=st, nh2=nh2):
                    kb.wait(t, "act")
                    if st > 0 or nh2 > 0:
                        kb.wait(t, "dve")
                    t.matmul(ps[4][:, 0:512],
                             lhsT=b[0][:, st * 128:(st + 1) * 128],
                             rhs=aw["glb16"][:, nh2 * 512:(nh2 + 1) * 512],
                             start=True, stop=False)
                    ins_ = t.matmul(ps[4][:, 0:512],
                                    lhsT=cst["onescol16"][0:1, 0:128],
                                    rhs=aw["gbias_row16"][0:1, nh2 * 512:(nh2 + 1) * 512],
                                    start=False, stop=True)
                    kb.inc(ins_, "pe")

                @block.vector
                def _(v, st=st, nh2=nh2):
                    kb.wait(v, "pe")
                    kb.inc(v.tensor_copy(
                        b[3][:, st * 1024 + nh2 * 512:st * 1024 + (nh2 + 1) * 512],
                        ps[4][:, 0:512]), "dve")

        @block.sync
        def _(s, ti=ti):
            kb.wait(s, "dve")
            for st in range(4):
                kb.inc(s.dma_start(out=dram["g_d"][ti * 512 + st * 128: ti * 512 + (st + 1) * 128, :],
                                   in_=b[3][:, st * 1024:(st + 1) * 1024]), "qs")

        if stop_after == "ggate":
            return
        def ef_arg(v, c, mode):
            cv = cval[:, c * 8:(c + 1) * 8]
            g_ = Cc[:, c * W:(c + 1) * W].rearrange("p (ch t) -> p ch t", t=64)
            e_ = Bb[:, c * W:(c + 1) * W].rearrange("p (ch t) -> p ch t", t=64)
            w_ = A
            cbc = cv.unsqueeze(2).to_broadcast([128, 8, 64])
            if mode == "b":
                kb.inc(v.tensor_tensor(e_, cbc, g_, op=ALU.subtract), "dve")
            else:
                kb.inc(v.tensor_tensor(e_, g_, cbc, op=ALU.subtract), "dve")

        for c in range(8):
            @block.scalar
            def _(sc, c=c):
                kb.wait(sc, "dve")
                if c == 0:
                    kb.wait(sc, "qs")
                kb.inc(sc.activation(out=CH(Bb, c), in_=CH(Bb, c), func=AF.Exp), "act")

            @block.vector
            def _(v, c=c):
                kb.wait(v, "act")
                kb.inc(v.scalar_tensor_tensor(out=CH(b[0], c), in0=CH(b[2], c), scalar=-1.0,
                                              in1=CH(Bb, c), op0=ALU.mult, op1=ALU.mult), "dve")

        @block.sync
        def _(s, ti=ti):
            kb.wait(s, "dve")
            for c in range(8):
                kb.inc(s.dma_start(out=dram["at"][:, c * T + ti * W:c * T + (ti + 1) * W],
                                   in_=CH(b[0], c)), "qs")

        if stop_after == "efa":
            return
        for c in range(8):
            @block.vector
            def _(v, c=c):
                kb.wait(v, "dve")
                kb.wait(v, "qs")
                ef_arg(v, c, "b")

            @block.scalar
            def _(sc, c=c):
                kb.wait(sc, "dve")
                kb.inc(sc.activation(out=CH(Bb, c), in_=CH(Bb, c), func=AF.Exp), "act")

            @block.vector
            def _(v, c=c):
                kb.wait(v, "act")
                kb.wait(v, "dve")
                kb.inc(v.tensor_tensor(CH(Ee, c % 4), CH(b[2], c), CH(b[1], c), op=ALU.mult), "dve")
                kb.wait(v, "dve")
                kb.inc(v.tensor_tensor(CH(b[0], c), CH(Ee, c % 4), CH(Bb, c), op=ALU.mult), "dve")
                kb.wait(v, "dve")
                kb.inc(v.tensor_scalar(CH(Ee, c % 4), CH(b[1], c), vcol(aw, "k_a", c),
                                       vcol(aw, "one_m_ka", c), op0=ALU.mult, op1=ALU.add), "dve")
                kb.wait(v, "dve")
                kb.inc(v.tensor_tensor(CH(A, c), CH(A, c), CH(Ee, c % 4), op=ALU.mult), "dve")
                kb.wait(v, "dve")
                kb.inc(v.tensor_tensor(CH(b[4], c), CH(A, c), CH(Bb, c), op=ALU.mult), "dve")

        @block.sync
        def _(s, ti=ti):
            kb.wait(s, "dve")
            for c in range(8):
                kb.inc(s.dma_start(out=dram["bt"][:, c * T + ti * W:c * T + (ti + 1) * W],
                                   in_=CH(b[0], c)), "qs")
                kb.inc(s.dma_start(out=dram["kt"][:, c * T + ti * W:c * T + (ti + 1) * W],
                                   in_=CH(b[4], c)), "qs")

        if stop_after == "efb":
            return
        for c in range(8):
            @block.vector
            def _(v, c=c):
                kb.wait(v, "dve")
                kb.wait(v, "qs")
                ef_arg(v, c, "r")

            @block.scalar
            def _(sc, c=c):
                kb.wait(sc, "dve")
                kb.inc(sc.activation(out=CH(Bb, c), in_=CH(Bb, c), func=AF.Exp), "act")

            @block.vector
            def _(v, c=c):
                kb.wait(v, "act")
                kb.wait(v, "dve")
                kb.inc(v.tensor_tensor(CH(b[0], c), CH(b[5], c), CH(Bb, c), op=ALU.mult), "dve")
                kb.wait(v, "dve")
                kb.inc(v.tensor_tensor(CH(Ee, c % 4), CH(b[5], c), CH(A, c), op=ALU.mult), "dve")
                kb.wait(v, "dve")
                kb.inc(v.tensor_scalar_mul(CH(b[4], c), CH(Ee, c % 4), vcol(aw, "r_k", c)), "dve")

        @block.sync
        def _(s, ti=ti):
            kb.wait(s, "dve")
            for c in range(8):
                kb.inc(s.dma_start(out=dram["rt"][:, c * T + ti * W:c * T + (ti + 1) * W],
                                   in_=CH(b[0], c)), "qs")

        @block.tensor
        def _(t, ti=ti):
            kb.wait(t, "dve")
            for c in range(8):
                ins_ = t.matmul(ps[5][0:16, 0:512], lhsT=aw[f"bdsel{c}"][:, :],
                                rhs=CH(b[4], c), start=(c == 0), stop=(c == 7))
            kb.inc(ins_, "pe")

        @block.vector
        def _(v, ti=ti):
            kb.wait(v, "pe")
            kb.inc(v.tensor_copy(bonus[:, ti * W:(ti + 1) * W], ps[5][0:16, 0:512]), "dve")


def emit_attn_setup(kb, block, nc, a_in, aw, aw16_dram, ps, cst, rbbuf):
    @block.gpsimd
    def _(g):
        for nm, src, csz in (("wr16", a_in["W_r"], 1024), ("wk16", a_in["W_k"], 1024),
                             ("wv16", a_in["W_v"], 1024), ("wo16", a_in["W_o"], 1024),
                             ("wla16", a_in["w_lora_a"], 64), ("ala16", a_in["a_lora_a"], 64),
                             ("gla16", a_in["g_lora_a"], 128)):
            kb.inc(g.dma_start(out=aw16_dram[nm],
                               in_=src.rearrange("(c p) n -> p c n", p=128)), "qp")
        for nm, src in (("wlb16", a_in["w_lora_b"]), ("alb16", a_in["a_lora_b"]),
                        ("glb16", a_in["g_lora_b"])):
            kb.inc(g.dma_start(out=aw16_dram[nm], in_=src), "qp")
        kb.inc(g.dma_start(out=aw16_dram["gbias"], in_=a_in["g_lora_bias"].unsqueeze(0)), "qp")

    @block.sync
    def _(s):
        kb.wait(s, "qp")
        for nm in ("wr16", "wk16", "wv16", "wla16", "ala16", "gla16"):
            kb.inc(s.dma_start(out=aw[nm][:, :],
                               in_=aw16_dram[nm].rearrange("p a b -> p (a b)")), "qs")
        for nm in ("wlb16", "alb16", "glb16"):
            kb.inc(s.dma_start(out=aw[nm][:, :], in_=aw16_dram[nm]), "qs")
        kb.inc(s.dma_start(out=aw["gbias_row16"][:, :], in_=aw16_dram["gbias"]), "qs")
        kb.wait(s, "qs")
        for nm in ("wr16", "wk16", "wv16", "wla16", "ala16", "gla16"):
            kb.inc(s.dma_start(out=aw[nm][:, :],
                               in_=aw16_dram[nm].rearrange("p a b -> p (a b)")), "qs")
        for nm in ("wlb16", "alb16", "glb16"):
            kb.inc(s.dma_start(out=aw[nm][:, :], in_=aw16_dram[nm]), "qs")
        kb.inc(s.dma_start(out=aw["gbias_row16"][:, :], in_=aw16_dram["gbias"]), "qs")
        vl = [("x_r", a_in["x_r"]), ("x_w", a_in["x_w"]), ("x_k", a_in["x_k"]),
              ("x_v", a_in["x_v"]), ("x_a", a_in["x_a"]), ("k_k", a_in["k_k"]),
              ("k_a", a_in["k_a"]), ("r_k", a_in["r_k"]),
              ("wbias", a_in["w_lora_bias"]), ("abias", a_in["a_lora_bias"])]
        for nm, src in vl:
            vi = VEC_IDX[nm]
            kb.inc(s.dma_start(out=aw["vstage"][vi * 8:(vi + 1) * 8, :],
                               in_=src.rearrange("(c p) -> c p", p=128) if len(src.shape) == 1
                               else src.rearrange("h d -> (h d)").rearrange("(c p) -> c p", p=128)), "qs")
        kb.inc(s.dma_start(out=aw["gnwrow"][0:1, :], in_=a_in["gn_weight"].unsqueeze(0)), "qs")
        kb.inc(s.dma_start(out=aw["gnbrow"][0:1, :], in_=a_in["gn_bias"].unsqueeze(0)), "qs")

    @block.tensor
    def _(t):
        kb.wait(t, "qs")
        kb.inc(t.transpose(ps[0][:, 0:128], in_=aw["vstage"][:, :],
                           identity=cst["ident"][:, :]), "pe")

    @block.vector
    def _(v):
        kb.wait(v, "pe")
        kb.inc(v.tensor_copy(aw["vecs"][:, 0:96], ps[0][:, 0:96]), "dve")
        kb.wait(v, "dve")
        kb.inc(v.tensor_scalar_mul(aw["vecs"][:, VEC_IDX["neg_wbias"] * 8:VEC_IDX["neg_wbias"] * 8 + 8],
                                   aw["vecs"][:, VEC_IDX["wbias"] * 8:VEC_IDX["wbias"] * 8 + 8], -1.0), "dve")
        kb.wait(v, "dve")
        kb.inc(v.tensor_scalar(aw["vecs"][:, VEC_IDX["one_m_ka"] * 8:VEC_IDX["one_m_ka"] * 8 + 8],
                               aw["vecs"][:, VEC_IDX["k_a"] * 8:VEC_IDX["k_a"] * 8 + 8],
                               -1.0, 1.0, op0=ALU.mult, op1=ALU.add), "dve")
        kb.wait(v, "dve")
        kb.inc(v.memset(cst["junk"][:, :], 0.0), "dve")


def emit_gnbc(kb, block, aw, ps, cst):
    for half in range(2):
        @block.tensor
        def _(t, half=half):
            kb.wait(t, "dve")
            kb.wait(t, "act")
            t.matmul(ps[1][:, :], lhsT=cst["onescol"][:, :],
                     rhs=aw["gnwrow"][0:1, half * 512:(half + 1) * 512], start=True, stop=True)
            kb.inc(t.matmul(ps[2][:, :], lhsT=cst["onescol"][:, :],
                            rhs=aw["gnbrow"][0:1, half * 512:(half + 1) * 512],
                            start=True, stop=True), "pe")

        @block.vector
        def _(v, half=half):
            kb.wait(v, "pe")
            v.tensor_copy(aw["gnwbc"][:, half * 512:(half + 1) * 512], ps[1][:, :])
            kb.inc(v.tensor_copy(aw["gnbbc"][:, half * 512:(half + 1) * 512], ps[2][:, :]), "dve")


def make_attn_epilogue(kb, block, aw, ps, ps16, cst, sb2, dram, x1_img, x2_img):
    o_tm, v_s, bonus = sb2["o_tm"], sb2["v_s"], sb2["bonus"]
    ep_g, ep_st, ep_st2 = sb2["ep_g"], sb2["ep_st"], sb2["ep_st2"]
    ep_sq, ep_bt = sb2["ep_sq"], sb2["ep_bt"]
    ep_ogf, ep_x1 = sb2["ep_ogf"], sb2["ep_x1"]
    SEG = 512

    def epi(seg):
        tok0 = seg * SEG

        for st in range(4):
            @block.sync
            def _(s, st=st):
                kb.wait(s, "dve")
                kb.inc(s.dma_start(out=ep_g[:, 0:1024],
                                   in_=dram["g_d"][tok0 + st * 128: tok0 + (st + 1) * 128, :]),
                       "qs")
            o = o_tm[:, st * C:(st + 1) * C]
            ov = o.rearrange("p (h i) -> p h i", i=64)

            @block.tensor
            def _(t, st=st):
                kb.wait(t, "dve")
                kb.inc(t.transpose(ps16[:, 0:16],
                                   in_=bonus[0:16, tok0 + st * 128: tok0 + (st + 1) * 128],
                                   identity=cst["ident16"][0:16, 0:16]), "pe")

            @block.vector
            def _(v, st=st):
                kb.wait(v, "pe")
                kb.inc(v.tensor_copy(ep_bt[:, st * 16:(st + 1) * 16], ps16[:, 0:16]), "dve")

            @block.scalar
            def _(sc, o=o):
                kb.wait(sc, "dve")
                kb.inc(sc.activation(out=ep_sq[:, :], in_=o, func=AF.Square), "act")

            @block.vector
            def _(v, o=o, ov=ov):
                kb.wait(v, "dve")
                kb.inc(v.reduce_sum(ep_st[:, :], ov, axis=AX.X), "dve")
                kb.wait(v, "act")
                kb.inc(v.reduce_sum(ep_st2[:, :],
                                    ep_sq[:, :].rearrange("p (h i) -> p h i", i=64),
                                    axis=AX.X), "dve")
                kb.wait(v, "dve")
                kb.inc(v.tensor_scalar_mul(ep_st[:, :], ep_st[:, :], 1.0 / 64), "dve")
                kb.wait(v, "dve")
                kb.inc(v.tensor_scalar_mul(ep_st2[:, :], ep_st2[:, :], 1.0 / 64), "dve")
                kb.wait(v, "dve")
                kb.inc(v.tensor_tensor(ep_sq[:, 0:16], ep_st[:, :], ep_st[:, :],
                                       op=ALU.mult), "dve")
                kb.wait(v, "dve")
                kb.inc(v.tensor_tensor(ep_st2[:, :], ep_st2[:, :], ep_sq[:, 0:16],
                                       op=ALU.subtract), "dve")

            @block.scalar
            def _(sc):
                kb.wait(sc, "dve")
                kb.inc(sc.activation(out=ep_st2[:, :], in_=ep_st2[:, :], func=AF.Ln,
                                     bias=cst["eps1"][:, 0:1]), "act")
                kb.wait(sc, "act")
                kb.inc(sc.activation(out=ep_st2[:, :], in_=ep_st2[:, :], func=AF.Exp,
                                     scale=-0.5), "act")

            @block.vector
            def _(v, st=st, o=o, ov=ov):
                kb.wait(v, "act")
                kb.wait(v, "qs")
                mbc = ep_st[:, :].unsqueeze(2).to_broadcast([128, 16, 64])
                rbc = ep_st2[:, :].unsqueeze(2).to_broadcast([128, 16, 64])
                bbc_ = ep_bt[:, st * 16:(st + 1) * 16].unsqueeze(2).to_broadcast([128, 16, 64])
                vv_ = v_s[:, st * C:(st + 1) * C].rearrange("p (h i) -> p h i", i=64)
                kb.inc(v.tensor_tensor(ov, ov, mbc, op=ALU.subtract), "dve")
                kb.wait(v, "dve")
                kb.inc(v.tensor_tensor(ov, ov, rbc, op=ALU.mult), "dve")
                kb.wait(v, "dve")
                kb.inc(v.tensor_tensor(o, o, aw["gnwbc"][:, :], op=ALU.mult), "dve")
                kb.wait(v, "dve")
                kb.inc(v.tensor_tensor(o, o, aw["gnbbc"][:, :], op=ALU.add), "dve")
                kb.wait(v, "dve")
                kb.inc(v.tensor_tensor(ep_sq[:, :].rearrange("p (h i) -> p h i", i=64),
                                       vv_, bbc_, op=ALU.mult), "dve")
                kb.wait(v, "dve")
                kb.inc(v.tensor_tensor(o, o, ep_sq[:, :], op=ALU.add), "dve")
                kb.wait(v, "dve")
                kb.inc(v.tensor_tensor(o, o, ep_g[:, 0:1024], op=ALU.mult), "dve")

            for half in range(2):
                @block.tensor
                def _(t, st=st, half=half):
                    kb.wait(t, "dve")
                    for cc in range(4):
                        c = half * 4 + cc
                        ins_ = t.transpose(ps[half][:, cc * 128:(cc + 1) * 128],
                                           in_=o_tm[:, st * C + c * 128: st * C + (c + 1) * 128],
                                           identity=cst["ident"][:, :])
                    kb.inc(ins_, "pe")

                @block.vector
                def _(v, st=st, half=half):
                    kb.wait(v, "pe")
                    for cc in range(4):
                        c = half * 4 + cc
                        ins_ = v.tensor_copy(
                            ep_ogf[:, c * 512 + st * 128: c * 512 + (st + 1) * 128],
                            ps[half][:, cc * 128:(cc + 1) * 128])
                    kb.inc(ins_, "dve")

        for oc in range(8):
            @block.sync
            def _(s, oc=oc):
                kb.wait(s, "dve")
                kb.inc(s.dma_start(out=ep_x1[:, (oc % 4) * 512:(oc % 4 + 1) * 512],
                                   in_=x1_img[:, oc * T + tok0: oc * T + tok0 + SEG]), "qs")

            @block.tensor
            def _(t, oc=oc):
                kb.wait(t, "dve")
                bank = ps[2 + oc % 2]
                for c in range(8):
                    ins_ = t.matmul(bank[:, 0:512],
                                    lhsT=aw["wo16"][:, c * 1024 + oc * 128: c * 1024 + (oc + 1) * 128],
                                    rhs=ep_ogf[:, c * 512:(c + 1) * 512],
                                    start=(c == 0), stop=(c == 7))
                kb.inc(ins_, "pe")

            @block.vector
            def _(v, oc=oc):
                kb.wait(v, "pe")
                kb.wait(v, "qs")
                sl = ep_x1[:, (oc % 4) * 512:(oc % 4 + 1) * 512]
                kb.inc(v.tensor_tensor(sl, ps[2 + oc % 2][:, 0:512], sl, op=ALU.add), "dve")

            @block.sync
            def _(s, oc=oc):
                kb.wait(s, "dve")
                kb.inc(s.dma_start(out=x2_img[:, oc * T + tok0: oc * T + tok0 + SEG],
                                   in_=ep_x1[:, (oc % 4) * 512:(oc % 4 + 1) * 512]), "qs")

    return epi


def emit_router(kb, block, x_src, sc_out, rw, ps, cst, bufs):
    rx, rsq, rst, rl = bufs["rx"], bufs["rsq"], bufs["rst"], bufs["rl"]
    for ti in range(8):
        @block.sync
        def _(s, ti=ti):
            kb.wait(s, "pe")
            kb.wait(s, "act")
            for c in range(8):
                kb.inc(s.dma_start(out=rx[:, c * 512:(c + 1) * 512],
                                   in_=x_src[:, c * T + ti * 512:c * T + (ti + 1) * 512]),
                       "qs")

        for st in range(4):
            @block.tensor
            def _(t, ti=ti, st=st):
                kb.wait(t, "qs")
                kb.wait(t, "dve")
                for c in range(8):
                    t.matmul(ps[0][:, 0:64],
                             lhsT=rx[:, c * 512 + st * 128:c * 512 + (st + 1) * 128],
                             rhs=rw["wr_sb"][:, c * 64:(c + 1) * 64],
                             start=(c == 0), stop=(c == 7))
                for c in range(8):
                    ins_ = t.matmul(ps[1][:, 0:1],
                                    lhsT=rx[:, c * 512 + st * 128:c * 512 + (st + 1) * 128],
                                    rhs=cst["ones1f"][:, :],
                                    start=(c == 0), stop=(c == 7))
                kb.inc(ins_, "pe")

            @block.scalar
            def _(sc, ti=ti, st=st):
                kb.wait(sc, "pe")
                for c in range(8):
                    kb.inc(sc.activation(
                        out=rsq[:, c * 128:(c + 1) * 128],
                        in_=rx[:, c * 512 + st * 128:c * 512 + (st + 1) * 128],
                        func=AF.Square), "act")

            @block.tensor
            def _(t, st=st):
                kb.wait(t, "act")
                for c in range(8):
                    ins_ = t.matmul(ps[1][:, 1:2],
                                    lhsT=rsq[:, c * 128:(c + 1) * 128],
                                    rhs=cst["ones1f"][:, :],
                                    start=(c == 0), stop=(c == 7))
                kb.inc(ins_, "pe")

            @block.vector
            def _(v, st=st):
                kb.wait(v, "pe")
                kb.inc(v.tensor_copy(rl[:, :], ps[0][:, 0:64]), "dve")
                kb.inc(v.tensor_scalar_mul(rst[:, 0:1], ps[1][:, 0:1], 1.0 / C), "dve")
                kb.inc(v.tensor_scalar_mul(rst[:, 5:6], ps[1][:, 1:2], 1.0 / C), "dve")
                kb.wait(v, "dve")
                kb.inc(v.tensor_tensor(rst[:, 2:3], rst[:, 0:1], rst[:, 0:1],
                                       op=ALU.mult), "dve")
                kb.wait(v, "dve")
                kb.inc(v.tensor_tensor(rst[:, 2:3], rst[:, 5:6], rst[:, 2:3],
                                       op=ALU.subtract), "dve")

            @block.scalar
            def _(sc):
                kb.wait(sc, "dve")
                kb.inc(sc.activation(out=rst[:, 1:2], in_=rst[:, 2:3], func=AF.Ln,
                                     bias=cst["eps1"][:, 0:1]), "act")
                kb.wait(sc, "act")
                kb.inc(sc.activation(out=rst[:, 1:2], in_=rst[:, 1:2], func=AF.Exp,
                                     scale=-0.5), "act")

            @block.vector
            def _(v):
                kb.wait(v, "act")
                kb.inc(v.tensor_scalar_mul(rl[:, :], rl[:, :], rst[:, 1:2]), "dve")
                kb.wait(v, "dve")
                kb.inc(v.scalar_tensor_tensor(out=rst[:, 2:3], in0=rst[:, 0:1],
                                              scalar=-1.0, in1=rst[:, 1:2],
                                              op0=ALU.mult, op1=ALU.mult), "dve")
                kb.wait(v, "dve")
                kb.inc(v.scalar_tensor_tensor(out=rl[:, :], in0=rw["wrcsbc"][:, :],
                                              scalar=rst[:, 2:3], in1=rl[:, :],
                                              op0=ALU.mult, op1=ALU.add), "dve")
                kb.wait(v, "dve")
                kb.inc(v.tensor_tensor(rl[:, :], rl[:, :], rw["brbc"][:, :],
                                       op=ALU.add), "dve")
                kb.wait(v, "dve")
                kb.inc(v.reduce_max(rst[:, 3:4], rl[:, :], axis=AX.X), "dve")
                kb.wait(v, "dve")
                kb.inc(v.tensor_scalar_mul(rst[:, 4:5], rst[:, 3:4], -1.0), "dve")

            @block.scalar
            def _(sc):
                kb.wait(sc, "dve")
                kb.inc(sc.activation(out=rl[:, :], in_=rl[:, :], func=AF.Exp,
                                     bias=rst[:, 4:5]), "act")

            @block.vector
            def _(v):
                kb.wait(v, "act")
                kb.inc(v.reduce_sum(rst[:, 6:7], rl[:, :], axis=AX.X), "dve")
                kb.wait(v, "dve")
                kb.inc(v.reciprocal(rst[:, 6:7], rst[:, 6:7]), "dve")
                kb.wait(v, "dve")
                kb.inc(v.tensor_scalar_mul(rl[:, :], rl[:, :], rst[:, 6:7]), "dve")

            @block.sync
            def _(s, ti=ti, st=st):
                kb.wait(s, "dve")
                kb.inc(s.dma_start(
                    out=sc_out[ti * 512 + st * 128: ti * 512 + (st + 1) * 128, :],
                    in_=rl[:, :]), "qs")


def emit_router_weights(kb, block, Wr_in, br_in, rw, ps, cst):
    @block.sync
    def _(s):
        kb.inc(s.dma_start(out=rw["wr_sb"][:, :].rearrange("p (c e) -> p c e", e=64),
                           in_=Wr_in.rearrange("(c p) e -> p c e", p=128)), "qs")
        kb.inc(s.dma_start(out=rw["brrow"][0:1, :], in_=br_in.unsqueeze(0)), "qs")

    @block.tensor
    def _(t):
        kb.wait(t, "qs")
        for c in range(8):
            ins_ = t.matmul(ps[0][0:1, 0:64], lhsT=cst["ones1f"][:, :],
                            rhs=rw["wr_sb"][:, c * 64:(c + 1) * 64],
                            start=(c == 0), stop=(c == 7))
        kb.inc(ins_, "pe")

    @block.vector
    def _(v):
        kb.wait(v, "pe")
        kb.inc(v.tensor_copy(rw["wcrow"][0:1, :], ps[0][0:1, 0:64]), "dve")

    @block.tensor
    def _(t):
        kb.wait(t, "dve")
        t.matmul(ps[0][:, 0:64], lhsT=cst["onescol"][:, :], rhs=rw["wcrow"][0:1, :],
                 start=True, stop=True)
        kb.inc(t.matmul(ps[1][:, 0:64], lhsT=cst["onescol"][:, :], rhs=rw["brrow"][0:1, :],
                        start=True, stop=True), "pe")

    @block.vector
    def _(v):
        kb.wait(v, "pe")
        v.tensor_copy(rw["wrcsbc"][:, :], ps[0][:, 0:64])
        kb.inc(v.tensor_copy(rw["brbc"][:, :], ps[1][:, 0:64]), "dve")


def build_lA():
    nc = bass.Bass("TRN2", target_bir_lowering=False, debug=False)
    x_img = nc.dram_tensor("x_img", [128, 8 * T], F32, kind="ExternalInput").ap()
    Wr_in = nc.dram_tensor("Wr", [C, E], F32, kind="ExternalInput").ap()
    br_in = nc.dram_tensor("br", [E], F32, kind="ExternalInput").ap()
    sc_out = nc.dram_tensor("sc_own", [T, E], F32, kind="ExternalOutput").ap()
    with ExitStack() as stack:
        kb = KB(nc, stack)
        cst_dram = {
            "ones1f": kb.const("c_ones1f", np.ones((128, 1), np.float32)),
            "onescol": kb.const("c_onescol", np.ones((1, 128), np.float32)),
            "eps1": kb.const("c_eps1", np.full((128, 1), 1e-5, np.float32)),
        }
        cst = {nm: sbuf(kb, nm, list(v.shape)) for nm, v in
               (("ones1f", np.ones((128, 1))), ("onescol", np.ones((1, 128))),
                ("eps1", np.ones((128, 1))))}
        ps = [mkpsum(kb, f"ps{i}") for i in range(2)]
        rw = {
            "wr_sb": sbuf(kb, "wr_sb", [128, 512]),
            "brrow": sbuf(kb, "brrow", [1, 64]),
            "wcrow": sbuf(kb, "wcrow", [1, 64]),
            "brbc": sbuf(kb, "brbc", [128, 64]),
            "wrcsbc": sbuf(kb, "wrcsbc", [128, 64]),
        }
        bufs = {
            "rx": sbuf(kb, "rx", [128, 4096]),
            "rsq": sbuf(kb, "rsq", [128, 1024]),
            "rst": sbuf(kb, "rst", [128, 8]),
            "rl": sbuf(kb, "rl", [128, 64]),
        }
        block = stack.enter_context(nc.Block())

        @block.sync
        def _(s):
            for nm in cst_dram:
                kb.inc(s.dma_start(out=cst[nm][:, :], in_=cst_dram[nm]), "qs")

        emit_router_weights(kb, block, Wr_in, br_in, rw, ps, cst)
        emit_router(kb, block, x_img, sc_out, rw, ps, cst, bufs)

        @block.sync
        def _(s):
            kb.wait_all(s)
    return nc


def build_lB():
    nc = bass.Bass("TRN2", target_bir_lowering=False, debug=False)
    x_img = nc.dram_tensor("x_img", [128, 8 * T], F32, kind="ExternalInput").ap()
    scores_em = nc.dram_tensor("scores_em", [E, NTOK], F32, kind="ExternalInput").ap()
    scores_own = nc.dram_tensor("scores_own", [T, E], F32, kind="ExternalInput").ap()
    w_in = {nm: nc.dram_tensor(nm, shp, F32, kind="ExternalInput").ap() for nm, shp in
            (("Wup", [E, C, HH]), ("Wgate", [E, C, HH]), ("Wdown", [E, HH, C]),
             ("bup", [E, HH]), ("bgate", [E, HH]), ("bdown", [E, C]))}
    Wr_in = nc.dram_tensor("Wr", [C, E], F32, kind="ExternalInput").ap()
    br_in = nc.dram_tensor("br", [E], F32, kind="ExternalInput").ap()
    a_names = ["x_r", "x_w", "x_k", "x_v", "x_a", "k_k", "k_a", "r_k", "W_r", "W_k",
               "W_v", "W_o", "w_lora_a", "w_lora_b", "w_lora_bias", "a_lora_a",
               "a_lora_b", "a_lora_bias", "g_lora_a", "g_lora_b", "g_lora_bias",
               "gn_weight", "gn_bias"]
    a_shapes = {"x_r": [C], "x_w": [C], "x_k": [C], "x_v": [C], "x_a": [C], "k_k": [C],
                "k_a": [C], "r_k": [NH, HD], "W_r": [C, C], "W_k": [C, C], "W_v": [C, C],
                "W_o": [C, C], "w_lora_a": [C, 64], "w_lora_b": [64, C],
                "w_lora_bias": [C], "a_lora_a": [C, 64], "a_lora_b": [64, C],
                "a_lora_bias": [C], "g_lora_a": [C, 128], "g_lora_b": [128, C],
                "g_lora_bias": [C], "gn_weight": [C], "gn_bias": [C]}
    a_in = {nm: nc.dram_tensor(nm, a_shapes[nm], F32, kind="ExternalInput").ap()
            for nm in a_names}
    x2_img = nc.dram_tensor("x2_img", [128, 8 * T], F32, kind="ExternalOutput").ap()
    sc3_own = nc.dram_tensor("sc3_own", [T, E], F32, kind="ExternalOutput").ap()

    x1_img = nc.dram_tensor("d_x1", [128, 8 * T], F32).ap()
    w16 = {
        "Wup": [nc.dram_tensor(f"w16up{e}", [128, 8, HH], BF16).ap() for e in range(E)],
        "Wgate": [nc.dram_tensor(f"w16ga{e}", [128, 8, HH], BF16).ap() for e in range(E)],
        "Wdown": [nc.dram_tensor(f"w16dn{e}", [128, 2, C], BF16).ap() for e in range(E)],
    }
    aw16_dram = {
        "wr16": nc.dram_tensor("d_wr16", [128, 8, 1024], BF16).ap(),
        "wk16": nc.dram_tensor("d_wk16", [128, 8, 1024], BF16).ap(),
        "wv16": nc.dram_tensor("d_wv16", [128, 8, 1024], BF16).ap(),
        "wo16": nc.dram_tensor("d_wo16", [128, 8, 1024], BF16).ap(),
        "wla16": nc.dram_tensor("d_wla16", [128, 8, 64], BF16).ap(),
        "ala16": nc.dram_tensor("d_ala16", [128, 8, 64], BF16).ap(),
        "gla16": nc.dram_tensor("d_gla16", [128, 8, 128], BF16).ap(),
        "wlb16": nc.dram_tensor("d_wlb16", [64, 1024], BF16).ap(),
        "alb16": nc.dram_tensor("d_alb16", [64, 1024], BF16).ap(),
        "glb16": nc.dram_tensor("d_glb16", [128, 1024], BF16).ap(),
        "gbias": nc.dram_tensor("d_gbias", [1, 1024], BF16).ap(),
    }
    adram = {
        "at": nc.dram_tensor("d_at", [128, 8 * T], BF16).ap(),
        "bt": nc.dram_tensor("d_bt", [128, 8 * T], BF16).ap(),
        "kt": nc.dram_tensor("d_kt", [128, 8 * T], BF16).ap(),
        "rt": nc.dram_tensor("d_rt", [128, 8 * T], BF16).ap(),
        "v_d": nc.dram_tensor("d_v", [T, C], BF16).ap(),
        "g_d": nc.dram_tensor("d_g", [T, C], BF16).ap(),
    }
    dram = {
        "theta": nc.dram_tensor("d_theta", [1, 1], F32).ap(),
        "sigma": nc.dram_tensor("d_sigma", [E, 1], F32).ap(),
        "arow": nc.dram_tensor("d_arow", [1, 1024], F32).ap(),
        "brow": nc.dram_tensor("d_brow", [1, 1024], F32).ap(),
        "fs": nc.dram_tensor("d_fs", [E, T], BF16).ap(),
    }

    with ExitStack() as stack:
        kb = KB(nc, stack)
        cstn = {
            "ident": np.eye(128, dtype=np.float32),
            "oh31": (np.arange(128) == 31).astype(np.float32).reshape(128, 1),
            "onescol": np.ones((1, 128), np.float32),
            "eps1": np.full((128, 1), 1e-5, np.float32),
            "ones64": np.ones((128, 64), np.float32),
            "ones1f": np.ones((128, 1), np.float32),
        }
        for m in range(8, 15):
            cstn[f"sgn{m}"] = np.where(((np.arange(128) >> (m - 8)) & 1) == 0,
                                       1.0, -1.0).astype(np.float32).reshape(128, 1)
        BFD = ml_dtypes.bfloat16
        cstn16 = {
            "ident16": np.eye(128, dtype=np.float32).astype(BFD),
            "onescol16": np.ones((1, 128), np.float32).astype(BFD),
            "ones1_16": np.ones((128, 1), np.float32).astype(BFD),
            "ones_row16": np.ones((1, 512), np.float32).astype(BFD),
        }
        tl = np.tril(np.ones((64, 64), np.float32), -1)
        tu_s = np.triu(np.ones((64, 64), np.float32), 1)
        tu_i = np.triu(np.ones((64, 64), np.float32), 0)

        def bd2(m):
            zz = np.zeros((128, 128), np.float32)
            zz[:64, :64] = m
            zz[64:, 64:] = m
            return zz
        cstn16["mask4"] = np.concatenate([bd2(tl), bd2(tu_s), bd2(tu_s), bd2(tu_i)],
                                         1).astype(BFD)
        cstn16["masku_inc"] = bd2(tu_i).astype(BFD)
        for par in range(2):
            pa = np.zeros((128, 128), np.float32)
            pb = np.zeros((128, 128), np.float32)
            qa_ = np.zeros((128, 128), np.float32)
            qb_ = np.zeros((128, 128), np.float32)
            for u in range(64):
                pa[par * 64 + u, u] = 1.0
                pb[par * 64 + u, 64 + u] = 1.0
                qa_[u, par * 64 + u] = 1.0
                qb_[64 + u, par * 64 + u] = 1.0
            cstn16[f"vperm{par}a"] = pa.astype(BFD)
            cstn16[f"vperm{par}b"] = pb.astype(BFD)
            cstn[f"operm{par}a"] = qa_
            cstn[f"operm{par}b"] = qb_
        awn = {}
        for c in range(8):
            m = np.zeros((128, 16), np.float32)
            m[0:64, 2 * c] = 1.0
            m[64:128, 2 * c + 1] = 1.0
            awn[f"bdsel{c}"] = m.astype(BFD)
            h = np.zeros((16, 128), np.float32)
            h[2 * c, 0:64] = 1.0
            h[2 * c + 1, 64:128] = 1.0
            awn[f"hsel{c}"] = h.astype(BFD)

        ATT_ONLY = {"ident16", "mask4", "masku_inc", "vperm0a", "vperm0b",
                    "vperm1a", "vperm1b", "operm0a", "operm0b", "operm1a",
                    "operm1b", "ones64"}
        cst_dram = {k: kb.const("c_" + k, v) for k, v in cstn.items()}
        cst_dram.update({k: kb.const("c_" + k, v) for k, v in cstn16.items()})
        cst_shapes = {k: (list(v.shape), F32) for k, v in cstn.items()}
        cst_shapes.update({k: (list(v.shape), BF16) for k, v in cstn16.items()})
        adr = {k: kb.const("c_" + k, v) for k, v in awn.items()}
        cst = {}
        for k in cst_shapes:
            if k not in ATT_ONLY:
                cst[k] = sbuf(kb, "s_" + k, *cst_shapes[k])
        cst["junk"] = sbuf(kb, "junk", [1, 1])

        ps = [mkpsum(kb, f"ps{i}") for i in range(7)]
        ps16 = mkpsum(kb, "ps16", (128, 1024), BF16)

        aw = {}
        for c in range(8):
            aw[f"bdsel{c}"] = sbuf(kb, f"bdsel{c}", [128, 16], BF16)
            aw[f"hsel{c}"] = sbuf(kb, f"hsel{c}", [16, 128], BF16)
        abufs = {}

        block = stack.enter_context(nc.Block())

        @block.sync
        def _(s):
            for k2 in cst_dram:
                if k2 not in ATT_ONLY:
                    kb.inc(s.dma_start(out=cst[k2][:, :], in_=cst_dram[k2]), "qs")
            for k2 in awn:
                kb.inc(s.dma_start(out=aw[k2][:, :], in_=adr[k2]), "qs")

        @block.gpsimd
        def _(g):
            kb.inc(g.memset(cst["junk"][:, :], 0.0), "pool")

        with ExitStack() as sf:
            def sbf(name, shape, dtype=F32):
                return sf.enter_context(nc.sbuf_tensor(name, shape, dtype))
            sig_fm = sbf("sig_fm", [E, 1])
            thbc = sbf("thbc", [128, 1])
            sigbc = sbf("sigbc", [128, E])
            fsfm = sbf("fsfm", [64, T], BF16)
            bias16 = {
                "bup": sbf("bupS", [128, 128]),
                "bgate": sbf("bgaS", [128, 128]),
                "bupR": sbf("bupR", [128, 128]),
                "bgaR": sbf("bgaR", [128, 128]),
            }
            bdown_sb = sbf("bdownS", [64, C], BF16)

            emit_weight_cast(kb, block, w_in, w16)

            @block.gpsimd
            def _(g):
                kb.inc(g.dma_start(out=bdown_sb[:, :], in_=w_in["bdown"]), "qp")

            @block.sync
            def _(s):
                kb.inc(s.dma_start(out=bias16["bupR"][:, :],
                                   in_=w_in["bup"].rearrange("e (r p) -> (e r) p", p=128)), "qs")
                kb.inc(s.dma_start(out=bias16["bgaR"][:, :],
                                   in_=w_in["bgate"].rearrange("e (r p) -> (e r) p", p=128)), "qs")

            @block.tensor
            def _(t):
                kb.wait(t, "qs")
                kb.wait(t, "dve")
                t.transpose(ps[0][:, 0:128], in_=bias16["bupR"][:, :],
                            identity=cst["ident"][:, :])
                kb.inc(t.transpose(ps[1][:, 0:128], in_=bias16["bgaR"][:, :],
                                   identity=cst["ident"][:, :]), "pe")

            @block.vector
            def _(v):
                kb.wait(v, "pe")
                v.tensor_copy(bias16["bup"][:, :], ps[0][:, 0:128])
                kb.inc(v.tensor_copy(bias16["bgate"][:, :], ps[1][:, 0:128]), "dve")

            with ExitStack() as sp:
                def sb2(name, shape, dtype=F32):
                    return sp.enter_context(nc.sbuf_tensor(name, shape, dtype))
                bufA = sb2("bufA", [128, E * 256])
                bufB = sb2("bufB", [128, E * 256])
                tbufA = sb2("tbufA", [128, 8 * 128])
                tbufB = sb2("tbufB", [128, 8 * 128])
                gs = sb2("gs", [128, E * 4])
                bm = sb2("bm", [128, E * 4])
                thA = sb2("thA", [128, 256])
                thB = sb2("thB", [128, 256])
                scr = {
                    "th11": sb2("th11", [1, 1]),
                    "mask": sb2("mask", [128, E * 4], mybir.dt.uint32),
                    "cand": sb2("cand", [128, E * 4]),
                    "emin": sb2("emin", [128, E]),
                    "eminT": sb2("eminT", [64, 128]),
                    "bigpos": sb2("bigpos", [128, E * 4]),
                    "sigrow": sb2("sigrow", [1, 64]),
                }
                fsb = {
                    "stile": sb2("stile", [128, 64]),
                    "fs0": sb2("fs0", [128, 64]),
                    "fsmask": sb2("fsmask", [128, 64]),
                    "fsden": sb2("fsden", [128, 1]),
                    "fsrb": sb2("fsrb", [64, T], BF16),
                }

                @block.sync
                def _(s):
                    kb.wait(s, "qs")
                    for gidx in range(8):
                        src = scores_em[gidx * 8:(gidx + 1) * 8, :].rearrange(
                            "e (p f) -> p e f", p=128)
                        kb.inc(s.dma_start(out=bufA[:, :].rearrange(
                            "p (e f) -> p e f", f=256)[:, gidx * 8:(gidx + 1) * 8, :],
                            in_=src), "qs")

                @block.vector
                def _(v):
                    kb.wait(v, "qs")
                    kb.wait(v, "dve")
                    kb.inc(v.memset(scr["bigpos"][:, :], BIGPOS), "dve")

                emit_sort(kb, block, bufA, bufB, tbufA, tbufB, ps[2], cst, nbat=E)
                emit_gs_bm(kb, block, bufA, gs, bm, nbat=E)
                emit_theta_sigma(kb, block, gs, bm, thA, thB, tbufA, tbufB, ps[2], cst,
                                 sig_fm, thbc, sigbc, scr, dram)
                emit_fs(kb, block, scores_own, sigbc, fsfm, dram["fs"], ps, cst, fsb)

                rbw = sb2("rbw", [128, 2048], BF16)

                @block.sync
                def _(s):
                    kb.wait(s, "qp")
                    for e in range(E):
                        for nm in ("Wup", "Wgate", "Wdown"):
                            kb.inc(s.dma_start(out=rbw[:, :],
                                               in_=w16[nm][e].rearrange("p c h -> p (c h)")), "qs")

                for engname in ("vector", "gpsimd", "scalar", "tensor", "sync"):
                    def mk(e, _n=engname):
                        kb.wait_all(e)
                    getattr(block, engname)(mk)

            with ExitStack() as sw:
                def sb3(name, shape, dtype=F32):
                    return sw.enter_context(nc.sbuf_tensor(name, shape, dtype))
                fbufs = {
                    "xq": sb3("xq", [128, 8 * 1024]),
                    "yq": sb3("yq", [128, 8 * 1024], BF16),
                    "hgrp": sb3("hgrp", [128, 16 * 1024], BF16),
                    "dacc": sb3("dacc", [128, 16 * 512]),
                    "silu": sb3("silu", [128, 2 * 512], BF16),
                    "fsbc": sb3("fsbc", [128, 2 * 1024], BF16),
                    "swu": sb3("swu", [128, 2 * 2048], BF16),
                    "swg": sb3("swg", [128, 2 * 2048], BF16),
                    "swd": sb3("swd", [128, 8 * 2048], BF16),
                    "arow": sb3("f_arow", [1, 1024]),
                    "brow": sb3("f_brow", [1, 1024]),
                    "abc": sb3("f_abc", [128, 1024]),
                    "bbc": sb3("f_bbc", [128, 1024]),
                    "xsq": sb3("xsq", [128, 512], BF16),
                    "xbf": sb3("xbf", [128, 512], BF16),
                    "srow": sb3("srow", [1, 1024]),
                }
                emit_ffn_sweep(kb, block, x_img, x1_img, dram["fs"], w16, bias16,
                               bdown_sb, fsfm, ps, cst, fbufs, dram)
                for engname in ("vector", "gpsimd", "scalar", "tensor", "sync"):
                    def mk(e, _n=engname):
                        kb.wait_all(e)
                    getattr(block, engname)(mk)

        with ExitStack() as sa:
            def sb4(name, shape, dtype=F32):
                return sa.enter_context(nc.sbuf_tensor(name, shape, dtype))
            aw["wr16"] = sb4("wr16", [128, 8192], BF16)
            aw["wk16"] = sb4("wk16", [128, 8192], BF16)
            aw["wv16"] = sb4("wv16", [128, 8192], BF16)
            aw["wla16"] = sb4("wla16", [128, 512], BF16)
            aw["ala16"] = sb4("ala16", [128, 512], BF16)
            aw["gla16"] = sb4("gla16", [128, 1024], BF16)
            aw["wlb16"] = sb4("wlb16", [64, 1024], BF16)
            aw["alb16"] = sb4("alb16", [64, 1024], BF16)
            aw["glb16"] = sb4("glb16", [128, 1024], BF16)
            aw["gbias_row16"] = sb4("gbias_row16", [1, 1024], BF16)
            aw["vecs"] = sb4("vecs", [128, 96])
            aw["vstage"] = sb4("vstage", [128, 128])
            aw["gnwrow"] = sb4("gnwrow", [1, 1024])
            aw["gnbrow"] = sb4("gnbrow", [1, 1024])
            abufs["cexp_sb"] = sb4("cexp_sb", [128, 512])
            abufs["bonus"] = sb4("bonus", [16, T], BF16)
            for k2 in ATT_ONLY:
                cst[k2] = sb4("s_" + k2, *cst_shapes[k2])

            @block.sync
            def _(s):
                kb.wait(s, "dve")
                for k2 in ATT_ONLY:
                    kb.inc(s.dma_start(out=cst[k2][:, :], in_=cst_dram[k2]), "qs")

            emit_attn_setup(kb, block, nc, a_in, aw, aw16_dram, ps, cst, None)

            with ExitStack() as s1:
                def sba(name, shape, dtype=F32):
                    return s1.enter_context(nc.sbuf_tensor(name, shape, dtype))
                for nm in ("fA", "fB", "fC"):
                    abufs[nm] = sba(nm, [128, 4096])
                abufs["fE"] = sba("fE", [128, 2048])
                for i in range(6):
                    abufs[f"b{i}"] = sba(f"b{i}", [128, 4096], BF16)
                abufs["carry"] = sba("carry", [128, 8])
                abufs["cval"] = sba("cval", [128, 64])
                abufs["arow"] = sba("p_arow", [1, 1024])
                abufs["brow"] = sba("p_brow", [1, 1024])
                abufs["abc"] = sba("p_abc", [128, 512])
                abufs["bbc"] = sba("p_bbc", [128, 512])
                abufs["z64"] = abufs["abc"][0:64, :].bitcast(BF16)[:, 0:512]
                emit_attn_prep(kb, block, x1_img, aw, ps, ps16, cst, abufs, adram)
                for engname in ("vector", "gpsimd", "scalar", "tensor", "sync"):
                    def mk(e, _n=engname):
                        kb.wait_all(e)
                    getattr(block, engname)(mk)

            aw["wo16"] = sb4("wo16", [128, 8192], BF16)
            aw["gnwbc"] = sb4("gnwbc", [128, 1024])
            aw["gnbbc"] = sb4("gnbbc", [128, 1024])

            @block.sync
            def _(s):
                kb.inc(s.dma_start(out=aw["wo16"][:, :],
                                   in_=aw16_dram["wo16"].rearrange("p a b -> p (a b)")), "qs")
            emit_gnbc(kb, block, aw, ps, cst)

            sb2d = {
                "a_s": sb4("a_s", [128, 8 * 512], BF16),
                "b_s": sb4("b_s", [128, 8 * 512], BF16),
                "k_s": sb4("k_s", [128, 8 * 512], BF16),
                "r_s": sb4("r_s", [128, 8 * 512], BF16),
                "v_s": sb4("v_s", [128, 4 * C], BF16),
                "cexp": None,
                "yt": sb4("yt", [128, 8 * 128]),
                "y0p": sb4("y0p", [128, 128], BF16),
                "bd": sb4("bd", [128, 5 * 128], BF16),
                "scmm": sb4("scmm", [128, 512], BF16),
                "lad": sb4("lad", [128, 10 * 128], BF16),
                "mrkt": sb4("mrkt", [128, 128], BF16),
                "mkav": sb4("mkav", [128, 128]),
                "btm": sb4("btm", [128, 256], BF16),
                "xs": sb4("xs", [128, 128], BF16),
                "dlt": sb4("dlt", [128, 128]),
                "o_tm": sb4("o_tm", [128, 4 * C]),
                "osb": sb4("osb", [128, 256]),
                "ps16": ps16,
                "bonus": abufs["bonus"],
                "ep_g": sb4("ep_g", [128, 1024], BF16),
                "ep_st": sb4("ep_st", [128, 16]),
                "ep_st2": sb4("ep_st2", [128, 16]),
                "ep_sq": sb4("ep_sq", [128, 1024]),
                "ep_bt": sb4("ep_bt", [128, 64]),
                "ep_ogf": sb4("ep_ogf", [128, 4096], BF16),
                "ep_x1": sb4("ep_x1", [128, 2048]),
            }
            epi = make_attn_epilogue(kb, block, aw, ps, ps16, cst, sb2d, adram,
                                     x1_img, x2_img)
            emit_scan(kb, block, adram["at"], adram["bt"], adram["kt"], adram["rt"],
                      adram["v_d"], None, None, ps, cst, sb2d,
                      epilogue_fn=epi, cexp_sb=abufs["cexp_sb"])
            for engname in ("vector", "gpsimd", "scalar", "tensor", "sync"):
                def mk(e, _n=engname):
                    kb.wait_all(e)
                getattr(block, engname)(mk)

        rw = {
            "wr_sb": sbuf(kb, "wr_sb", [128, 512]),
            "brrow": sbuf(kb, "brrow", [1, 64]),
            "wcrow": sbuf(kb, "wcrow", [1, 64]),
            "brbc": sbuf(kb, "brbc", [128, 64]),
            "wrcsbc": sbuf(kb, "wrcsbc", [128, 64]),
        }
        emit_router_weights(kb, block, Wr_in, br_in, rw, ps, cst)
        rbufs = {
            "rx": sbuf(kb, "rx", [128, 4096]),
            "rsq": sbuf(kb, "rsq", [128, 1024]),
            "rst": sbuf(kb, "rst", [128, 8]),
            "rl": sbuf(kb, "rl", [128, 64]),
        }
        emit_router(kb, block, x2_img, sc3_own, rw, ps, cst, rbufs)

        @block.sync
        def _(s):
            kb.wait_all(s)
    return nc


A_NAMES = ["x_r", "x_w", "x_k", "x_v", "x_a", "k_k", "k_a", "r_k", "W_r", "W_k",
           "W_v", "W_o", "w_lora_a", "w_lora_b", "w_lora_bias", "a_lora_a",
           "a_lora_b", "a_lora_bias", "g_lora_a", "g_lora_b", "g_lora_bias",
           "gn_weight", "gn_bias"]


def _fm_img(xb):
    return np.ascontiguousarray(
        xb.T.reshape(8, 128, T).transpose(1, 0, 2).reshape(128, 8 * T))


def _from_img(img):
    return np.ascontiguousarray(
        np.asarray(img).reshape(128, 8, T).transpose(1, 0, 2).reshape(C, T).T)


def kernel(**inputs):
    from concourse.bass_utils import run_bass_kernel_spmd
    x = np.ascontiguousarray(np.asarray(inputs["L0_attn_hidden_states"], np.float32))
    imgs = [_fm_img(np.ascontiguousarray(x[b])) for b in range(B)]
    Wr = np.ascontiguousarray(np.asarray(inputs["L0_ffn_Wr"], np.float32))
    br = np.ascontiguousarray(np.asarray(inputs["L0_ffn_br"], np.float32))
    ffn = {nm: np.ascontiguousarray(np.asarray(inputs["L0_ffn_" + nm], np.float32))
           for nm in ("Wup", "Wgate", "Wdown", "bup", "bgate", "bdown")}
    attn = {nm: np.ascontiguousarray(np.asarray(inputs["L0_attn_" + nm], np.float32))
            for nm in A_NAMES}
    cores = list(range(B))

    ncA = build_lA()
    resA = run_bass_kernel_spmd(
        ncA, [{"x_img": imgs[b], "Wr": Wr, "br": br} for b in range(B)],
        core_ids=cores)
    sc1 = [np.ascontiguousarray(np.asarray(resA.results[b]["sc_own"], np.float32))
           for b in range(B)]
    scores_em = np.ascontiguousarray(np.concatenate(sc1, axis=0).T)

    ncB = build_lB()
    insB = []
    for b in range(B):
        d = {"x_img": imgs[b], "scores_em": scores_em, "scores_own": sc1[b],
             "Wr": Wr, "br": br}
        d.update(ffn)
        d.update(attn)
        insB.append(d)
    resB = run_bass_kernel_spmd(ncB, insB, core_ids=cores)
    x2 = [np.ascontiguousarray(np.asarray(resB.results[b]["x2_img"], np.float32))
          for b in range(B)]
    sc3 = [np.ascontiguousarray(np.asarray(resB.results[b]["sc3_own"], np.float32))
           for b in range(B)]
    scores_em3 = np.ascontiguousarray(np.concatenate(sc3, axis=0).T)

    ncC = build_l2()
    insC = []
    for b in range(B):
        d = {"x_img": x2[b], "scores_em": scores_em3, "scores_own": sc3[b]}
        d.update(ffn)
        insC.append(d)
    resC = run_bass_kernel_spmd(ncC, insC, core_ids=cores)
    out = np.empty((B, T, C), np.float32)
    for b in range(B):
        out[b] = _from_img(np.asarray(resC.results[b]["x3_img"], np.float32))
    return out


# version 03
# speedup vs baseline: 1.0123x; 1.0072x over previous
import numpy as np
import ml_dtypes
import concourse.bass as bass
import concourse.mybir as mybir
mybir = mybir
from contextlib import ExitStack

F32 = mybir.dt.float32
BF16 = mybir.dt.bfloat16
I32 = mybir.dt.int32
AF = mybir.ActivationFunctionType
ALU = mybir.AluOpType
AX = mybir.AxisListType

B, T, C = 8, 4096, 1024
NH, HD = 16, 64
E, HH = 64, 256
Gsz, KTOP = 64, 16
NTOK = B * T
TQ = NTOK // Gsz
BUDGET = TQ * KTOP
SCF = NTOK // 128
BIGPOS = 1e30


class KB:
    def __init__(self, nc, stack):
        self.nc = nc
        self.stack = stack
        self.sems = {}
        self.cnt = {}
        for name in ("pe", "dve", "act", "pool", "qs", "qa", "qp"):
            self.sems[name] = stack.enter_context(nc.semaphore("sem_" + name))
            self.cnt[name] = 0
        self.consts = {}

    def inc(self, instr, eng):
        n = 16 if eng in ("qs", "qa", "qp") else 1
        instr.then_inc(self.sems[eng], n)
        self.cnt[eng] += 1
        return self.cnt[eng]

    def val(self, eng, count=None):
        if count is None:
            count = self.cnt[eng]
        return count * (16 if eng in ("qs", "qa", "qp") else 1)

    def wait(self, e, eng, count=None):
        e.wait_ge(self.sems[eng], self.val(eng, count))

    def wait_all(self, e, engs=("pe", "dve", "act", "pool", "qs", "qa", "qp")):
        for name in engs:
            if self.cnt[name] > 0:
                self.wait(e, name)

    def const(self, name, arr):
        h = self.nc.inline_tensor(np.ascontiguousarray(arr), name=name)
        self.consts[name] = h
        return h.ap()


def sbuf(kb, name, shape, dtype=F32):
    return kb.stack.enter_context(kb.nc.sbuf_tensor(name, shape, dtype))


def mkpsum(kb, name, shape=(128, 512), dtype=F32):
    return kb.stack.enter_context(kb.nc.psum_tensor(name, list(shape), dtype))


def emit_sort(kb, block, bufA, bufB, tbufA, tbufB, ps_tr, cst, nbat):
    nc = kb.nc
    W = nbat * 256
    state = {"cur": bufA, "alt": bufB}

    def wait_cde(e, minus_dve=0):
        kb.wait(e, "dve", kb.cnt["dve"] - minus_dve)
        kb.wait(e, "pool")
        kb.wait(e, "act")

    def stage_fland(m, j):
        cur, alt = state["cur"], state["alt"]
        d = 1 << j

        def views(buf, dirbit=None):
            if dirbit is None:
                vv = buf[:, :].rearrange("p (q two d) -> p q two d", two=2, d=d)
                return vv[:, :, 0, :], vv[:, :, 1, :]
            dm = 1 << m
            vv = buf[:, :].rearrange(
                "p (q dirb mid two d) -> p q dirb mid two d",
                dirb=2, two=2, d=d, mid=dm // (2 * d))
            return vv[:, :, dirbit, :, 0, :], vv[:, :, dirbit, :, 1, :]

        @block.vector
        def _(v):
            wait_cde(v)
            if m <= 7:
                loS, hiS = views(cur, 0)
                loD, hiD = views(alt, 0)
                v.tensor_tensor(loD, loS, hiS, op=ALU.max)
                v.tensor_tensor(hiD, loS, hiS, op=ALU.min)
                loS2, hiS2 = views(cur, 1)
                loD2, hiD2 = views(alt, 1)
                v.tensor_tensor(loD2, loS2, hiS2, op=ALU.min)
                kb.inc(v.tensor_tensor(hiD2, loS2, hiS2, op=ALU.max), "dve")
            else:
                loS, hiS = views(cur)
                loD, hiD = views(alt)
                v.tensor_tensor(loD, loS, hiS, op=ALU.max)
                kb.inc(v.tensor_tensor(hiD, loS, hiS, op=ALU.min), "dve")

        state["cur"], state["alt"] = alt, cur

    def negate(m):
        cur = state["cur"]
        half = (W // 2 // 512) * 512 or W // 2

        @block.vector
        def _(v):
            wait_cde(v)
            kb.inc(v.tensor_scalar_mul(cur[:, 0:half], cur[:, 0:half],
                                       cst[f"sgn{m}"][:, :]), "dve")

        @block.gpsimd
        def _(g):
            wait_cde(g, minus_dve=1)
            kb.inc(g.tensor_scalar_mul(cur[:, half:W], cur[:, half:W],
                                       cst[f"sgn{m}"][:, :]), "pool")

    def tland(m, jlist):
        cur = state["cur"]
        ntile = nbat * 2
        GT = 8

        for g0 in range(0, ntile, GT):
            ngt = min(g0 + GT, ntile) - g0
            for h0 in range(0, ngt, 4):
                nh = min(h0 + 4, ngt) - h0

                @block.tensor
                def _(t, g0=g0, h0=h0, nh=nh):
                    kb.wait(t, "dve")
                    kb.wait(t, "pool")
                    kb.wait(t, "act")
                    for k in range(nh):
                        kb.inc(t.transpose(
                            ps_tr[:, k * 128:(k + 1) * 128],
                            in_=cur[:, (g0 + h0 + k) * 128:(g0 + h0 + k + 1) * 128],
                            identity=cst["ident"][:, :]), "pe")

                @block.scalar
                def _(sc, h0=h0, nh=nh):
                    kb.wait(sc, "pe")
                    kb.inc(sc.copy(tbufA[:, h0 * 128:(h0 + nh) * 128],
                                   ps_tr[:, 0:nh * 128]), "act")

            tst = {"cur": tbufA, "alt": tbufB}
            for j in jlist:
                dp = 1 << (j - 8)
                dm = 1 << (m - 8)
                tcur, talt = tst["cur"], tst["alt"]

                def tviews(buf, dirbit, ngt=ngt, dp=dp, dm=dm):
                    if dm >= 128:
                        if dirbit == 1:
                            return None
                        vv = buf[:, 0:ngt * 128].rearrange(
                            "q (tile hi two d) -> q tile hi two d", two=2, d=dp,
                            tile=ngt, hi=128 // (2 * dp))
                        return vv[:, :, :, 0, :], vv[:, :, :, 1, :]
                    vv = buf[:, 0:ngt * 128].rearrange(
                        "q (tile hi dirb mid two d) -> q tile hi dirb mid two d",
                        tile=ngt, dirb=2, two=2, d=dp,
                        hi=128 // (2 * dm), mid=max(dm // (2 * dp), 1))
                    sel_lo = vv[:, :, :, dirbit, :, 0, :]
                    sel_hi = vv[:, :, :, dirbit, :, 1, :]
                    return sel_lo, sel_hi

                @block.vector
                def _(v, tcur=tcur, talt=talt, tviews=tviews):
                    wait_cde(v)
                    r0 = tviews(tcur, 0)
                    d0 = tviews(talt, 0)
                    v.tensor_tensor(d0[0], r0[0], r0[1], op=ALU.max)
                    v.tensor_tensor(d0[1], r0[0], r0[1], op=ALU.min)
                    r1 = tviews(tcur, 1)
                    if r1 is not None:
                        d1 = tviews(talt, 1)
                        v.tensor_tensor(d1[0], r1[0], r1[1], op=ALU.min)
                        kb.inc(v.tensor_tensor(d1[1], r1[0], r1[1], op=ALU.max), "dve")
                    else:
                        kb.inc(v.memset(cst["junk"][:, :], 0.0), "dve")

                tst["cur"], tst["alt"] = talt, tcur

            tc = tst["cur"]
            for h0 in range(0, ngt, 4):
                nh = min(h0 + 4, ngt) - h0

                @block.tensor
                def _(t, g0=g0, h0=h0, nh=nh, tc=tc):
                    kb.wait(t, "dve")
                    kb.wait(t, "pool")
                    kb.wait(t, "act")
                    for k in range(nh):
                        kb.inc(t.transpose(
                            ps_tr[:, k * 128:(k + 1) * 128],
                            in_=tc[:, (h0 + k) * 128:(h0 + k + 1) * 128],
                            identity=cst["ident"][:, :]), "pe")

                @block.scalar
                def _(sc, g0=g0, h0=h0, nh=nh):
                    kb.wait(sc, "pe")
                    kb.inc(sc.copy(cur[:, (g0 + h0) * 128:(g0 + h0 + nh) * 128],
                                   ps_tr[:, 0:nh * 128]), "act")

    for m in range(1, 16):
        tj = list(range(m - 1, 7, -1))
        if tj:
            tland(m, tj)
        fj = list(range(min(m - 1, 7), -1, -1))
        if m >= 8:
            skip_neg = (m - 8) >= 7
            if not skip_neg:
                negate(m)
            for j in fj:
                stage_fland(m, j)
            if not skip_neg:
                negate(m)
        else:
            for j in fj:
                stage_fland(m, j)

    if state["cur"] is not bufA:
        @block.vector
        def _(v):
            wait_cde(v)
            kb.inc(v.tensor_copy(bufA[:, 0:W // 2], bufB[:, 0:W // 2]), "dve")

        @block.gpsimd
        def _(g):
            wait_cde(g, minus_dve=1)
            kb.inc(g.tensor_copy(bufA[:, W // 2:W], bufB[:, W // 2:W]), "pool")


def emit_gs_bm(kb, block, buf, gs_out, bm_out, nbat):
    @block.vector
    def _(v):
        kb.wait_all(v, ("dve", "pool", "act"))
        vv = buf[:, :].rearrange("p (e q s) -> p e q s", q=4, s=64)
        kb.inc(v.reduce_sum(gs_out[:, :].rearrange("p (e q) -> p e q", q=4),
                            vv, axis=AX.X), "dve")

    @block.vector
    def _(v):
        kb.wait_all(v, ("pool", "act"))
        vv = buf[:, :].rearrange("p (e q s) -> p e q s", q=4, s=64)
        kb.inc(v.tensor_copy(bm_out[:, :].rearrange("p (e q) -> p e q", q=4),
                             vv[:, :, :, 63:64].squeeze(3)), "dve")


def emit_theta_sigma(kb, block, gs, bm, thA, thB, tbufA, tbufB, ps_tr, cst,
                     sig_fm, thbc, sigbc, scr, dram_scratch):
    nc = kb.nc

    @block.vector
    def _(v):
        kb.wait_all(v, ("dve", "pool", "act"))
        kb.inc(v.tensor_copy(thA[:, :], gs[:, :]), "dve")

    emit_sort(kb, block, thA, thB, tbufA, tbufB, ps_tr, cst, nbat=1)

    @block.tensor
    def _(t):
        kb.wait_all(t, ("dve", "pool", "act"))
        kb.inc(t.matmul(ps_tr[0:1, 0:1], lhsT=cst["oh31"][:, :],
                        rhs=thA[:, 255:256], start=True, stop=True), "pe")

    @block.scalar
    def _(sc):
        kb.wait(sc, "pe")
        kb.inc(sc.copy(scr["th11"][:, :], ps_tr[0:1, 0:1]), "act")

    @block.tensor
    def _(t):
        kb.wait(t, "act")
        kb.inc(t.matmul(ps_tr[:, 128:129], lhsT=cst["onescol"][:, :],
                        rhs=scr["th11"][:, :], start=True, stop=True), "pe")

    @block.vector
    def _(v):
        kb.wait(v, "pe")
        kb.inc(v.tensor_copy(thbc[:, :], ps_tr[:, 128:129]), "dve")

    @block.vector
    def _(v):
        kb.wait(v, "dve")
        kb.wait_all(v, ("pool", "act"))
        kb.inc(v.tensor_single_scalar(thA[:, :], gs[:, :], thbc[:, 0:1], op=ALU.is_ge), "dve")
        kb.wait(v, "dve")
        kb.inc(v.tensor_tensor(scr["cand"][:, :], thA[:, :], bm[:, :], op=ALU.mult), "dve")
        kb.wait(v, "dve")
        kb.inc(v.tensor_scalar(thB[:, :], thA[:, :], -BIGPOS, BIGPOS,
                               op0=ALU.mult, op1=ALU.add), "dve")
        kb.wait(v, "dve")
        kb.inc(v.tensor_tensor(scr["cand"][:, :], scr["cand"][:, :], thB[:, :],
                               op=ALU.add), "dve")
        kb.wait(v, "dve")
        kb.inc(v.tensor_reduce(scr["emin"][:, :],
                               scr["cand"][:, :].rearrange("p (e q) -> p e q", q=4),
                               axis=AX.X, op=ALU.min), "dve")

    @block.tensor
    def _(t):
        kb.wait(t, "dve")
        kb.inc(t.transpose(ps_tr[0:64, 0:128], in_=scr["emin"][:, :],
                           identity=cst["ident"][:, :]), "pe")

    @block.vector
    def _(v):
        kb.wait(v, "pe")
        kb.inc(v.tensor_copy(scr["eminT"][:, :], ps_tr[0:64, 0:128]), "dve")
        kb.wait(v, "dve")
        kb.inc(v.tensor_reduce(sig_fm[:, :], scr["eminT"][:, :],
                               axis=AX.X, op=ALU.min), "dve")

    @block.tensor
    def _(t):
        kb.wait(t, "dve")
        kb.inc(t.transpose(ps_tr[0:1, 256:320], in_=sig_fm[:, :],
                           identity=cst["ident"][0:64, 0:64]), "pe")

    @block.vector
    def _(v):
        kb.wait(v, "pe")
        kb.inc(v.tensor_copy(scr["sigrow"][:, :], ps_tr[0:1, 256:320]), "dve")

    @block.tensor
    def _(t):
        kb.wait(t, "dve")
        kb.inc(t.matmul(ps_tr[:, 320:384], lhsT=cst["onescol"][:, :],
                        rhs=scr["sigrow"][:, :], start=True, stop=True), "pe")

    @block.vector
    def _(v):
        kb.wait(v, "pe")
        kb.inc(v.tensor_copy(sigbc[:, :], ps_tr[:, 320:384]), "dve")


def emit_weight_cast(kb, block, w_in, w16):
    @block.gpsimd
    def _(g):
        for e in range(E):
            for nm in ("Wup", "Wgate"):
                src = w_in[nm][e].rearrange("(c p) h -> p c h", p=128)
                kb.inc(g.dma_start(out=w16[nm][e], in_=src), "qp")
            src = w_in["Wdown"][e].rearrange("(c p) d -> p c d", p=128)
            kb.inc(g.dma_start(out=w16["Wdown"][e], in_=src), "qp")


def emit_ffn_sweep(kb, block, x_img_in, x_img_out, fs_dram, w16, bias16,
                   bdown_sb, fsfm, ps, cst, bufs, dram, dbg=None):
    TQL = 1024
    NQ = T // TQL
    NTC = TQL // 512
    xq, yq, hgrp = bufs["xq"], bufs["yq"], bufs["hgrp"]
    silu_sb, fsbc = bufs["silu"], bufs["fsbc"]
    swu, swg, swd = bufs["swu"], bufs["swg"], bufs["swd"]
    arow, brow, abc, bbc = bufs["arow"], bufs["brow"], bufs["abc"], bufs["bbc"]
    xsq, xbf, srow = bufs["xsq"], bufs["xbf"], bufs["srow"]

    for q in range(NQ):
        @block.sync
        def _(s, q=q):
            kb.wait_all(s, ("dve", "qs"))
            for c in range(8):
                kb.inc(s.dma_start(out=xq[:, c * TQL:(c + 1) * TQL],
                                   in_=x_img_in[:, c * T + q * TQL: c * T + (q + 1) * TQL]), "qs")

        for tc in range(NTC):
            sl = slice(tc * 512, (tc + 1) * 512)
            for c in range(8):
                @block.scalar
                def _(sc, tc=tc, c=c, sl=sl):
                    if c == 0:
                        kb.wait(sc, "qs")
                    kb.wait(sc, "pe")
                    kb.inc(sc.activation(out=xsq[:, :], in_=xq[:, c * TQL:][:, sl],
                                         func=AF.Square), "act")

                @block.gpsimd
                def _(g, tc=tc, c=c, sl=sl):
                    if c == 0:
                        kb.wait(g, "qs")
                    kb.wait(g, "pe")
                    kb.inc(g.tensor_copy(xbf[:, :], xq[:, c * TQL:][:, sl]), "pool")

                @block.tensor
                def _(t, c=c):
                    kb.wait(t, "act")
                    kb.wait(t, "pool")
                    kb.wait(t, "dve")
                    t.matmul(ps[2][0:1, :], lhsT=cst["ones1_16"][:, :], rhs=xbf[:, :],
                             start=(c == 0), stop=(c == 7))
                    kb.inc(t.matmul(ps[3][0:1, :], lhsT=cst["ones1_16"][:, :],
                                    rhs=xsq[:, :], start=(c == 0), stop=(c == 7)), "pe")

            @block.vector
            def _(v, sl=sl):
                kb.wait(v, "pe")
                v.tensor_scalar_mul(srow[0:1, 0:512], ps[2][0:1, :], 1.0 / C)
                kb.inc(v.tensor_scalar_mul(srow[0:1, 512:1024], ps[3][0:1, :], 1.0 / C), "dve")
                kb.wait(v, "dve")
                kb.inc(v.tensor_tensor(abc[0:1, sl], srow[0:1, 0:512], srow[0:1, 0:512],
                                       op=ALU.mult), "dve")
                kb.wait(v, "dve")
                kb.inc(v.tensor_tensor(abc[0:1, sl], srow[0:1, 512:1024], abc[0:1, sl],
                                       op=ALU.subtract), "dve")

            @block.scalar
            def _(sc, sl=sl):
                kb.wait(sc, "dve")
                kb.inc(sc.activation(out=bbc[0:1, sl], in_=abc[0:1, sl], func=AF.Ln,
                              bias=cst["eps1"][0:1, 0:1]), "act")
                kb.wait(sc, "act")
                kb.inc(sc.activation(out=arow[0:1, sl], in_=bbc[0:1, sl], func=AF.Exp,
                                     scale=-0.5), "act")

            @block.vector
            def _(v, sl=sl):
                kb.wait(v, "act")
                kb.wait(v, "dve")
                kb.inc(v.tensor_tensor(brow[0:1, sl], srow[0:1, 0:512], arow[0:1, sl],
                                       op=ALU.mult), "dve")

        for half in range(2):
            @block.tensor
            def _(t, half=half):
                kb.wait(t, "dve")
                kb.wait(t, "act")
                t.matmul(ps[2][:, :], lhsT=cst["onescol"][:, :],
                         rhs=arow[0:1, half * 512:(half + 1) * 512], start=True, stop=True)
                kb.inc(t.matmul(ps[3][:, :], lhsT=cst["onescol"][:, :],
                                rhs=brow[0:1, half * 512:(half + 1) * 512],
                                start=True, stop=True), "pe")

            @block.vector
            def _(v, half=half):
                kb.wait(v, "pe")
                v.tensor_copy(abc[:, half * 512:(half + 1) * 512], ps[2][:, :])
                kb.inc(v.tensor_copy(bbc[:, half * 512:(half + 1) * 512], ps[3][:, :]), "dve")

        for c in range(8):
            @block.vector
            def _(v, c=c):
                kb.wait(v, "pe")
                kb.inc(v.tensor_tensor(yq[:, c * TQL:(c + 1) * TQL], xq[:, c * TQL:(c + 1) * TQL],
                                       abc[:, :], op=ALU.mult), "dve")
                kb.wait(v, "dve")
                kb.inc(v.tensor_tensor(yq[:, c * TQL:(c + 1) * TQL],
                                       yq[:, c * TQL:(c + 1) * TQL], bbc[:, :],
                                       op=ALU.subtract), "dve")

        if dbg and q == 0:
            @block.sync
            def _(s):
                kb.wait(s, "dve")
                kb.inc(s.dma_start(out=dbg["y0"], in_=yq[:, :]), "qs")

        for eg in range(8):
            @block.scalar
            def _(sc, q=q, eg=eg):
                if q + eg > 0:
                    kb.wait(sc, "dve")
                for ei in range(8):
                    kb.inc(sc.dma_start(out=swd[:, ei * 2048:(ei + 1) * 2048],
                                        in_=w16["Wdown"][eg * 8 + ei]), "qa")

            for ei in range(8):
                e = eg * 8 + ei
                wslot = e % 2

                @block.sync
                def _(s, e=e, wslot=wslot):
                    if e >= 2 or q > 0:
                        kb.wait(s, "pe")
                        kb.wait(s, "pool")
                    kb.inc(s.dma_start(out=swu[:, wslot * 2048:(wslot + 1) * 2048],
                                       in_=w16["Wup"][e]), "qs")
                    kb.inc(s.dma_start(out=swg[:, wslot * 2048:(wslot + 1) * 2048],
                                       in_=w16["Wgate"][e]), "qs")
                    kb.inc(s.dma_start(
                        out=fsbc[:, wslot * TQL:(wslot + 1) * TQL],
                        in_=fs_dram[e:e + 1, q * TQL:(q + 1) * TQL].to_broadcast([128, TQL])), "qs")

                for tc in range(NTC):
                    sl = slice(tc * 512, (tc + 1) * 512)
                    for hh in range(2):
                        upb = ps[0]
                        gab = ps[1]

                        @block.tensor
                        def _(t, e=e, tc=tc, hh=hh, wslot=wslot, sl=sl, upb=upb, gab=gab):
                            kb.wait(t, "qs")
                            kb.wait(t, "dve")
                            kb.wait(t, "act")
                            for c in range(8):
                                t.matmul(upb[:, :],
                                         lhsT=swu[:, wslot * 2048 + c * 256 + hh * 128:
                                                  wslot * 2048 + c * 256 + (hh + 1) * 128],
                                         rhs=yq[:, c * TQL:][:, sl],
                                         start=(c == 0), stop=(c == 7))
                            for c in range(8):
                                ins = t.matmul(gab[:, :],
                                         lhsT=swg[:, wslot * 2048 + c * 256 + hh * 128:
                                                  wslot * 2048 + c * 256 + (hh + 1) * 128],
                                         rhs=yq[:, c * TQL:][:, sl],
                                         start=(c == 0), stop=(c == 7))
                            kb.inc(ins, "pe")

                        @block.scalar
                        def _(sc, e=e, hh=hh, upb=upb):
                            kb.wait(sc, "pe")
                            kb.inc(sc.activation(out=silu_sb[:, hh * 512:(hh + 1) * 512],
                                                 in_=upb[:, :], func=AF.Silu,
                                                 bias=bias16["bup"][:, e * 2 + hh:e * 2 + hh + 1]), "act")

                        @block.vector
                        def _(v, e=e, ei=ei, tc=tc, hh=hh, sl=sl, gab=gab):
                            kb.wait(v, "act")
                            kb.wait(v, "pe")
                            dst = hgrp[:, ((ei * 2 + hh) * TQL):][:, sl]
                            kb.inc(v.scalar_tensor_tensor(
                                out=dst, in0=gab[:, :],
                                scalar=bias16["bgate"][:, e * 2 + hh:e * 2 + hh + 1],
                                in1=silu_sb[:, hh * 512:(hh + 1) * 512],
                                op0=ALU.add, op1=ALU.mult), "dve")

                        @block.gpsimd
                        def _(g, ei=ei, tc=tc, hh=hh, sl=sl, wslot=wslot):
                            kb.wait(g, "dve")
                            kb.wait(g, "qs")
                            dst = hgrp[:, ((ei * 2 + hh) * TQL):][:, sl]
                            kb.inc(g.tensor_tensor(dst, dst,
                                                   fsbc[:, wslot * TQL:][:, sl],
                                                   op=ALU.mult), "pool")

            for tc in range(NTC):
                sl = slice(tc * 512, (tc + 1) * 512)
                for dw in range(2):
                    for do in range(4):
                        dout = dw * 4 + do
                        pbank = ps[2 + do]

                        @block.tensor
                        def _(t, q=q, eg=eg, tc=tc, dout=dout, sl=sl, pbank=pbank):
                            kb.wait(t, "qa")
                            kb.wait(t, "pool")
                            kb.wait(t, "dve")
                            for ei in range(8):
                                for hc in range(2):
                                    first = (ei == 0 and hc == 0)
                                    last = (ei == 7 and hc == 1 and eg != 7)
                                    ins = t.matmul(pbank[:, :],
                                             lhsT=swd[:, ei * 2048 + hc * 1024 + dout * 128:
                                                      ei * 2048 + hc * 1024 + (dout + 1) * 128],
                                             rhs=hgrp[:, ((ei * 2 + hc) * TQL):][:, sl],
                                             start=first, stop=last)
                                    if last:
                                        kb.inc(ins, "pe")
                            if eg == 7:
                                kb.inc(t.matmul(pbank[:, :],
                                                lhsT=bdown_sb[:, dout * 128:(dout + 1) * 128],
                                                rhs=fsfm[:, q * TQL:][:, sl],
                                                start=False, stop=True), "pe")

                        @block.vector
                        def _(v, q=q, eg=eg, tc=tc, dout=dout, sl=sl, pbank=pbank):
                            kb.wait(v, "pe")
                            dsl = bufs["dacc"][:, (tc * 8 + dout) * 512:(tc * 8 + dout + 1) * 512]
                            if eg == 0:
                                kb.inc(v.tensor_copy(dsl, pbank[:, :]), "dve")
                            else:
                                kb.inc(v.tensor_tensor(dsl, dsl, pbank[:, :], op=ALU.add), "dve")

        for tc in range(NTC):
            for dout in range(8):
                @block.vector
                def _(v, tc=tc, dout=dout):
                    kb.wait(v, "pe")
                    sl = slice(tc * 512, (tc + 1) * 512)
                    kb.inc(v.scalar_tensor_tensor(
                        out=xq[:, dout * TQL:][:, sl],
                        in0=bufs["dacc"][:, (tc * 8 + dout) * 512:(tc * 8 + dout + 1) * 512],
                        scalar=0.5, in1=xq[:, dout * TQL:][:, sl],
                        op0=ALU.mult, op1=ALU.add), "dve")

        @block.sync
        def _(s, q=q):
            kb.wait(s, "dve")
            for c in range(8):
                kb.inc(s.dma_start(out=x_img_out[:, c * T + q * TQL: c * T + (q + 1) * TQL],
                                   in_=xq[:, c * TQL:(c + 1) * TQL]), "qs")


def emit_fs(kb, block, scores_own, sigbc, fsfm, fs_dram, ps, cst, bufs):
    stile = bufs["stile"]
    f0 = bufs["fs0"]
    msk = bufs["fsmask"]
    den = bufs["fsden"]
    for i in range(T // 128):
        @block.scalar
        def _(sc, i=i):
            if i > 0:
                kb.wait(sc, "dve")
            kb.inc(sc.dma_start(out=stile[:, :], in_=scores_own[i * 128:(i + 1) * 128, :]), "qa")

        @block.vector
        def _(v, i=i):
            kb.wait(v, "qa")
            kb.wait(v, "qs")
            if i > 0:
                kb.wait(v, "pe")
            kb.inc(v.tensor_tensor(msk[:, :], stile[:, :], sigbc[:, :], op=ALU.is_ge), "dve")
            kb.wait(v, "dve")
            kb.inc(v.tensor_tensor(f0[:, :], stile[:, :], msk[:, :], op=ALU.mult), "dve")
            kb.wait(v, "dve")
            kb.inc(v.reduce_sum(den[:, :], f0[:, :], axis=AX.X), "dve")
            kb.wait(v, "dve")
            v.tensor_scalar_add(den[:, :], den[:, :], 1e-6)
            kb.inc(v.reciprocal(den[:, :], den[:, :]), "dve")
            kb.wait(v, "dve")
            kb.inc(v.tensor_scalar_mul(f0[:, :], f0[:, :], den[:, :]), "dve")

        @block.tensor
        def _(t, i=i):
            kb.wait(t, "dve")
            if i % 4 == 0 and i > 0:
                kb.wait(t, "act")
            kb.inc(t.transpose(ps[2][0:64, (i % 4) * 128:(i % 4 + 1) * 128],
                               in_=f0[:, :], identity=cst["ident"][:, :]), "pe")

        if i % 4 == 3:
            @block.scalar
            def _(sc, i=i):
                kb.wait(sc, "pe")
                kb.inc(sc.copy(fsfm[:, (i - 3) * 128:(i + 1) * 128],
                               ps[2][0:64, 0:512]), "act")

    @block.sync
    def _(s):
        kb.wait(s, "act")
        kb.inc(s.dma_start(out=fs_dram, in_=fsfm[:, :]), "qs")
        kb.wait(s, "qs")
        kb.inc(s.dma_start(out=bufs["fsrb"][:, :], in_=fs_dram), "qs")


def build_l2(debug=False):
    nc = bass.Bass("TRN2", target_bir_lowering=False, debug=False)
    x_img = nc.dram_tensor("x_img", [128, 8 * T], F32, kind="ExternalInput").ap()
    scores_em = nc.dram_tensor("scores_em", [E, NTOK], F32, kind="ExternalInput").ap()
    scores_own = nc.dram_tensor("scores_own", [T, E], F32, kind="ExternalInput").ap()
    w_in = {
        "Wup": nc.dram_tensor("Wup", [E, C, HH], F32, kind="ExternalInput").ap(),
        "Wgate": nc.dram_tensor("Wgate", [E, C, HH], F32, kind="ExternalInput").ap(),
        "Wdown": nc.dram_tensor("Wdown", [E, HH, C], F32, kind="ExternalInput").ap(),
        "bup": nc.dram_tensor("bup", [E, HH], F32, kind="ExternalInput").ap(),
        "bgate": nc.dram_tensor("bgate", [E, HH], F32, kind="ExternalInput").ap(),
        "bdown": nc.dram_tensor("bdown", [E, C], F32, kind="ExternalInput").ap(),
    }
    x3_img = nc.dram_tensor("x3_img", [128, 8 * T], F32, kind="ExternalOutput").ap()
    dbg = {}
    if debug:
        dbg["fs"] = nc.dram_tensor("o_dbg_fs", [64, T], BF16, kind="ExternalOutput").ap()
        dbg["y0"] = nc.dram_tensor("o_dbg_y0", [128, 8 * 1024], BF16, kind="ExternalOutput").ap()
        dbg["sig"] = nc.dram_tensor("o_dbg_sig", [E, 1], F32, kind="ExternalOutput").ap()
        dbg["gs"] = nc.dram_tensor("o_dbg_gs", [128, E * 4], F32, kind="ExternalOutput").ap()
        dbg["bm"] = nc.dram_tensor("o_dbg_bm", [128, E * 4], F32, kind="ExternalOutput").ap()
        dbg["th"] = nc.dram_tensor("o_dbg_th", [1, 1], F32, kind="ExternalOutput").ap()
        dbg["srt"] = nc.dram_tensor("o_dbg_srt", [128, E * 256], F32, kind="ExternalOutput").ap()
        dbg["cand"] = nc.dram_tensor("o_dbg_cand", [128, E * 4], F32, kind="ExternalOutput").ap()
        dbg["emin"] = nc.dram_tensor("o_dbg_emin", [128, E], F32, kind="ExternalOutput").ap()
        dbg["mask"] = nc.dram_tensor("o_dbg_mask", [128, E * 4], mybir.dt.uint32, kind="ExternalOutput").ap()
        dbg["sigbc"] = nc.dram_tensor("o_dbg_sigbc", [128, E], F32, kind="ExternalOutput").ap()

    w16 = {
        "Wup": [nc.dram_tensor(f"w16up{e}", [128, 8, HH], BF16).ap() for e in range(E)],
        "Wgate": [nc.dram_tensor(f"w16ga{e}", [128, 8, HH], BF16).ap() for e in range(E)],
        "Wdown": [nc.dram_tensor(f"w16dn{e}", [128, 2, C], BF16).ap() for e in range(E)],
    }
    dram = {
        "theta": nc.dram_tensor("d_theta", [1, 1], F32).ap(),
        "sigma": nc.dram_tensor("d_sigma", [E, 1], F32).ap(),
        "arow": nc.dram_tensor("d_arow", [1, 1024], F32).ap(),
        "brow": nc.dram_tensor("d_brow", [1, 1024], F32).ap(),
        "fs": nc.dram_tensor("d_fs", [E, T], BF16).ap(),
    }

    with ExitStack() as stack:
        kb = KB(nc, stack)
        cst_dram = {
            "ident": kb.const("c_ident", np.eye(128, dtype=np.float32)),
            "oh31": kb.const("c_oh31", (np.arange(128) == 31).astype(np.float32).reshape(128, 1)),
            "ones1_16": kb.const("c_ones1_16", np.ones((128, 1), ml_dtypes.bfloat16)),
            "ones_row16": kb.const("c_ones_row16", np.ones((1, 512), ml_dtypes.bfloat16)),
            "eps1": kb.const("c_eps1", np.full((128, 1), 1e-5, np.float32)),
            "onescol": kb.const("c_onescol", np.ones((1, 128), np.float32)),
        }
        for m in range(8, 15):
            s = np.where(((np.arange(128) >> (m - 8)) & 1) == 0, 1.0, -1.0).astype(np.float32)
            cst_dram[f"sgn{m}"] = kb.const(f"c_sgn{m}", s.reshape(128, 1))

        cst = {}
        cst["ident"] = sbuf(kb, "ident", [128, 128])
        cst["oh31"] = sbuf(kb, "oh31", [128, 1])
        cst["ones1_16"] = sbuf(kb, "ones1_16", [128, 1], BF16)
        cst["ones_row16"] = sbuf(kb, "ones_row16", [1, 512], BF16)
        cst["eps1"] = sbuf(kb, "eps1", [128, 1])
        cst["onescol"] = sbuf(kb, "onescol", [1, 128])
        for m in range(8, 15):
            cst[f"sgn{m}"] = sbuf(kb, f"sgn{m}", [128, 1])
        cst["junk"] = sbuf(kb, "junk", [1, 1])

        ps = [mkpsum(kb, f"ps{i}") for i in range(8)]

        sig_fm = sbuf(kb, "sig_fm", [E, 1])
        thbc = sbuf(kb, "thbc", [128, 1])
        sigbc = sbuf(kb, "sigbc", [128, E])
        fsfm = sbuf(kb, "fsfm", [64, T], BF16)
        bias16 = {
            "bup": sbuf(kb, "bupS", [128, 128]),
            "bgate": sbuf(kb, "bgaS", [128, 128]),
            "bupR": sbuf(kb, "bupR", [128, 128]),
            "bgaR": sbuf(kb, "bgaR", [128, 128]),
        }
        bdown_sb = sbuf(kb, "bdownS", [64, C], BF16)

        bufs_rb = sbuf(kb, "bufs_rb", [128, 2048], BF16)
        block = stack.enter_context(nc.Block())

        emit_weight_cast(kb, block, w_in, w16)

        @block.sync
        def _(s):
            for nm in ("ident", "oh31", "ones1_16", "ones_row16", "eps1", "onescol"):
                kb.inc(s.dma_start(out=cst[nm][:, :], in_=cst_dram[nm]), "qs")
            for m in range(8, 15):
                kb.inc(s.dma_start(out=cst[f"sgn{m}"][:, :], in_=cst_dram[f"sgn{m}"]), "qs")

        @block.gpsimd
        def _(g):
            kb.inc(g.dma_start(out=bdown_sb[:, :], in_=w_in["bdown"]), "qp")
            kb.inc(g.memset(cst["junk"][:, :], 0.0), "pool")

        @block.sync
        def _(s):
            kb.inc(s.dma_start(out=bias16["bupR"][:, :],
                               in_=w_in["bup"].rearrange("e (r p) -> (e r) p", p=128)), "qs")
            kb.inc(s.dma_start(out=bias16["bgaR"][:, :],
                               in_=w_in["bgate"].rearrange("e (r p) -> (e r) p", p=128)), "qs")

        @block.tensor
        def _(t):
            kb.wait(t, "qs")
            t.transpose(ps[0][:, 0:128], in_=bias16["bupR"][:, :], identity=cst["ident"][:, :])
            kb.inc(t.transpose(ps[1][:, 0:128], in_=bias16["bgaR"][:, :],
                               identity=cst["ident"][:, :]), "pe")

        @block.vector
        def _(v):
            kb.wait(v, "pe")
            v.tensor_copy(bias16["bup"][:, :], ps[0][:, 0:128])
            kb.inc(v.tensor_copy(bias16["bgate"][:, :], ps[1][:, 0:128]), "dve")

        with ExitStack() as sp:
            def sb2(name, shape, dtype=F32):
                return sp.enter_context(nc.sbuf_tensor(name, shape, dtype))
            bufA = sb2("bufA", [128, E * 256])
            bufB = sb2("bufB", [128, E * 256])
            tbufA = sb2("tbufA", [128, 8 * 128])
            tbufB = sb2("tbufB", [128, 8 * 128])
            gs = sb2("gs", [128, E * 4])
            bm = sb2("bm", [128, E * 4])
            thA = sb2("thA", [128, 256])
            thB = sb2("thB", [128, 256])
            scr = {
                "th11": sb2("th11", [1, 1]),
                "mask": sb2("mask", [128, E * 4], mybir.dt.uint32),
                "cand": sb2("cand", [128, E * 4]),
                "emin": sb2("emin", [128, E]),
                "eminT": sb2("eminT", [64, 128]),
                "bigpos": sb2("bigpos", [128, E * 4]),
                "sigrow": sb2("sigrow", [1, 64]),
            }
            fsb = {
                "stile": sb2("stile", [128, 64]),
                "fs0": sb2("fs0", [128, 64]),
                "fsmask": sb2("fsmask", [128, 64]),
                "fsden": sb2("fsden", [128, 1]),
                "fsrb": sb2("fsrb", [64, T], BF16),
            }

            @block.sync
            def _(s):
                kb.wait(s, "qs")
                for gidx in range(8):
                    src = scores_em[gidx * 8:(gidx + 1) * 8, :].rearrange(
                        "e (p f) -> p e f", p=128)
                    kb.inc(s.dma_start(out=bufA[:, :].rearrange(
                        "p (e f) -> p e f", f=256)[:, gidx * 8:(gidx + 1) * 8, :],
                        in_=src), "qs")

            @block.vector
            def _(v):
                kb.wait(v, "qs")
                kb.inc(v.memset(scr["bigpos"][:, :], BIGPOS), "dve")

            emit_sort(kb, block, bufA, bufB, tbufA, tbufB, ps[2], cst, nbat=E)
            emit_gs_bm(kb, block, bufA, gs, bm, nbat=E)
            emit_theta_sigma(kb, block, gs, bm, thA, thB, tbufA, tbufB, ps[2], cst,
                             sig_fm, thbc, sigbc, scr, dram)
            if dbg:
                @block.sync
                def _(s):
                    kb.wait_all(s)
                    kb.inc(s.dma_start(out=dbg["gs"], in_=gs[:, :]), "qs")
                    kb.inc(s.dma_start(out=dbg["bm"], in_=bm[:, :]), "qs")
                    kb.inc(s.dma_start(out=dbg["th"], in_=scr["th11"][:, :]), "qs")
                    kb.inc(s.dma_start(out=dbg["srt"], in_=bufA[:, :]), "qs")
                    kb.inc(s.dma_start(out=dbg["cand"], in_=scr["cand"][:, :]), "qs")
                    kb.inc(s.dma_start(out=dbg["emin"], in_=scr["emin"][:, :]), "qs")
                    kb.inc(s.dma_start(out=dbg["mask"], in_=scr["mask"][:, :]), "qs")
                    kb.inc(s.dma_start(out=dbg["sigbc"], in_=sigbc[:, :]), "qs")
            emit_fs(kb, block, scores_own, sigbc, fsfm, dram["fs"], ps, cst, fsb)

            for engname in ("vector", "gpsimd", "scalar", "tensor", "sync"):
                def mk(e, _n=engname):
                    kb.wait_all(e)
                getattr(block, engname)(mk)

        @block.sync
        def _(s):
            kb.wait(s, "qp")
            for e in range(E):
                for nm in ("Wup", "Wgate", "Wdown"):
                    kb.inc(s.dma_start(out=bufs_rb[:, :],
                                       in_=w16[nm][e].rearrange("p c h -> p (c h)")), "qs")

        bufs = {
            "xq": sbuf(kb, "xq", [128, 8 * 1024]),
            "yq": sbuf(kb, "yq", [128, 8 * 1024], BF16),
            "hgrp": sbuf(kb, "hgrp", [128, 16 * 1024], BF16),
            "dacc": sbuf(kb, "dacc", [128, 16 * 512]),
            "silu": sbuf(kb, "silu", [128, 2 * 512], BF16),
            "fsbc": sbuf(kb, "fsbc", [128, 2 * 1024], BF16),
            "swu": sbuf(kb, "swu", [128, 2 * 2048], BF16),
            "swg": sbuf(kb, "swg", [128, 2 * 2048], BF16),
            "swd": sbuf(kb, "swd", [128, 8 * 2048], BF16),
            "arow": sbuf(kb, "arow", [1, 1024]),
            "brow": sbuf(kb, "brow", [1, 1024]),
            "abc": sbuf(kb, "abc", [128, 1024]),
            "bbc": sbuf(kb, "bbc", [128, 1024]),
            "xsq": sbuf(kb, "xsq", [128, 512], BF16),
            "xbf": sbuf(kb, "xbf", [128, 512], BF16),
            "srow": sbuf(kb, "srow", [1, 1024]),
        }
        if dbg:
            @block.sync
            def _(s):
                kb.wait_all(s)
                kb.inc(s.dma_start(out=dbg["fs"], in_=fsfm[:, :]), "qs")
                kb.inc(s.dma_start(out=dbg["sig"], in_=sig_fm[:, :]), "qs")
        emit_ffn_sweep(kb, block, x_img, x3_img, dram["fs"], w16, bias16,
                       bdown_sb, fsfm, ps, cst, bufs, dram, dbg=dbg)

        @block.sync
        def _(s):
            kb.wait_all(s)
    return nc


def emit_scan(kb, block, at_d, bt_d, kt_d, rt_d, v_d, cexp_d, o_d, ps, cst, bufs,
              epilogue_fn=None, cexp_sb=None):
    NCH = T // 64
    SEG = 512
    NSEG = T // SEG
    CPS = SEG // 64

    a_s, b_s, k_s, r_s = bufs["a_s"], bufs["b_s"], bufs["k_s"], bufs["r_s"]
    v_s = bufs["v_s"]
    cexp = bufs["cexp"]
    yt = bufs["yt"]
    y0p = bufs["y0p"]
    bd = bufs["bd"]
    scmm = bufs["scmm"]
    lad = bufs["lad"]
    mrkt = bufs["mrkt"]
    mkav = bufs["mkav"]
    btm = bufs["btm"]
    xs = bufs["xs"]
    dlt = bufs["dlt"]
    o_tm = bufs["o_tm"]

    if cexp_sb is None:
        @block.sync
        def _(s):
            kb.wait_all(s)
            kb.inc(s.dma_start(out=cexp[:, :], in_=cexp_d), "qs")
    else:
        cexp = cexp_sb

    @block.vector
    def _(v):
        kb.wait(v, "qs")
        kb.inc(v.memset(yt[:, :], 0.0), "dve")

    @block.gpsimd
    def _(g):
        kb.wait(g, "qs")
        kb.inc(g.memset(bd[:, :], 0.0), "pool")

    for seg in range(NSEG):
        @block.sync
        def _(s, seg=seg):
            kb.wait_all(s, ("pe", "dve", "act", "pool", "qs"))
            for c in range(8):
                for nm, buf in ((at_d, a_s), (bt_d, b_s), (kt_d, k_s), (rt_d, r_s)):
                    kb.inc(s.dma_start(out=buf[:, c * SEG:(c + 1) * SEG],
                                       in_=nm[:, c * T + seg * SEG:c * T + (seg + 1) * SEG]), "qs")
            for st in range(SEG // 128):
                kb.inc(s.dma_start(out=v_s[:, st * C:(st + 1) * C],
                                   in_=v_d[seg * SEG + st * 128: seg * SEG + (st + 1) * 128, :]), "qs")

        for hp in range(8):
            for ci in range(CPS):
                ch = seg * CPS + ci
                tsl = slice(ci * 64, (ci + 1) * 64)

                @block.vector
                def _(v, hp=hp, ci=ci, tsl=tsl):
                    kb.wait(v, "qs")
                    kb.wait(v, "pe")
                    for bi, buf in enumerate((a_s, b_s, k_s, r_s)):
                        src = buf[:, hp * SEG:][:, tsl]
                        v.tensor_copy(bd[0:64, bi * 128:bi * 128 + 64], src[0:64, :])
                        v.tensor_copy(bd[64:128, bi * 128 + 64:bi * 128 + 128], src[64:128, :])
                    kb.inc(v.memset(cst["junk"][:, :], 0.0), "dve")

                Abd = bd[:, 0:128]
                Bbd = bd[:, 128:256]
                Kbd = bd[:, 256:384]
                Rbd = bd[:, 384:512]
                Vbd = bd[:, 512:640]

                st0 = (ci * 64) // 128
                r0 = (ci * 64) % 128
                par = r0 // 64
                vs_blk = v_s[:, st0 * C + hp * 128: st0 * C + (hp + 1) * 128]

                @block.tensor
                def _(t, vs_blk=vs_blk, par=par):
                    kb.wait(t, "dve")
                    t.matmul(ps[1][:, 0:64], lhsT=cst[f"vperm{par}a"][:, :],
                             rhs=vs_blk[:, 0:64], start=True, stop=True)
                    kb.inc(t.matmul(ps[1][:, 64:128], lhsT=cst[f"vperm{par}b"][:, :],
                                    rhs=vs_blk[:, 64:128], start=True, stop=True), "pe")

                @block.vector
                def _(v, Vbd=Vbd):
                    kb.wait(v, "pe")
                    kb.inc(v.tensor_copy(Vbd, ps[1][:, 0:128]), "dve")

                @block.tensor
                def _(t, hp=hp):
                    kb.wait(t, "dve")
                    t.matmul(ps[0][:, 0:128], lhsT=Abd, rhs=Bbd, start=True, stop=True)
                    t.matmul(ps[0][:, 128:256], lhsT=Bbd, rhs=Abd, start=True, stop=True)
                    t.matmul(ps[0][:, 256:384], lhsT=Kbd, rhs=Abd, start=True, stop=True)
                    kb.inc(t.matmul(ps[0][:, 384:512], lhsT=Bbd, rhs=Rbd, start=True, stop=True), "pe")

                @block.vector
                def _(v):
                    kb.wait(v, "pe")
                    kb.inc(v.tensor_tensor(scmm[:, :], ps[0][:, :], cst["mask4"][:, :],
                                           op=ALU.mult), "dve")

                M_, N_, Nka, MrbT = (scmm[:, 0:128], scmm[:, 128:256],
                                     scmm[:, 256:384], scmm[:, 384:512])

                @block.tensor
                def _(t):
                    kb.wait(t, "dve")
                    t.matmul(ps[1][:, 0:128], lhsT=Kbd, rhs=Rbd, start=True, stop=True)
                    t.matmul(ps[1][:, 128:256], lhsT=Nka, rhs=Vbd, start=True, stop=True)
                    t.transpose(bufs["ps16"][:, 0:128], in_=Bbd, identity=cst["ident16"][:, :])
                    kb.inc(t.transpose(bufs["ps16"][:, 128:256], in_=Kbd,
                                       identity=cst["ident16"][:, :]), "pe")

                @block.vector
                def _(v):
                    kb.wait(v, "pe")
                    v.tensor_tensor(mrkt[:, :], ps[1][:, 0:128], cst["masku_inc"][:, :],
                                    op=ALU.mult)
                    v.tensor_copy(mkav[:, :], ps[1][:, 128:256])
                    kb.inc(v.tensor_copy(btm[:, :], bufs["ps16"][:, 0:256]), "dve")

                lpairs = [(None, None)]
                prevM, prevN = M_, N_
                for li in range(5):
                    o0 = (li % 2) * 256

                    @block.tensor
                    def _(t, prevM=prevM, prevN=prevN, o0=o0, li=li):
                        kb.wait(t, "dve")
                        t.matmul(ps[2][:, o0:o0 + 128], lhsT=prevN, rhs=prevM,
                                 start=True, stop=True)
                        kb.inc(t.matmul(ps[2][:, o0 + 128:o0 + 256], lhsT=prevM, rhs=prevN,
                                        start=True, stop=True), "pe")

                    @block.vector
                    def _(v, li=li, o0=o0):
                        kb.wait(v, "pe")
                        kb.inc(v.tensor_copy(lad[:, li * 256:(li + 1) * 256],
                                             ps[2][:, o0:o0 + 256]), "dve")

                    prevM = lad[:, li * 256:li * 256 + 128]
                    prevN = lad[:, li * 256 + 128:(li + 1) * 256]

                yth = yt[:, hp * 128:(hp + 1) * 128]
                ce = cexp[:, hp * NCH + ch:hp * NCH + ch + 1]

                @block.vector
                def _(v, yth=yth, ce=ce):
                    kb.wait(v, "dve")
                    kb.inc(v.tensor_scalar_mul(y0p[:, :], yth, ce), "dve")

                @block.tensor
                def _(t):
                    kb.wait(t, "dve")
                    kb.inc(t.matmul(ps[3][:, 0:128], lhsT=Abd, rhs=y0p[:, :],
                                    start=True, stop=True), "pe")

                @block.vector
                def _(v):
                    kb.wait(v, "pe")
                    kb.inc(v.tensor_tensor(xs[:, :], ps[3][:, 0:128], mkav[:, :],
                                           op=ALU.add), "dve")

                for li in range(6):
                    if li == 5:
                        lhs = N_
                    else:
                        lhs = lad[:, (4 - li) * 256 + 128:(5 - li) * 256]
                    o0 = 128 + (li % 2) * 128

                    @block.tensor
                    def _(t, lhs=lhs, o0=o0):
                        kb.wait(t, "dve")
                        kb.inc(t.matmul(ps[3][:, o0:o0 + 128], lhsT=lhs, rhs=xs[:, :],
                                        start=True, stop=True), "pe")

                    @block.vector
                    def _(v, o0=o0):
                        kb.wait(v, "pe")
                        kb.inc(v.tensor_tensor(xs[:, :], ps[3][:, o0:o0 + 128], xs[:, :],
                                               op=ALU.add), "dve")

                @block.tensor
                def _(t):
                    kb.wait(t, "dve")
                    t.matmul(ps[4][:, 0:128], lhsT=Rbd, rhs=y0p[:, :], start=True, stop=False)
                    t.matmul(ps[4][:, 0:128], lhsT=MrbT, rhs=xs[:, :], start=False, stop=False)
                    kb.inc(t.matmul(ps[4][:, 0:128], lhsT=mrkt[:, :], rhs=Vbd,
                                    start=False, stop=True), "pe")

                @block.vector
                def _(v, par=par):
                    kb.wait(v, "pe")
                    kb.inc(v.tensor_copy(bufs["osb"][:, par * 128:(par + 1) * 128],
                                         ps[4][:, 0:128]), "dve")

                if par == 1:
                    @block.tensor
                    def _(t):
                        kb.wait(t, "dve")
                        t.matmul(ps[4][:, 128:192], lhsT=cst["operm0a"][:, :],
                                 rhs=bufs["osb"][:, 0:64], start=True, stop=False)
                        t.matmul(ps[4][:, 192:256], lhsT=cst["operm0b"][:, :],
                                 rhs=bufs["osb"][:, 64:128], start=False, stop=False)
                        t.matmul(ps[4][:, 128:192], lhsT=cst["operm1a"][:, :],
                                 rhs=bufs["osb"][:, 128:192], start=False, stop=False)
                        kb.inc(t.matmul(ps[4][:, 192:256], lhsT=cst["operm1b"][:, :],
                                        rhs=bufs["osb"][:, 192:256], start=False, stop=True), "pe")

                    @block.vector
                    def _(v, hp=hp, st0=st0):
                        kb.wait(v, "pe")
                        dst = o_tm[:, st0 * C + hp * 128: st0 * C + (hp + 1) * 128]
                        kb.inc(v.tensor_copy(dst, ps[4][:, 128:256]), "dve")

                @block.tensor
                def _(t):
                    kb.wait(t, "dve")
                    t.matmul(ps[5][:, 0:128], lhsT=btm[:, 0:128], rhs=xs[:, :],
                             start=True, stop=False)
                    kb.inc(t.matmul(ps[5][:, 0:128], lhsT=btm[:, 128:256], rhs=Vbd,
                                    start=False, stop=True), "pe")

                @block.vector
                def _(v, yth=yth, ce=ce):
                    kb.wait(v, "pe")
                    kb.inc(v.tensor_scalar_mul(dlt[:, :], ps[5][:, 0:128], ce), "dve")
                    kb.wait(v, "dve")
                    kb.inc(v.tensor_scalar_mul(yth, yth, ce), "dve")
                    kb.wait(v, "dve")
                    kb.inc(v.scalar_tensor_tensor(out=yth, in0=yth, scalar=ce,
                                                  in1=dlt[:, :], op0=ALU.mult,
                                                  op1=ALU.add), "dve")

        if epilogue_fn is None:
            @block.sync
            def _(s, seg=seg):
                kb.wait(s, "dve")
                for st in range(SEG // 128):
                    kb.inc(s.dma_start(out=o_d[seg * SEG + st * 128: seg * SEG + (st + 1) * 128, :],
                                       in_=o_tm[:, st * C:(st + 1) * C]), "qs")
        else:
            epilogue_fn(seg)


VEC_IDX = {"x_r": 0, "x_w": 1, "x_k": 2, "x_v": 3, "x_a": 4, "k_k": 5, "k_a": 6,
           "r_k": 7, "wbias": 8, "neg_wbias": 9, "abias": 10, "one_m_ka": 11}


def vcol(aw, name, c):
    return aw["vecs"][:, VEC_IDX[name] * 8 + c:VEC_IDX[name] * 8 + c + 1]


def emit_attn_prep(kb, block, x1_img, aw, ps, ps16, cst, bufs, dram, stop_after=None):
    W = 512
    NT = T // W
    A, Bb, Cc, Ee = bufs["fA"], bufs["fB"], bufs["fC"], bufs["fE"]
    b = [bufs[f"b{i}"] for i in range(6)]
    z64 = bufs["z64"]
    carry = bufs["carry"]
    cval = bufs["cval"]
    arow, brow = bufs["arow"], bufs["brow"]
    abc, bbc = bufs["abc"], bufs["bbc"]
    class _SrowView:
        pass
    cexp = bufs["cexp_sb"]
    bonus = bufs["bonus"]

    def CH(buf, c):
        return buf[:, c * W:(c + 1) * W]

    @block.vector
    def _(v):
        kb.wait_all(v)
        kb.inc(v.memset(carry[:, :], 0.0), "dve")

    for ti in range(NT):
        @block.sync
        def _(s, ti=ti):
            kb.wait_all(s, ("dve", "pe", "act", "pool", "qs"))
            for c in range(8):
                kb.inc(s.dma_start(out=CH(A, c),
                                   in_=x1_img[:, c * T + ti * W:c * T + (ti + 1) * W]), "qs")

        @block.scalar
        def _(sc):
            kb.wait(sc, "qs")
            kb.wait(sc, "pe")
            for c in range(8):
                kb.inc(sc.activation(out=CH(b[0], c), in_=CH(A, c), func=AF.Square), "act")

        @block.gpsimd
        def _(g):
            kb.wait(g, "qs")
            kb.wait(g, "pe")
            for c in range(8):
                kb.inc(g.tensor_copy(CH(b[5], c), CH(A, c)), "pool")

        @block.tensor
        def _(t):
            kb.wait(t, "act")
            kb.wait(t, "pool")
            kb.wait(t, "dve")
            for c in range(8):
                t.matmul(ps[2][0:1, :], lhsT=cst["ones1_16"][:, :], rhs=CH(b[5], c),
                         start=(c == 0), stop=(c == 7))
            for c in range(8):
                ins_ = t.matmul(ps[3][0:1, :], lhsT=cst["ones1_16"][:, :], rhs=CH(b[0], c),
                                start=(c == 0), stop=(c == 7))
            kb.inc(ins_, "pe")

        @block.vector
        def _(v):
            kb.wait(v, "pe")
            v.tensor_scalar_mul(brow[0:1, 512:1024], ps[2][0:1, :], 1.0 / C)
            kb.inc(v.tensor_scalar_mul(arow[0:1, 512:1024], ps[3][0:1, :], 1.0 / C), "dve")
            kb.wait(v, "dve")
            kb.inc(v.tensor_tensor(arow[0:1, 0:512], brow[0:1, 512:1024], brow[0:1, 512:1024],
                                   op=ALU.mult), "dve")
            kb.wait(v, "dve")
            kb.inc(v.tensor_tensor(arow[0:1, 0:512], arow[0:1, 512:1024], arow[0:1, 0:512],
                                   op=ALU.subtract), "dve")

        @block.scalar
        def _(sc):
            kb.wait(sc, "dve")
            kb.inc(sc.activation(out=brow[0:1, 0:512], in_=arow[0:1, 0:512], func=AF.Ln,
                                 bias=cst["eps1"][0:1, 0:1]), "act")
            kb.wait(sc, "act")
            kb.inc(sc.activation(out=arow[0:1, 0:512], in_=brow[0:1, 0:512], func=AF.Exp,
                                 scale=-0.5), "act")

        @block.vector
        def _(v):
            kb.wait(v, "act")
            kb.wait(v, "dve")
            kb.inc(v.tensor_tensor(brow[0:1, 0:512], brow[0:1, 512:1024], arow[0:1, 0:512],
                                   op=ALU.mult), "dve")

        @block.tensor
        def _(t):
            kb.wait(t, "dve")
            t.matmul(ps[2][:, :], lhsT=cst["onescol"][:, :], rhs=arow[0:1, 0:512],
                     start=True, stop=True)
            kb.inc(t.matmul(ps[3][:, :], lhsT=cst["onescol"][:, :], rhs=brow[0:1, 0:512],
                            start=True, stop=True), "pe")

        @block.vector
        def _(v):
            kb.wait(v, "pe")
            v.tensor_copy(abc[:, :], ps[2][:, :])
            kb.inc(v.tensor_copy(bbc[:, :], ps[3][:, :]), "dve")
            kb.wait(v, "dve")
            for c in range(8):
                kb.inc(v.tensor_tensor(CH(Bb, c), CH(A, c), abc[:, :], op=ALU.mult), "dve")
            kb.wait(v, "dve")
            for c in range(8):
                kb.inc(v.tensor_tensor(CH(Bb, c), CH(Bb, c), bbc[:, :], op=ALU.subtract), "dve")

        @block.vector
        def _(v, ti=ti):
            kb.wait(v, "dve")
            for c in range(8):
                v.tensor_tensor(Cc[:, c * W + 1:(c + 1) * W], Bb[:, c * W:(c + 1) * W - 1],
                                Bb[:, c * W + 1:(c + 1) * W], op=ALU.subtract)
            kb.inc(v.tensor_tensor(Cc[:, 0:4096:W], carry[:, :], Bb[:, 0:4096:W],
                                   op=ALU.subtract), "dve")
            kb.wait(v, "dve")
            kb.inc(v.tensor_copy(carry[:, :], Bb[:, W - 1:4096:W]), "dve")
            kb.wait(v, "dve")
            for vi, nm in enumerate(("x_r", "x_w", "x_k", "x_v", "x_a")):
                for c in range(8):
                    ins_ = v.scalar_tensor_tensor(
                        out=CH(b[vi], c), in0=CH(Cc, c), scalar=vcol(aw, nm, c),
                        in1=CH(Bb, c), op0=ALU.mult, op1=ALU.add)
            kb.inc(ins_, "dve")

        if stop_after == "mix":
            return
        for oc in range(8):
            @block.tensor
            def _(t, oc=oc):
                kb.wait(t, "dve")
                kb.wait(t, "act")
                for c in range(8):
                    ins_ = t.matmul(ps[0][:, :],
                                    lhsT=aw["wr16"][:, c * 1024 + oc * 128:c * 1024 + (oc + 1) * 128],
                                    rhs=CH(b[0], c), start=(c == 0), stop=(c == 7))
                kb.inc(ins_, "pe")

            @block.scalar
            def _(sc, oc=oc):
                kb.wait(sc, "pe")
                kb.inc(sc.copy(CH(b[5], oc), ps[0][:, :]), "act")

        if stop_after == "rproj":
            return
        @block.tensor
        def _(t):
            kb.wait(t, "dve")
            kb.wait(t, "act")
            for c in range(8):
                ins_ = t.matmul(ps[5][0:64, 0:512], lhsT=aw["wla16"][:, c * 64:(c + 1) * 64],
                                rhs=CH(b[1], c), start=(c == 0), stop=(c == 7))
            kb.inc(ins_, "pe")

        @block.scalar
        def _(sc):
            kb.wait(sc, "pe")
            kb.inc(sc.activation(out=z64, in_=ps[5][0:64, 0:512], func=AF.Tanh), "act")

        for wv in range(2):
            for oci in range(4):
                oc = wv * 4 + oci

                @block.tensor
                def _(t, oc=oc, wv=wv, oci=oci):
                    kb.wait(t, "act")
                    if wv > 0 and oci == 0:
                        kb.wait(t, "dve")
                    kb.inc(t.matmul(ps[5][:, 0:512], lhsT=aw["wlb16"][:, oc * 128:(oc + 1) * 128],
                                    rhs=z64, start=True, stop=True), "pe")

                @block.scalar
                def _(sc, oc=oc, oci=oci):
                    kb.wait(sc, "pe")
                    sc.activation(out=CH(Ee, oci), in_=ps[5][:, 0:512], func=AF.Relu,
                                  scale=-1.0, bias=vcol(aw, "neg_wbias", oc))
                    kb.inc(sc.activation(out=CH(A, oc), in_=ps[5][:, 0:512],
                                         func=AF.Abs, bias=vcol(aw, "wbias", oc)), "act")

            @block.scalar
            def _(sc, wv=wv):
                kb.wait(sc, "act")
                for oci in range(4):
                    kb.inc(sc.activation(out=CH(A, wv * 4 + oci), in_=CH(A, wv * 4 + oci),
                                         func=AF.Exp, scale=-1.0), "act")
                kb.wait(sc, "act")
                for oci in range(4):
                    kb.inc(sc.activation(out=CH(A, wv * 4 + oci), in_=CH(A, wv * 4 + oci),
                                         func=AF.Ln, bias=1.0), "act")

            @block.vector
            def _(v, wv=wv):
                kb.wait(v, "act")
                for oci in range(4):
                    kb.inc(v.tensor_tensor(CH(A, wv * 4 + oci), CH(A, wv * 4 + oci),
                                           CH(Ee, oci), op=ALU.add), "dve")
                kb.wait(v, "dve")
                for oci in range(4):
                    kb.inc(v.tensor_scalar(CH(A, wv * 4 + oci), CH(A, wv * 4 + oci), -1.0, -0.5,
                                           op0=ALU.mult, op1=ALU.add), "dve")

        if stop_after == "wfm":
            return
        @block.vector
        def _(v, ti=ti):
            kb.wait(v, "dve")
            for c in range(8):
                for ch in range(8):
                    ins_ = v.tensor_tensor_scan(
                        out=Cc[:, c * W + ch * 64:c * W + (ch + 1) * 64],
                        data0=cst["ones64"][:, :],
                        data1=A[:, c * W + ch * 64:c * W + (ch + 1) * 64],
                        initial=0.0, op0=ALU.mult, op1=ALU.add)
            kb.inc(ins_, "dve")
            kb.wait(v, "dve")
            kb.inc(v.tensor_scalar_mul(
                cval[:, :], Cc[:, :].rearrange("p (cc t) -> p cc t", t=64)[:, :, 63], 0.5), "dve")

        @block.scalar
        def _(sc, ti=ti):
            kb.wait(sc, "dve")
            kb.inc(sc.activation(
                out=cexp[:, :].rearrange("p (cc n) -> p cc n", n=64)[:, :, ti * 8:(ti + 1) * 8],
                in_=cval[:, :].rearrange("p (cc n) -> p cc n", n=8), func=AF.Exp), "act")

        if stop_after == "cum":
            return
        for c in range(8):
            @block.vector
            def _(v, c=c):
                kb.wait(v, "dve")
                if c == 0:
                    kb.wait(v, "act")
                cv = cval[:, c * 8:(c + 1) * 8]
                g_ = Cc[:, c * W:(c + 1) * W].rearrange("p (ch t) -> p ch t", t=64)
                w_ = A[:, c * W:(c + 1) * W].rearrange("p (ch t) -> p ch t", t=64)
                e_ = Bb[:, c * W:(c + 1) * W].rearrange("p (ch t) -> p ch t", t=64)
                cbc = cv.unsqueeze(2).to_broadcast([128, 8, 64])
                kb.inc(v.tensor_tensor(e_, g_, w_, op=ALU.subtract), "dve")
                kb.wait(v, "dve")
                kb.inc(v.tensor_tensor(e_, e_, cbc, op=ALU.subtract), "dve")

        if stop_after == "efa_arg":
            return
        @block.tensor
        def _(t):
            kb.wait(t, "dve")
            kb.wait(t, "act")
            for c in range(8):
                ins_ = t.matmul(ps[5][0:64, 0:512], lhsT=aw["ala16"][:, c * 64:(c + 1) * 64],
                                rhs=CH(b[4], c), start=(c == 0), stop=(c == 7))
            kb.inc(ins_, "pe")

        @block.scalar
        def _(sc):
            kb.wait(sc, "pe")
            kb.inc(sc.copy(z64, ps[5][0:64, 0:512]), "act")

        for oc in range(8):
            @block.tensor
            def _(t, oc=oc):
                kb.wait(t, "act")
                kb.inc(t.matmul(ps[5][:, 0:512], lhsT=aw["alb16"][:, oc * 128:(oc + 1) * 128],
                                rhs=z64, start=True, stop=True), "pe")

            @block.scalar
            def _(sc, oc=oc):
                kb.wait(sc, "pe")
                kb.inc(sc.activation(out=CH(b[1], oc), in_=ps[5][:, 0:512],
                                     func=AF.Sigmoid, bias=vcol(aw, "abias", oc)), "act")

        if stop_after == "alora":
            return
        for oc in range(8):
            @block.tensor
            def _(t, oc=oc):
                kb.wait(t, "dve")
                kb.wait(t, "act")
                for c in range(8):
                    ins_ = t.matmul(ps[0][:, :],
                                    lhsT=aw["wk16"][:, c * 1024 + oc * 128:c * 1024 + (oc + 1) * 128],
                                    rhs=CH(b[2], c), start=(c == 0), stop=(c == 7))
                kb.inc(ins_, "pe")

            @block.scalar
            def _(sc, oc=oc):
                kb.wait(sc, "pe")
                kb.inc(sc.copy(CH(A, oc), ps[0][:, :]), "act")

        if stop_after == "kproj":
            return
        for c in range(8):
            @block.vector
            def _(v, c=c):
                kb.wait(v, "act")
                if c == 0:
                    kb.wait(v, "dve")
                kb.inc(v.tensor_scalar_mul(CH(Ee, c % 4), CH(A, c), vcol(aw, "k_k", c)), "dve")

            @block.scalar
            def _(sc, c=c):
                kb.wait(sc, "dve")
                kb.inc(sc.activation(out=CH(b[0], c), in_=CH(Ee, c % 4), func=AF.Square), "act")

        @block.tensor
        def _(t):
            kb.wait(t, "act")
            for c in range(8):
                ins_ = t.matmul(ps[5][0:16, 0:512], lhsT=aw[f"bdsel{c}"][:, :],
                                rhs=CH(b[0], c), start=(c == 0), stop=(c == 7))
            kb.inc(ins_, "pe")

        @block.vector
        def _(v):
            kb.wait(v, "pe")
            kb.inc(v.tensor_scalar_max(abc[0:16, 0:512], ps[5][0:16, 0:512], 1e-24), "dve")

        @block.scalar
        def _(sc):
            kb.wait(sc, "dve")
            kb.inc(sc.activation(out=bbc[0:16, 0:512], in_=abc[0:16, 0:512],
                                 func=AF.Ln), "act")
            kb.wait(sc, "act")
            kb.inc(sc.activation(out=b[0][0:16, 0:512], in_=bbc[0:16, 0:512], func=AF.Exp,
                                 scale=-0.5), "act")

        for c in range(8):
            @block.tensor
            def _(t, c=c):
                kb.wait(t, "act")
                if c > 0:
                    kb.wait(t, "dve")
                kb.inc(t.matmul(ps[5][:, 0:512], lhsT=aw[f"hsel{c}"][:, :],
                                rhs=b[0][0:16, 0:512], start=True, stop=True), "pe")

            @block.vector
            def _(v, c=c):
                kb.wait(v, "pe")
                kb.inc(v.tensor_scalar_mul(CH(Ee, c % 4), CH(A, c), vcol(aw, "k_k", c)), "dve")
                kb.wait(v, "dve")
                kb.inc(v.tensor_tensor(CH(b[2], c), CH(Ee, c % 4), ps[5][:, 0:512],
                                       op=ALU.mult), "dve")

        if stop_after == "t0":
            return
        for st in range(4):
            for nh2 in range(2):
                @block.tensor
                def _(t, st=st, nh2=nh2):
                    kb.wait(t, "dve")
                    if st == 0 and nh2 == 0:
                        kb.wait(t, "act")
                    for c in range(8):
                        ins_ = t.matmul(ps[4][:, 0:512],
                                        lhsT=b[3][:, c * W + st * 128:c * W + (st + 1) * 128],
                                        rhs=aw["wv16"][:, c * 1024 + nh2 * 512:c * 1024 + (nh2 + 1) * 512],
                                        start=(c == 0), stop=(c == 7))
                    kb.inc(ins_, "pe")

                @block.vector
                def _(v, st=st, nh2=nh2):
                    kb.wait(v, "pe")
                    kb.inc(v.tensor_copy(
                        b[4][:, st * 1024 + nh2 * 512:st * 1024 + (nh2 + 1) * 512],
                        ps[4][:, 0:512]), "dve")

        @block.sync
        def _(s, ti=ti):
            kb.wait(s, "dve")
            for st in range(4):
                kb.inc(s.dma_start(out=dram["v_d"][ti * 512 + st * 128: ti * 512 + (st + 1) * 128, :],
                                   in_=b[4][:, st * 1024:(st + 1) * 1024]), "qs")

        if stop_after == "vproj":
            return
        @block.tensor
        def _(t):
            kb.wait(t, "dve")
            for c in range(8):
                ins_ = t.matmul(ps[5][:, 0:512], lhsT=aw["gla16"][:, c * 128:(c + 1) * 128],
                                rhs=CH(b[5], c), start=(c == 0), stop=(c == 7))
            kb.inc(ins_, "pe")

        @block.scalar
        def _(sc):
            kb.wait(sc, "pe")
            kb.inc(sc.activation(out=b[0][:, 0:512], in_=ps[5][:, 0:512], func=AF.Sigmoid), "act")

        for st in range(4):
            for nh2 in range(2):
                @block.tensor
                def _(t, st=st, nh2=nh2):
                    kb.wait(t, "act")
                    if st > 0 or nh2 > 0:
                        kb.wait(t, "dve")
                    t.matmul(ps[4][:, 0:512],
                             lhsT=b[0][:, st * 128:(st + 1) * 128],
                             rhs=aw["glb16"][:, nh2 * 512:(nh2 + 1) * 512],
                             start=True, stop=False)
                    ins_ = t.matmul(ps[4][:, 0:512],
                                    lhsT=cst["onescol16"][0:1, 0:128],
                                    rhs=aw["gbias_row16"][0:1, nh2 * 512:(nh2 + 1) * 512],
                                    start=False, stop=True)
                    kb.inc(ins_, "pe")

                @block.vector
                def _(v, st=st, nh2=nh2):
                    kb.wait(v, "pe")
                    kb.inc(v.tensor_copy(
                        b[3][:, st * 1024 + nh2 * 512:st * 1024 + (nh2 + 1) * 512],
                        ps[4][:, 0:512]), "dve")

        @block.sync
        def _(s, ti=ti):
            kb.wait(s, "dve")
            for st in range(4):
                kb.inc(s.dma_start(out=dram["g_d"][ti * 512 + st * 128: ti * 512 + (st + 1) * 128, :],
                                   in_=b[3][:, st * 1024:(st + 1) * 1024]), "qs")

        if stop_after == "ggate":
            return
        def ef_arg(v, c, mode):
            cv = cval[:, c * 8:(c + 1) * 8]
            g_ = Cc[:, c * W:(c + 1) * W].rearrange("p (ch t) -> p ch t", t=64)
            e_ = Bb[:, c * W:(c + 1) * W].rearrange("p (ch t) -> p ch t", t=64)
            w_ = A
            cbc = cv.unsqueeze(2).to_broadcast([128, 8, 64])
            if mode == "b":
                kb.inc(v.tensor_tensor(e_, cbc, g_, op=ALU.subtract), "dve")
            else:
                kb.inc(v.tensor_tensor(e_, g_, cbc, op=ALU.subtract), "dve")

        for c in range(8):
            @block.scalar
            def _(sc, c=c):
                kb.wait(sc, "dve")
                if c == 0:
                    kb.wait(sc, "qs")
                kb.inc(sc.activation(out=CH(Bb, c), in_=CH(Bb, c), func=AF.Exp), "act")

            @block.vector
            def _(v, c=c):
                kb.wait(v, "act")
                kb.inc(v.scalar_tensor_tensor(out=CH(b[0], c), in0=CH(b[2], c), scalar=-1.0,
                                              in1=CH(Bb, c), op0=ALU.mult, op1=ALU.mult), "dve")

        @block.sync
        def _(s, ti=ti):
            kb.wait(s, "dve")
            for c in range(8):
                kb.inc(s.dma_start(out=dram["at"][:, c * T + ti * W:c * T + (ti + 1) * W],
                                   in_=CH(b[0], c)), "qs")

        if stop_after == "efa":
            return
        for c in range(8):
            @block.vector
            def _(v, c=c):
                kb.wait(v, "dve")
                kb.wait(v, "qs")
                ef_arg(v, c, "b")

            @block.scalar
            def _(sc, c=c):
                kb.wait(sc, "dve")
                kb.inc(sc.activation(out=CH(Bb, c), in_=CH(Bb, c), func=AF.Exp), "act")

            @block.vector
            def _(v, c=c):
                kb.wait(v, "act")
                kb.wait(v, "dve")
                kb.inc(v.tensor_tensor(CH(Ee, c % 4), CH(b[2], c), CH(b[1], c), op=ALU.mult), "dve")
                kb.wait(v, "dve")
                kb.inc(v.tensor_tensor(CH(b[0], c), CH(Ee, c % 4), CH(Bb, c), op=ALU.mult), "dve")
                kb.wait(v, "dve")
                kb.inc(v.tensor_scalar(CH(Ee, c % 4), CH(b[1], c), vcol(aw, "k_a", c),
                                       vcol(aw, "one_m_ka", c), op0=ALU.mult, op1=ALU.add), "dve")
                kb.wait(v, "dve")
                kb.inc(v.tensor_tensor(CH(A, c), CH(A, c), CH(Ee, c % 4), op=ALU.mult), "dve")
                kb.wait(v, "dve")
                kb.inc(v.tensor_tensor(CH(b[4], c), CH(A, c), CH(Bb, c), op=ALU.mult), "dve")

        @block.sync
        def _(s, ti=ti):
            kb.wait(s, "dve")
            for c in range(8):
                kb.inc(s.dma_start(out=dram["bt"][:, c * T + ti * W:c * T + (ti + 1) * W],
                                   in_=CH(b[0], c)), "qs")
                kb.inc(s.dma_start(out=dram["kt"][:, c * T + ti * W:c * T + (ti + 1) * W],
                                   in_=CH(b[4], c)), "qs")

        if stop_after == "efb":
            return
        for c in range(8):
            @block.vector
            def _(v, c=c):
                kb.wait(v, "dve")
                kb.wait(v, "qs")
                ef_arg(v, c, "r")

            @block.scalar
            def _(sc, c=c):
                kb.wait(sc, "dve")
                kb.inc(sc.activation(out=CH(Bb, c), in_=CH(Bb, c), func=AF.Exp), "act")

            @block.vector
            def _(v, c=c):
                kb.wait(v, "act")
                kb.wait(v, "dve")
                kb.inc(v.tensor_tensor(CH(b[0], c), CH(b[5], c), CH(Bb, c), op=ALU.mult), "dve")
                kb.wait(v, "dve")
                kb.inc(v.tensor_tensor(CH(Ee, c % 4), CH(b[5], c), CH(A, c), op=ALU.mult), "dve")
                kb.wait(v, "dve")
                kb.inc(v.tensor_scalar_mul(CH(b[4], c), CH(Ee, c % 4), vcol(aw, "r_k", c)), "dve")

        @block.sync
        def _(s, ti=ti):
            kb.wait(s, "dve")
            for c in range(8):
                kb.inc(s.dma_start(out=dram["rt"][:, c * T + ti * W:c * T + (ti + 1) * W],
                                   in_=CH(b[0], c)), "qs")

        @block.tensor
        def _(t, ti=ti):
            kb.wait(t, "dve")
            for c in range(8):
                ins_ = t.matmul(ps[5][0:16, 0:512], lhsT=aw[f"bdsel{c}"][:, :],
                                rhs=CH(b[4], c), start=(c == 0), stop=(c == 7))
            kb.inc(ins_, "pe")

        @block.vector
        def _(v, ti=ti):
            kb.wait(v, "pe")
            kb.inc(v.tensor_copy(bonus[:, ti * W:(ti + 1) * W], ps[5][0:16, 0:512]), "dve")


def emit_attn_setup(kb, block, nc, a_in, aw, aw16_dram, ps, cst, rbbuf):
    @block.gpsimd
    def _(g):
        for nm, src, csz in (("wr16", a_in["W_r"], 1024), ("wk16", a_in["W_k"], 1024),
                             ("wv16", a_in["W_v"], 1024), ("wo16", a_in["W_o"], 1024),
                             ("wla16", a_in["w_lora_a"], 64), ("ala16", a_in["a_lora_a"], 64),
                             ("gla16", a_in["g_lora_a"], 128)):
            kb.inc(g.dma_start(out=aw16_dram[nm],
                               in_=src.rearrange("(c p) n -> p c n", p=128)), "qp")
        for nm, src in (("wlb16", a_in["w_lora_b"]), ("alb16", a_in["a_lora_b"]),
                        ("glb16", a_in["g_lora_b"])):
            kb.inc(g.dma_start(out=aw16_dram[nm], in_=src), "qp")
        kb.inc(g.dma_start(out=aw16_dram["gbias"], in_=a_in["g_lora_bias"].unsqueeze(0)), "qp")

    @block.sync
    def _(s):
        kb.wait(s, "qp")
        for nm in ("wr16", "wk16", "wv16", "wla16", "ala16", "gla16"):
            kb.inc(s.dma_start(out=aw[nm][:, :],
                               in_=aw16_dram[nm].rearrange("p a b -> p (a b)")), "qs")
        for nm in ("wlb16", "alb16", "glb16"):
            kb.inc(s.dma_start(out=aw[nm][:, :], in_=aw16_dram[nm]), "qs")
        kb.inc(s.dma_start(out=aw["gbias_row16"][:, :], in_=aw16_dram["gbias"]), "qs")
        kb.wait(s, "qs")
        for nm in ("wr16", "wk16", "wv16", "wla16", "ala16", "gla16"):
            kb.inc(s.dma_start(out=aw[nm][:, :],
                               in_=aw16_dram[nm].rearrange("p a b -> p (a b)")), "qs")
        for nm in ("wlb16", "alb16", "glb16"):
            kb.inc(s.dma_start(out=aw[nm][:, :], in_=aw16_dram[nm]), "qs")
        kb.inc(s.dma_start(out=aw["gbias_row16"][:, :], in_=aw16_dram["gbias"]), "qs")
        vl = [("x_r", a_in["x_r"]), ("x_w", a_in["x_w"]), ("x_k", a_in["x_k"]),
              ("x_v", a_in["x_v"]), ("x_a", a_in["x_a"]), ("k_k", a_in["k_k"]),
              ("k_a", a_in["k_a"]), ("r_k", a_in["r_k"]),
              ("wbias", a_in["w_lora_bias"]), ("abias", a_in["a_lora_bias"])]
        for nm, src in vl:
            vi = VEC_IDX[nm]
            kb.inc(s.dma_start(out=aw["vstage"][vi * 8:(vi + 1) * 8, :],
                               in_=src.rearrange("(c p) -> c p", p=128) if len(src.shape) == 1
                               else src.rearrange("h d -> (h d)").rearrange("(c p) -> c p", p=128)), "qs")
        kb.inc(s.dma_start(out=aw["gnwrow"][0:1, :], in_=a_in["gn_weight"].unsqueeze(0)), "qs")
        kb.inc(s.dma_start(out=aw["gnbrow"][0:1, :], in_=a_in["gn_bias"].unsqueeze(0)), "qs")

    @block.tensor
    def _(t):
        kb.wait(t, "qs")
        kb.inc(t.transpose(ps[0][:, 0:128], in_=aw["vstage"][:, :],
                           identity=cst["ident"][:, :]), "pe")

    @block.vector
    def _(v):
        kb.wait(v, "pe")
        kb.inc(v.tensor_copy(aw["vecs"][:, 0:96], ps[0][:, 0:96]), "dve")
        kb.wait(v, "dve")
        kb.inc(v.tensor_scalar_mul(aw["vecs"][:, VEC_IDX["neg_wbias"] * 8:VEC_IDX["neg_wbias"] * 8 + 8],
                                   aw["vecs"][:, VEC_IDX["wbias"] * 8:VEC_IDX["wbias"] * 8 + 8], -1.0), "dve")
        kb.wait(v, "dve")
        kb.inc(v.tensor_scalar(aw["vecs"][:, VEC_IDX["one_m_ka"] * 8:VEC_IDX["one_m_ka"] * 8 + 8],
                               aw["vecs"][:, VEC_IDX["k_a"] * 8:VEC_IDX["k_a"] * 8 + 8],
                               -1.0, 1.0, op0=ALU.mult, op1=ALU.add), "dve")
        kb.wait(v, "dve")
        kb.inc(v.memset(cst["junk"][:, :], 0.0), "dve")


def emit_gnbc(kb, block, aw, ps, cst):
    for half in range(2):
        @block.tensor
        def _(t, half=half):
            kb.wait(t, "dve")
            kb.wait(t, "act")
            t.matmul(ps[1][:, :], lhsT=cst["onescol"][:, :],
                     rhs=aw["gnwrow"][0:1, half * 512:(half + 1) * 512], start=True, stop=True)
            kb.inc(t.matmul(ps[2][:, :], lhsT=cst["onescol"][:, :],
                            rhs=aw["gnbrow"][0:1, half * 512:(half + 1) * 512],
                            start=True, stop=True), "pe")

        @block.vector
        def _(v, half=half):
            kb.wait(v, "pe")
            v.tensor_copy(aw["gnwbc"][:, half * 512:(half + 1) * 512], ps[1][:, :])
            kb.inc(v.tensor_copy(aw["gnbbc"][:, half * 512:(half + 1) * 512], ps[2][:, :]), "dve")


def make_attn_epilogue(kb, block, aw, ps, ps16, cst, sb2, dram, x1_img, x2_img):
    o_tm, v_s, bonus = sb2["o_tm"], sb2["v_s"], sb2["bonus"]
    ep_g, ep_st, ep_st2 = sb2["ep_g"], sb2["ep_st"], sb2["ep_st2"]
    ep_sq, ep_bt = sb2["ep_sq"], sb2["ep_bt"]
    ep_ogf, ep_x1 = sb2["ep_ogf"], sb2["ep_x1"]
    SEG = 512

    def epi(seg):
        tok0 = seg * SEG

        for st in range(4):
            @block.sync
            def _(s, st=st):
                kb.wait(s, "dve")
                kb.inc(s.dma_start(out=ep_g[:, 0:1024],
                                   in_=dram["g_d"][tok0 + st * 128: tok0 + (st + 1) * 128, :]),
                       "qs")
            o = o_tm[:, st * C:(st + 1) * C]
            ov = o.rearrange("p (h i) -> p h i", i=64)

            @block.tensor
            def _(t, st=st):
                kb.wait(t, "dve")
                kb.inc(t.transpose(ps16[:, 0:16],
                                   in_=bonus[0:16, tok0 + st * 128: tok0 + (st + 1) * 128],
                                   identity=cst["ident16"][0:16, 0:16]), "pe")

            @block.vector
            def _(v, st=st):
                kb.wait(v, "pe")
                kb.inc(v.tensor_copy(ep_bt[:, st * 16:(st + 1) * 16], ps16[:, 0:16]), "dve")

            @block.scalar
            def _(sc, o=o):
                kb.wait(sc, "dve")
                kb.inc(sc.activation(out=ep_sq[:, :], in_=o, func=AF.Square), "act")

            @block.vector
            def _(v, o=o, ov=ov):
                kb.wait(v, "dve")
                kb.inc(v.reduce_sum(ep_st[:, :], ov, axis=AX.X), "dve")
                kb.wait(v, "act")
                kb.inc(v.reduce_sum(ep_st2[:, :],
                                    ep_sq[:, :].rearrange("p (h i) -> p h i", i=64),
                                    axis=AX.X), "dve")
                kb.wait(v, "dve")
                kb.inc(v.tensor_scalar_mul(ep_st[:, :], ep_st[:, :], 1.0 / 64), "dve")
                kb.wait(v, "dve")
                kb.inc(v.tensor_scalar_mul(ep_st2[:, :], ep_st2[:, :], 1.0 / 64), "dve")
                kb.wait(v, "dve")
                kb.inc(v.tensor_tensor(ep_sq[:, 0:16], ep_st[:, :], ep_st[:, :],
                                       op=ALU.mult), "dve")
                kb.wait(v, "dve")
                kb.inc(v.tensor_tensor(ep_st2[:, :], ep_st2[:, :], ep_sq[:, 0:16],
                                       op=ALU.subtract), "dve")

            @block.scalar
            def _(sc):
                kb.wait(sc, "dve")
                kb.inc(sc.activation(out=ep_st2[:, :], in_=ep_st2[:, :], func=AF.Ln,
                                     bias=cst["eps1"][:, 0:1]), "act")
                kb.wait(sc, "act")
                kb.inc(sc.activation(out=ep_st2[:, :], in_=ep_st2[:, :], func=AF.Exp,
                                     scale=-0.5), "act")

            @block.vector
            def _(v, st=st, o=o, ov=ov):
                kb.wait(v, "act")
                kb.wait(v, "qs")
                mbc = ep_st[:, :].unsqueeze(2).to_broadcast([128, 16, 64])
                rbc = ep_st2[:, :].unsqueeze(2).to_broadcast([128, 16, 64])
                bbc_ = ep_bt[:, st * 16:(st + 1) * 16].unsqueeze(2).to_broadcast([128, 16, 64])
                vv_ = v_s[:, st * C:(st + 1) * C].rearrange("p (h i) -> p h i", i=64)
                kb.inc(v.tensor_tensor(ov, ov, mbc, op=ALU.subtract), "dve")
                kb.wait(v, "dve")
                kb.inc(v.tensor_tensor(ov, ov, rbc, op=ALU.mult), "dve")
                kb.wait(v, "dve")
                kb.inc(v.tensor_tensor(o, o, aw["gnwbc"][:, :], op=ALU.mult), "dve")
                kb.wait(v, "dve")
                kb.inc(v.tensor_tensor(o, o, aw["gnbbc"][:, :], op=ALU.add), "dve")
                kb.wait(v, "dve")
                kb.inc(v.tensor_tensor(ep_sq[:, :].rearrange("p (h i) -> p h i", i=64),
                                       vv_, bbc_, op=ALU.mult), "dve")
                kb.wait(v, "dve")
                kb.inc(v.tensor_tensor(o, o, ep_sq[:, :], op=ALU.add), "dve")
                kb.wait(v, "dve")
                kb.inc(v.tensor_tensor(o, o, ep_g[:, 0:1024], op=ALU.mult), "dve")

            for half in range(2):
                @block.tensor
                def _(t, st=st, half=half):
                    kb.wait(t, "dve")
                    for cc in range(4):
                        c = half * 4 + cc
                        ins_ = t.transpose(ps[half][:, cc * 128:(cc + 1) * 128],
                                           in_=o_tm[:, st * C + c * 128: st * C + (c + 1) * 128],
                                           identity=cst["ident"][:, :])
                    kb.inc(ins_, "pe")

                @block.vector
                def _(v, st=st, half=half):
                    kb.wait(v, "pe")
                    for cc in range(4):
                        c = half * 4 + cc
                        ins_ = v.tensor_copy(
                            ep_ogf[:, c * 512 + st * 128: c * 512 + (st + 1) * 128],
                            ps[half][:, cc * 128:(cc + 1) * 128])
                    kb.inc(ins_, "dve")

        for oc in range(8):
            @block.sync
            def _(s, oc=oc):
                kb.wait(s, "dve")
                kb.inc(s.dma_start(out=ep_x1[:, (oc % 4) * 512:(oc % 4 + 1) * 512],
                                   in_=x1_img[:, oc * T + tok0: oc * T + tok0 + SEG]), "qs")

            @block.tensor
            def _(t, oc=oc):
                kb.wait(t, "dve")
                bank = ps[2 + oc % 2]
                for c in range(8):
                    ins_ = t.matmul(bank[:, 0:512],
                                    lhsT=aw["wo16"][:, c * 1024 + oc * 128: c * 1024 + (oc + 1) * 128],
                                    rhs=ep_ogf[:, c * 512:(c + 1) * 512],
                                    start=(c == 0), stop=(c == 7))
                kb.inc(ins_, "pe")

            @block.vector
            def _(v, oc=oc):
                kb.wait(v, "pe")
                kb.wait(v, "qs")
                sl = ep_x1[:, (oc % 4) * 512:(oc % 4 + 1) * 512]
                kb.inc(v.tensor_tensor(sl, ps[2 + oc % 2][:, 0:512], sl, op=ALU.add), "dve")

            @block.sync
            def _(s, oc=oc):
                kb.wait(s, "dve")
                kb.inc(s.dma_start(out=x2_img[:, oc * T + tok0: oc * T + tok0 + SEG],
                                   in_=ep_x1[:, (oc % 4) * 512:(oc % 4 + 1) * 512]), "qs")

    return epi


def emit_router(kb, block, x_src, sc_out, rw, ps, cst, bufs):
    rx, rsq, rst, rl = bufs["rx"], bufs["rsq"], bufs["rst"], bufs["rl"]
    for ti in range(8):
        @block.sync
        def _(s, ti=ti):
            kb.wait(s, "pe")
            kb.wait(s, "act")
            for c in range(8):
                kb.inc(s.dma_start(out=rx[:, c * 512:(c + 1) * 512],
                                   in_=x_src[:, c * T + ti * 512:c * T + (ti + 1) * 512]),
                       "qs")

        for st in range(4):
            @block.tensor
            def _(t, ti=ti, st=st):
                kb.wait(t, "qs")
                kb.wait(t, "dve")
                for c in range(8):
                    t.matmul(ps[0][:, 0:64],
                             lhsT=rx[:, c * 512 + st * 128:c * 512 + (st + 1) * 128],
                             rhs=rw["wr_sb"][:, c * 64:(c + 1) * 64],
                             start=(c == 0), stop=(c == 7))
                for c in range(8):
                    ins_ = t.matmul(ps[1][:, 0:1],
                                    lhsT=rx[:, c * 512 + st * 128:c * 512 + (st + 1) * 128],
                                    rhs=cst["ones1f"][:, :],
                                    start=(c == 0), stop=(c == 7))
                kb.inc(ins_, "pe")

            @block.scalar
            def _(sc, ti=ti, st=st):
                kb.wait(sc, "pe")
                for c in range(8):
                    kb.inc(sc.activation(
                        out=rsq[:, c * 128:(c + 1) * 128],
                        in_=rx[:, c * 512 + st * 128:c * 512 + (st + 1) * 128],
                        func=AF.Square), "act")

            @block.tensor
            def _(t, st=st):
                kb.wait(t, "act")
                for c in range(8):
                    ins_ = t.matmul(ps[1][:, 1:2],
                                    lhsT=rsq[:, c * 128:(c + 1) * 128],
                                    rhs=cst["ones1f"][:, :],
                                    start=(c == 0), stop=(c == 7))
                kb.inc(ins_, "pe")

            @block.vector
            def _(v, st=st):
                kb.wait(v, "pe")
                kb.inc(v.tensor_copy(rl[:, :], ps[0][:, 0:64]), "dve")
                kb.inc(v.tensor_scalar_mul(rst[:, 0:1], ps[1][:, 0:1], 1.0 / C), "dve")
                kb.inc(v.tensor_scalar_mul(rst[:, 5:6], ps[1][:, 1:2], 1.0 / C), "dve")
                kb.wait(v, "dve")
                kb.inc(v.tensor_tensor(rst[:, 2:3], rst[:, 0:1], rst[:, 0:1],
                                       op=ALU.mult), "dve")
                kb.wait(v, "dve")
                kb.inc(v.tensor_tensor(rst[:, 2:3], rst[:, 5:6], rst[:, 2:3],
                                       op=ALU.subtract), "dve")

            @block.scalar
            def _(sc):
                kb.wait(sc, "dve")
                kb.inc(sc.activation(out=rst[:, 1:2], in_=rst[:, 2:3], func=AF.Ln,
                                     bias=cst["eps1"][:, 0:1]), "act")
                kb.wait(sc, "act")
                kb.inc(sc.activation(out=rst[:, 1:2], in_=rst[:, 1:2], func=AF.Exp,
                                     scale=-0.5), "act")

            @block.vector
            def _(v):
                kb.wait(v, "act")
                kb.inc(v.tensor_scalar_mul(rl[:, :], rl[:, :], rst[:, 1:2]), "dve")
                kb.wait(v, "dve")
                kb.inc(v.scalar_tensor_tensor(out=rst[:, 2:3], in0=rst[:, 0:1],
                                              scalar=-1.0, in1=rst[:, 1:2],
                                              op0=ALU.mult, op1=ALU.mult), "dve")
                kb.wait(v, "dve")
                kb.inc(v.scalar_tensor_tensor(out=rl[:, :], in0=rw["wrcsbc"][:, :],
                                              scalar=rst[:, 2:3], in1=rl[:, :],
                                              op0=ALU.mult, op1=ALU.add), "dve")
                kb.wait(v, "dve")
                kb.inc(v.tensor_tensor(rl[:, :], rl[:, :], rw["brbc"][:, :],
                                       op=ALU.add), "dve")
                kb.wait(v, "dve")
                kb.inc(v.reduce_max(rst[:, 3:4], rl[:, :], axis=AX.X), "dve")
                kb.wait(v, "dve")
                kb.inc(v.tensor_scalar_mul(rst[:, 4:5], rst[:, 3:4], -1.0), "dve")

            @block.scalar
            def _(sc):
                kb.wait(sc, "dve")
                kb.inc(sc.activation(out=rl[:, :], in_=rl[:, :], func=AF.Exp,
                                     bias=rst[:, 4:5]), "act")

            @block.vector
            def _(v):
                kb.wait(v, "act")
                kb.inc(v.reduce_sum(rst[:, 6:7], rl[:, :], axis=AX.X), "dve")
                kb.wait(v, "dve")
                kb.inc(v.reciprocal(rst[:, 6:7], rst[:, 6:7]), "dve")
                kb.wait(v, "dve")
                kb.inc(v.tensor_scalar_mul(rl[:, :], rl[:, :], rst[:, 6:7]), "dve")

            @block.sync
            def _(s, ti=ti, st=st):
                kb.wait(s, "dve")
                kb.inc(s.dma_start(
                    out=sc_out[ti * 512 + st * 128: ti * 512 + (st + 1) * 128, :],
                    in_=rl[:, :]), "qs")


def emit_router_weights(kb, block, Wr_in, br_in, rw, ps, cst):
    @block.sync
    def _(s):
        kb.inc(s.dma_start(out=rw["wr_sb"][:, :].rearrange("p (c e) -> p c e", e=64),
                           in_=Wr_in.rearrange("(c p) e -> p c e", p=128)), "qs")
        kb.inc(s.dma_start(out=rw["brrow"][0:1, :], in_=br_in.unsqueeze(0)), "qs")

    @block.tensor
    def _(t):
        kb.wait(t, "qs")
        for c in range(8):
            ins_ = t.matmul(ps[0][0:1, 0:64], lhsT=cst["ones1f"][:, :],
                            rhs=rw["wr_sb"][:, c * 64:(c + 1) * 64],
                            start=(c == 0), stop=(c == 7))
        kb.inc(ins_, "pe")

    @block.vector
    def _(v):
        kb.wait(v, "pe")
        kb.inc(v.tensor_copy(rw["wcrow"][0:1, :], ps[0][0:1, 0:64]), "dve")

    @block.tensor
    def _(t):
        kb.wait(t, "dve")
        t.matmul(ps[0][:, 0:64], lhsT=cst["onescol"][:, :], rhs=rw["wcrow"][0:1, :],
                 start=True, stop=True)
        kb.inc(t.matmul(ps[1][:, 0:64], lhsT=cst["onescol"][:, :], rhs=rw["brrow"][0:1, :],
                        start=True, stop=True), "pe")

    @block.vector
    def _(v):
        kb.wait(v, "pe")
        v.tensor_copy(rw["wrcsbc"][:, :], ps[0][:, 0:64])
        kb.inc(v.tensor_copy(rw["brbc"][:, :], ps[1][:, 0:64]), "dve")


def build_lA():
    nc = bass.Bass("TRN2", target_bir_lowering=False, debug=False)
    x_img = nc.dram_tensor("x_img", [128, 8 * T], F32, kind="ExternalInput").ap()
    Wr_in = nc.dram_tensor("Wr", [C, E], F32, kind="ExternalInput").ap()
    br_in = nc.dram_tensor("br", [E], F32, kind="ExternalInput").ap()
    sc_out = nc.dram_tensor("sc_own", [T, E], F32, kind="ExternalOutput").ap()
    with ExitStack() as stack:
        kb = KB(nc, stack)
        cst_dram = {
            "ones1f": kb.const("c_ones1f", np.ones((128, 1), np.float32)),
            "onescol": kb.const("c_onescol", np.ones((1, 128), np.float32)),
            "eps1": kb.const("c_eps1", np.full((128, 1), 1e-5, np.float32)),
        }
        cst = {nm: sbuf(kb, nm, list(v.shape)) for nm, v in
               (("ones1f", np.ones((128, 1))), ("onescol", np.ones((1, 128))),
                ("eps1", np.ones((128, 1))))}
        ps = [mkpsum(kb, f"ps{i}") for i in range(2)]
        rw = {
            "wr_sb": sbuf(kb, "wr_sb", [128, 512]),
            "brrow": sbuf(kb, "brrow", [1, 64]),
            "wcrow": sbuf(kb, "wcrow", [1, 64]),
            "brbc": sbuf(kb, "brbc", [128, 64]),
            "wrcsbc": sbuf(kb, "wrcsbc", [128, 64]),
        }
        bufs = {
            "rx": sbuf(kb, "rx", [128, 4096]),
            "rsq": sbuf(kb, "rsq", [128, 1024]),
            "rst": sbuf(kb, "rst", [128, 8]),
            "rl": sbuf(kb, "rl", [128, 64]),
        }
        block = stack.enter_context(nc.Block())

        @block.sync
        def _(s):
            for nm in cst_dram:
                kb.inc(s.dma_start(out=cst[nm][:, :], in_=cst_dram[nm]), "qs")

        emit_router_weights(kb, block, Wr_in, br_in, rw, ps, cst)
        emit_router(kb, block, x_img, sc_out, rw, ps, cst, bufs)

        @block.sync
        def _(s):
            kb.wait_all(s)
    return nc


def build_lB():
    nc = bass.Bass("TRN2", target_bir_lowering=False, debug=False)
    x_img = nc.dram_tensor("x_img", [128, 8 * T], F32, kind="ExternalInput").ap()
    scores_em = nc.dram_tensor("scores_em", [E, NTOK], F32, kind="ExternalInput").ap()
    scores_own = nc.dram_tensor("scores_own", [T, E], F32, kind="ExternalInput").ap()
    w_in = {nm: nc.dram_tensor(nm, shp, F32, kind="ExternalInput").ap() for nm, shp in
            (("Wup", [E, C, HH]), ("Wgate", [E, C, HH]), ("Wdown", [E, HH, C]),
             ("bup", [E, HH]), ("bgate", [E, HH]), ("bdown", [E, C]))}
    Wr_in = nc.dram_tensor("Wr", [C, E], F32, kind="ExternalInput").ap()
    br_in = nc.dram_tensor("br", [E], F32, kind="ExternalInput").ap()
    a_names = ["x_r", "x_w", "x_k", "x_v", "x_a", "k_k", "k_a", "r_k", "W_r", "W_k",
               "W_v", "W_o", "w_lora_a", "w_lora_b", "w_lora_bias", "a_lora_a",
               "a_lora_b", "a_lora_bias", "g_lora_a", "g_lora_b", "g_lora_bias",
               "gn_weight", "gn_bias"]
    a_shapes = {"x_r": [C], "x_w": [C], "x_k": [C], "x_v": [C], "x_a": [C], "k_k": [C],
                "k_a": [C], "r_k": [NH, HD], "W_r": [C, C], "W_k": [C, C], "W_v": [C, C],
                "W_o": [C, C], "w_lora_a": [C, 64], "w_lora_b": [64, C],
                "w_lora_bias": [C], "a_lora_a": [C, 64], "a_lora_b": [64, C],
                "a_lora_bias": [C], "g_lora_a": [C, 128], "g_lora_b": [128, C],
                "g_lora_bias": [C], "gn_weight": [C], "gn_bias": [C]}
    a_in = {nm: nc.dram_tensor(nm, a_shapes[nm], F32, kind="ExternalInput").ap()
            for nm in a_names}
    x2_img = nc.dram_tensor("x2_img", [128, 8 * T], F32, kind="ExternalOutput").ap()
    sc3_own = nc.dram_tensor("sc3_own", [T, E], F32, kind="ExternalOutput").ap()

    x1_img = nc.dram_tensor("d_x1", [128, 8 * T], F32).ap()
    w16 = {
        "Wup": [nc.dram_tensor(f"w16up{e}", [128, 8, HH], BF16).ap() for e in range(E)],
        "Wgate": [nc.dram_tensor(f"w16ga{e}", [128, 8, HH], BF16).ap() for e in range(E)],
        "Wdown": [nc.dram_tensor(f"w16dn{e}", [128, 2, C], BF16).ap() for e in range(E)],
    }
    aw16_dram = {
        "wr16": nc.dram_tensor("d_wr16", [128, 8, 1024], BF16).ap(),
        "wk16": nc.dram_tensor("d_wk16", [128, 8, 1024], BF16).ap(),
        "wv16": nc.dram_tensor("d_wv16", [128, 8, 1024], BF16).ap(),
        "wo16": nc.dram_tensor("d_wo16", [128, 8, 1024], BF16).ap(),
        "wla16": nc.dram_tensor("d_wla16", [128, 8, 64], BF16).ap(),
        "ala16": nc.dram_tensor("d_ala16", [128, 8, 64], BF16).ap(),
        "gla16": nc.dram_tensor("d_gla16", [128, 8, 128], BF16).ap(),
        "wlb16": nc.dram_tensor("d_wlb16", [64, 1024], BF16).ap(),
        "alb16": nc.dram_tensor("d_alb16", [64, 1024], BF16).ap(),
        "glb16": nc.dram_tensor("d_glb16", [128, 1024], BF16).ap(),
        "gbias": nc.dram_tensor("d_gbias", [1, 1024], BF16).ap(),
    }
    adram = {
        "at": nc.dram_tensor("d_at", [128, 8 * T], BF16).ap(),
        "bt": nc.dram_tensor("d_bt", [128, 8 * T], BF16).ap(),
        "kt": nc.dram_tensor("d_kt", [128, 8 * T], BF16).ap(),
        "rt": nc.dram_tensor("d_rt", [128, 8 * T], BF16).ap(),
        "v_d": nc.dram_tensor("d_v", [T, C], BF16).ap(),
        "g_d": nc.dram_tensor("d_g", [T, C], BF16).ap(),
    }
    dram = {
        "theta": nc.dram_tensor("d_theta", [1, 1], F32).ap(),
        "sigma": nc.dram_tensor("d_sigma", [E, 1], F32).ap(),
        "arow": nc.dram_tensor("d_arow", [1, 1024], F32).ap(),
        "brow": nc.dram_tensor("d_brow", [1, 1024], F32).ap(),
        "fs": nc.dram_tensor("d_fs", [E, T], BF16).ap(),
    }

    with ExitStack() as stack:
        kb = KB(nc, stack)
        cstn = {
            "ident": np.eye(128, dtype=np.float32),
            "oh31": (np.arange(128) == 31).astype(np.float32).reshape(128, 1),
            "onescol": np.ones((1, 128), np.float32),
            "eps1": np.full((128, 1), 1e-5, np.float32),
            "ones64": np.ones((128, 64), np.float32),
            "ones1f": np.ones((128, 1), np.float32),
        }
        for m in range(8, 15):
            cstn[f"sgn{m}"] = np.where(((np.arange(128) >> (m - 8)) & 1) == 0,
                                       1.0, -1.0).astype(np.float32).reshape(128, 1)
        BFD = ml_dtypes.bfloat16
        cstn16 = {
            "ident16": np.eye(128, dtype=np.float32).astype(BFD),
            "onescol16": np.ones((1, 128), np.float32).astype(BFD),
            "ones1_16": np.ones((128, 1), np.float32).astype(BFD),
            "ones_row16": np.ones((1, 512), np.float32).astype(BFD),
        }
        tl = np.tril(np.ones((64, 64), np.float32), -1)
        tu_s = np.triu(np.ones((64, 64), np.float32), 1)
        tu_i = np.triu(np.ones((64, 64), np.float32), 0)

        def bd2(m):
            zz = np.zeros((128, 128), np.float32)
            zz[:64, :64] = m
            zz[64:, 64:] = m
            return zz
        cstn16["mask4"] = np.concatenate([bd2(tl), bd2(tu_s), bd2(tu_s), bd2(tu_i)],
                                         1).astype(BFD)
        cstn16["masku_inc"] = bd2(tu_i).astype(BFD)
        for par in range(2):
            pa = np.zeros((128, 128), np.float32)
            pb = np.zeros((128, 128), np.float32)
            qa_ = np.zeros((128, 128), np.float32)
            qb_ = np.zeros((128, 128), np.float32)
            for u in range(64):
                pa[par * 64 + u, u] = 1.0
                pb[par * 64 + u, 64 + u] = 1.0
                qa_[u, par * 64 + u] = 1.0
                qb_[64 + u, par * 64 + u] = 1.0
            cstn16[f"vperm{par}a"] = pa.astype(BFD)
            cstn16[f"vperm{par}b"] = pb.astype(BFD)
            cstn[f"operm{par}a"] = qa_
            cstn[f"operm{par}b"] = qb_
        awn = {}
        for c in range(8):
            m = np.zeros((128, 16), np.float32)
            m[0:64, 2 * c] = 1.0
            m[64:128, 2 * c + 1] = 1.0
            awn[f"bdsel{c}"] = m.astype(BFD)
            h = np.zeros((16, 128), np.float32)
            h[2 * c, 0:64] = 1.0
            h[2 * c + 1, 64:128] = 1.0
            awn[f"hsel{c}"] = h.astype(BFD)

        ATT_ONLY = {"ident16", "mask4", "masku_inc", "vperm0a", "vperm0b",
                    "vperm1a", "vperm1b", "operm0a", "operm0b", "operm1a",
                    "operm1b", "ones64"}
        cst_dram = {k: kb.const("c_" + k, v) for k, v in cstn.items()}
        cst_dram.update({k: kb.const("c_" + k, v) for k, v in cstn16.items()})
        cst_shapes = {k: (list(v.shape), F32) for k, v in cstn.items()}
        cst_shapes.update({k: (list(v.shape), BF16) for k, v in cstn16.items()})
        adr = {k: kb.const("c_" + k, v) for k, v in awn.items()}
        cst = {}
        for k in cst_shapes:
            if k not in ATT_ONLY:
                cst[k] = sbuf(kb, "s_" + k, *cst_shapes[k])
        cst["junk"] = sbuf(kb, "junk", [1, 1])

        ps = [mkpsum(kb, f"ps{i}") for i in range(7)]
        ps16 = mkpsum(kb, "ps16", (128, 1024), BF16)

        aw = {}
        for c in range(8):
            aw[f"bdsel{c}"] = sbuf(kb, f"bdsel{c}", [128, 16], BF16)
            aw[f"hsel{c}"] = sbuf(kb, f"hsel{c}", [16, 128], BF16)
        abufs = {}

        block = stack.enter_context(nc.Block())

        @block.sync
        def _(s):
            for k2 in cst_dram:
                if k2 not in ATT_ONLY:
                    kb.inc(s.dma_start(out=cst[k2][:, :], in_=cst_dram[k2]), "qs")
            for k2 in awn:
                kb.inc(s.dma_start(out=aw[k2][:, :], in_=adr[k2]), "qs")

        @block.gpsimd
        def _(g):
            kb.inc(g.memset(cst["junk"][:, :], 0.0), "pool")

        with ExitStack() as sf:
            def sbf(name, shape, dtype=F32):
                return sf.enter_context(nc.sbuf_tensor(name, shape, dtype))
            sig_fm = sbf("sig_fm", [E, 1])
            thbc = sbf("thbc", [128, 1])
            sigbc = sbf("sigbc", [128, E])
            fsfm = sbf("fsfm", [64, T], BF16)
            bias16 = {
                "bup": sbf("bupS", [128, 128]),
                "bgate": sbf("bgaS", [128, 128]),
                "bupR": sbf("bupR", [128, 128]),
                "bgaR": sbf("bgaR", [128, 128]),
            }
            bdown_sb = sbf("bdownS", [64, C], BF16)

            emit_weight_cast(kb, block, w_in, w16)

            @block.gpsimd
            def _(g):
                kb.inc(g.dma_start(out=bdown_sb[:, :], in_=w_in["bdown"]), "qp")

            @block.sync
            def _(s):
                kb.inc(s.dma_start(out=bias16["bupR"][:, :],
                                   in_=w_in["bup"].rearrange("e (r p) -> (e r) p", p=128)), "qs")
                kb.inc(s.dma_start(out=bias16["bgaR"][:, :],
                                   in_=w_in["bgate"].rearrange("e (r p) -> (e r) p", p=128)), "qs")

            @block.tensor
            def _(t):
                kb.wait(t, "qs")
                kb.wait(t, "dve")
                t.transpose(ps[0][:, 0:128], in_=bias16["bupR"][:, :],
                            identity=cst["ident"][:, :])
                kb.inc(t.transpose(ps[1][:, 0:128], in_=bias16["bgaR"][:, :],
                                   identity=cst["ident"][:, :]), "pe")

            @block.vector
            def _(v):
                kb.wait(v, "pe")
                v.tensor_copy(bias16["bup"][:, :], ps[0][:, 0:128])
                kb.inc(v.tensor_copy(bias16["bgate"][:, :], ps[1][:, 0:128]), "dve")

            with ExitStack() as sp:
                def sb2(name, shape, dtype=F32):
                    return sp.enter_context(nc.sbuf_tensor(name, shape, dtype))
                bufA = sb2("bufA", [128, E * 256])
                bufB = sb2("bufB", [128, E * 256])
                tbufA = sb2("tbufA", [128, 8 * 128])
                tbufB = sb2("tbufB", [128, 8 * 128])
                gs = sb2("gs", [128, E * 4])
                bm = sb2("bm", [128, E * 4])
                thA = sb2("thA", [128, 256])
                thB = sb2("thB", [128, 256])
                scr = {
                    "th11": sb2("th11", [1, 1]),
                    "mask": sb2("mask", [128, E * 4], mybir.dt.uint32),
                    "cand": sb2("cand", [128, E * 4]),
                    "emin": sb2("emin", [128, E]),
                    "eminT": sb2("eminT", [64, 128]),
                    "bigpos": sb2("bigpos", [128, E * 4]),
                    "sigrow": sb2("sigrow", [1, 64]),
                }
                fsb = {
                    "stile": sb2("stile", [128, 64]),
                    "fs0": sb2("fs0", [128, 64]),
                    "fsmask": sb2("fsmask", [128, 64]),
                    "fsden": sb2("fsden", [128, 1]),
                    "fsrb": sb2("fsrb", [64, T], BF16),
                }

                @block.sync
                def _(s):
                    kb.wait(s, "qs")
                    for gidx in range(8):
                        src = scores_em[gidx * 8:(gidx + 1) * 8, :].rearrange(
                            "e (p f) -> p e f", p=128)
                        kb.inc(s.dma_start(out=bufA[:, :].rearrange(
                            "p (e f) -> p e f", f=256)[:, gidx * 8:(gidx + 1) * 8, :],
                            in_=src), "qs")

                @block.vector
                def _(v):
                    kb.wait(v, "qs")
                    kb.wait(v, "dve")
                    kb.inc(v.memset(scr["bigpos"][:, :], BIGPOS), "dve")

                emit_sort(kb, block, bufA, bufB, tbufA, tbufB, ps[2], cst, nbat=E)
                emit_gs_bm(kb, block, bufA, gs, bm, nbat=E)
                emit_theta_sigma(kb, block, gs, bm, thA, thB, tbufA, tbufB, ps[2], cst,
                                 sig_fm, thbc, sigbc, scr, dram)
                emit_fs(kb, block, scores_own, sigbc, fsfm, dram["fs"], ps, cst, fsb)

                rbw = sb2("rbw", [128, 2048], BF16)

                @block.sync
                def _(s):
                    kb.wait(s, "qp")
                    for e in range(E):
                        for nm in ("Wup", "Wgate", "Wdown"):
                            kb.inc(s.dma_start(out=rbw[:, :],
                                               in_=w16[nm][e].rearrange("p c h -> p (c h)")), "qs")

                for engname in ("vector", "gpsimd", "scalar", "tensor", "sync"):
                    def mk(e, _n=engname):
                        kb.wait_all(e)
                    getattr(block, engname)(mk)

            with ExitStack() as sw:
                def sb3(name, shape, dtype=F32):
                    return sw.enter_context(nc.sbuf_tensor(name, shape, dtype))
                fbufs = {
                    "xq": sb3("xq", [128, 8 * 1024]),
                    "yq": sb3("yq", [128, 8 * 1024], BF16),
                    "hgrp": sb3("hgrp", [128, 16 * 1024], BF16),
                    "dacc": sb3("dacc", [128, 16 * 512]),
                    "silu": sb3("silu", [128, 2 * 512], BF16),
                    "fsbc": sb3("fsbc", [128, 2 * 1024], BF16),
                    "swu": sb3("swu", [128, 2 * 2048], BF16),
                    "swg": sb3("swg", [128, 2 * 2048], BF16),
                    "swd": sb3("swd", [128, 8 * 2048], BF16),
                    "arow": sb3("f_arow", [1, 1024]),
                    "brow": sb3("f_brow", [1, 1024]),
                    "abc": sb3("f_abc", [128, 1024]),
                    "bbc": sb3("f_bbc", [128, 1024]),
                    "xsq": sb3("xsq", [128, 512], BF16),
                    "xbf": sb3("xbf", [128, 512], BF16),
                    "srow": sb3("srow", [1, 1024]),
                }
                emit_ffn_sweep(kb, block, x_img, x1_img, dram["fs"], w16, bias16,
                               bdown_sb, fsfm, ps, cst, fbufs, dram)
                for engname in ("vector", "gpsimd", "scalar", "tensor", "sync"):
                    def mk(e, _n=engname):
                        kb.wait_all(e)
                    getattr(block, engname)(mk)

        with ExitStack() as sa:
            def sb4(name, shape, dtype=F32):
                return sa.enter_context(nc.sbuf_tensor(name, shape, dtype))
            aw["wr16"] = sb4("wr16", [128, 8192], BF16)
            aw["wk16"] = sb4("wk16", [128, 8192], BF16)
            aw["wv16"] = sb4("wv16", [128, 8192], BF16)
            aw["wla16"] = sb4("wla16", [128, 512], BF16)
            aw["ala16"] = sb4("ala16", [128, 512], BF16)
            aw["gla16"] = sb4("gla16", [128, 1024], BF16)
            aw["wlb16"] = sb4("wlb16", [64, 1024], BF16)
            aw["alb16"] = sb4("alb16", [64, 1024], BF16)
            aw["glb16"] = sb4("glb16", [128, 1024], BF16)
            aw["gbias_row16"] = sb4("gbias_row16", [1, 1024], BF16)
            aw["vecs"] = sb4("vecs", [128, 96])
            aw["vstage"] = sb4("vstage", [128, 128])
            aw["gnwrow"] = sb4("gnwrow", [1, 1024])
            aw["gnbrow"] = sb4("gnbrow", [1, 1024])
            abufs["cexp_sb"] = sb4("cexp_sb", [128, 512])
            abufs["bonus"] = sb4("bonus", [16, T], BF16)
            for k2 in ATT_ONLY:
                cst[k2] = sb4("s_" + k2, *cst_shapes[k2])

            @block.sync
            def _(s):
                kb.wait(s, "dve")
                for k2 in ATT_ONLY:
                    kb.inc(s.dma_start(out=cst[k2][:, :], in_=cst_dram[k2]), "qs")

            emit_attn_setup(kb, block, nc, a_in, aw, aw16_dram, ps, cst, None)

            with ExitStack() as s1:
                def sba(name, shape, dtype=F32):
                    return s1.enter_context(nc.sbuf_tensor(name, shape, dtype))
                for nm in ("fA", "fB", "fC"):
                    abufs[nm] = sba(nm, [128, 4096])
                abufs["fE"] = sba("fE", [128, 2048])
                for i in range(6):
                    abufs[f"b{i}"] = sba(f"b{i}", [128, 4096], BF16)
                abufs["carry"] = sba("carry", [128, 8])
                abufs["cval"] = sba("cval", [128, 64])
                abufs["arow"] = sba("p_arow", [1, 1024])
                abufs["brow"] = sba("p_brow", [1, 1024])
                abufs["abc"] = sba("p_abc", [128, 512])
                abufs["bbc"] = sba("p_bbc", [128, 512])
                abufs["z64"] = abufs["abc"][0:64, :].bitcast(BF16)[:, 0:512]
                emit_attn_prep(kb, block, x1_img, aw, ps, ps16, cst, abufs, adram)
                for engname in ("vector", "gpsimd", "scalar", "tensor", "sync"):
                    def mk(e, _n=engname):
                        kb.wait_all(e)
                    getattr(block, engname)(mk)

            aw["wo16"] = sb4("wo16", [128, 8192], BF16)
            aw["gnwbc"] = sb4("gnwbc", [128, 1024])
            aw["gnbbc"] = sb4("gnbbc", [128, 1024])

            @block.sync
            def _(s):
                kb.inc(s.dma_start(out=aw["wo16"][:, :],
                                   in_=aw16_dram["wo16"].rearrange("p a b -> p (a b)")), "qs")
            emit_gnbc(kb, block, aw, ps, cst)

            sb2d = {
                "a_s": sb4("a_s", [128, 8 * 512], BF16),
                "b_s": sb4("b_s", [128, 8 * 512], BF16),
                "k_s": sb4("k_s", [128, 8 * 512], BF16),
                "r_s": sb4("r_s", [128, 8 * 512], BF16),
                "v_s": sb4("v_s", [128, 4 * C], BF16),
                "cexp": None,
                "yt": sb4("yt", [128, 8 * 128]),
                "y0p": sb4("y0p", [128, 128], BF16),
                "bd": sb4("bd", [128, 5 * 128], BF16),
                "scmm": sb4("scmm", [128, 512], BF16),
                "lad": sb4("lad", [128, 10 * 128], BF16),
                "mrkt": sb4("mrkt", [128, 128], BF16),
                "mkav": sb4("mkav", [128, 128]),
                "btm": sb4("btm", [128, 256], BF16),
                "xs": sb4("xs", [128, 128], BF16),
                "dlt": sb4("dlt", [128, 128]),
                "o_tm": sb4("o_tm", [128, 4 * C]),
                "osb": sb4("osb", [128, 256]),
                "ps16": ps16,
                "bonus": abufs["bonus"],
                "ep_g": sb4("ep_g", [128, 1024], BF16),
                "ep_st": sb4("ep_st", [128, 16]),
                "ep_st2": sb4("ep_st2", [128, 16]),
                "ep_sq": sb4("ep_sq", [128, 1024]),
                "ep_bt": sb4("ep_bt", [128, 64]),
                "ep_ogf": sb4("ep_ogf", [128, 4096], BF16),
                "ep_x1": sb4("ep_x1", [128, 2048]),
            }
            epi = make_attn_epilogue(kb, block, aw, ps, ps16, cst, sb2d, adram,
                                     x1_img, x2_img)
            emit_scan(kb, block, adram["at"], adram["bt"], adram["kt"], adram["rt"],
                      adram["v_d"], None, None, ps, cst, sb2d,
                      epilogue_fn=epi, cexp_sb=abufs["cexp_sb"])
            for engname in ("vector", "gpsimd", "scalar", "tensor", "sync"):
                def mk(e, _n=engname):
                    kb.wait_all(e)
                getattr(block, engname)(mk)

        rw = {
            "wr_sb": sbuf(kb, "wr_sb", [128, 512]),
            "brrow": sbuf(kb, "brrow", [1, 64]),
            "wcrow": sbuf(kb, "wcrow", [1, 64]),
            "brbc": sbuf(kb, "brbc", [128, 64]),
            "wrcsbc": sbuf(kb, "wrcsbc", [128, 64]),
        }
        emit_router_weights(kb, block, Wr_in, br_in, rw, ps, cst)
        rbufs = {
            "rx": sbuf(kb, "rx", [128, 4096]),
            "rsq": sbuf(kb, "rsq", [128, 1024]),
            "rst": sbuf(kb, "rst", [128, 8]),
            "rl": sbuf(kb, "rl", [128, 64]),
        }
        emit_router(kb, block, x2_img, sc3_own, rw, ps, cst, rbufs)

        @block.sync
        def _(s):
            kb.wait_all(s)
    return nc


A_NAMES = ["x_r", "x_w", "x_k", "x_v", "x_a", "k_k", "k_a", "r_k", "W_r", "W_k",
           "W_v", "W_o", "w_lora_a", "w_lora_b", "w_lora_bias", "a_lora_a",
           "a_lora_b", "a_lora_bias", "g_lora_a", "g_lora_b", "g_lora_bias",
           "gn_weight", "gn_bias"]


def _fm_img(xb):
    return np.ascontiguousarray(
        xb.T.reshape(8, 128, T).transpose(1, 0, 2).reshape(128, 8 * T))


def _from_img(img):
    return np.ascontiguousarray(
        np.asarray(img).reshape(128, 8, T).transpose(1, 0, 2).reshape(C, T).T)


def kernel(**inputs):
    from concourse.bass_utils import run_bass_kernel_spmd
    _ = inputs.get("L0_attn_x_g")
    x = np.ascontiguousarray(np.asarray(inputs["L0_attn_hidden_states"], np.float32))
    imgs = [_fm_img(np.ascontiguousarray(x[b])) for b in range(B)]
    Wr = np.ascontiguousarray(np.asarray(inputs["L0_ffn_Wr"], np.float32))
    br = np.ascontiguousarray(np.asarray(inputs["L0_ffn_br"], np.float32))
    ffn = {nm: np.ascontiguousarray(np.asarray(inputs["L0_ffn_" + nm], np.float32))
           for nm in ("Wup", "Wgate", "Wdown", "bup", "bgate", "bdown")}
    attn = {nm: np.ascontiguousarray(np.asarray(inputs["L0_attn_" + nm], np.float32))
            for nm in A_NAMES}
    cores = list(range(B))

    ncA = build_lA()
    resA = run_bass_kernel_spmd(
        ncA, [{"x_img": imgs[b], "Wr": Wr, "br": br} for b in range(B)],
        core_ids=cores)
    sc1 = [np.ascontiguousarray(np.asarray(resA.results[b]["sc_own"], np.float32))
           for b in range(B)]
    scores_em = np.ascontiguousarray(np.concatenate(sc1, axis=0).T)

    ncB = build_lB()
    insB = []
    for b in range(B):
        d = {"x_img": imgs[b], "scores_em": scores_em, "scores_own": sc1[b],
             "Wr": Wr, "br": br}
        d.update(ffn)
        d.update(attn)
        insB.append(d)
    resB = run_bass_kernel_spmd(ncB, insB, core_ids=cores)
    x2 = [np.ascontiguousarray(np.asarray(resB.results[b]["x2_img"], np.float32))
          for b in range(B)]
    sc3 = [np.ascontiguousarray(np.asarray(resB.results[b]["sc3_own"], np.float32))
           for b in range(B)]
    scores_em3 = np.ascontiguousarray(np.concatenate(sc3, axis=0).T)

    ncC = build_l2()
    insC = []
    for b in range(B):
        d = {"x_img": x2[b], "scores_em": scores_em3, "scores_own": sc3[b]}
        d.update(ffn)
        insC.append(d)
    resC = run_bass_kernel_spmd(ncC, insC, core_ids=cores)
    out = np.empty((B, T, C), np.float32)
    for b in range(B):
        out[b] = _from_img(np.asarray(resC.results[b]["x3_img"], np.float32))
    return out
